# version 05
# baseline (speedup 1.0000x reference)
import numpy as np
import ml_dtypes
import concourse.bass as bass
import concourse.mybir as mybir
from concourse.bass_utils import run_bass_kernel_spmd

F32 = mybir.dt.float32
BF16 = mybir.dt.bfloat16
AF = mybir.ActivationFunctionType
ALU = mybir.AluOpType
AX = mybir.AxisListType

TC = 2048
NT = 16
C = 1024
IC = 8
H = 16
E = 8
FF = 4096
FC = 32
NFULL = 8192
RG = [[0, 1, 2, 3], [4, 5, 6, 7]]
BF = ml_dtypes.bfloat16
INVN2 = 1.0 / (float(NFULL) * float(NFULL))

PS_S1 = 0
PS_S2 = 512
PS_MA = 1024
PS_MB = 1536
R0, R1, R2, R3 = 2048, 2560, 3072, 3584

ROWSA = {
    "k1a": (0, 0), "k1b": (32, 0), "k2a": (64, 0),
    "k2b": (0, 1), "v1a": (32, 1), "v1b": (64, 1),
    "v2a": (0, 2), "v2b": (32, 2), "qa": (64, 2),
    "qb": (0, 3), "oa": (32, 3), "ob": (64, 3),
}


class Wk:

    def __init__(self, cur, eng, sems):
        self.cur = cur
        self.e = eng
        self.sems = sems
        self.c = {k: 0 for k in sems}

    def dma(self, qn, out, in_, sem=None, slow=False):
        sem = sem or qn
        self.c[sem] += 16
        if self.cur == qn:
            self.e.dma_start(out=out, in_=in_,
                             allow_slow_non_contiguous=slow).then_inc(
                self.sems[sem], 16)
        return self.c[sem]

    def w(self, qn, sem, val):
        if self.cur == qn and val > 0:
            self.e.wait_ge(self.sems[sem], val)

    def op(self, qn, fn, inc=None):
        if inc:
            self.c[inc] += 1
        if self.cur == qn:
            ins = fn(self.e)
            if inc:
                ins.then_inc(self.sems[inc], 1)
        return self.c[inc] if inc else None


class MK:
    def __init__(self, taps=(), stop_after=None, nt=NT, init_only=False,
                 a_stage=4, dbg=()):
        self.taps = set(taps)
        self.stop_after = stop_after
        self.nt = nt
        self.init_only = init_only
        self.a_stage = a_stage
        self.dbg = set(dbg)
        nc = bass.Bass("TRN2", target_bir_lowering=False,
                       detect_race_conditions=False)
        self.nc = nc

        self.io = {}
        for nm, sh in (
            ("x", [TC, C]), ("wq", [C, C]), ("wk1", [C, C]), ("wk2", [C, C]),
            ("wv1", [C, C]), ("wv2", [C, C]), ("wo", [C, C]),
            ("biases", [6, C]), ("gvec", [4, H]), ("lng", [C]), ("lnb", [C]),
            ("gw", [E, C]), ("gb", [E]), ("eb", [E]),
            ("w1", [E, C, FF]), ("b1", [E, FF]), ("w2", [E, FF, C]), ("b2", [E, C]),
        ):
            self.io[nm] = nc.dram_tensor(nm, sh, F32, kind="ExternalInput").ap()
        self.io["out"] = nc.dram_tensor("out", [TC, C], F32, kind="ExternalOutput").ap()
        self.io["cc_in"] = nc.dram_tensor("cc_in", [128, 1536], F32, kind="Internal").ap()
        self.io["cc_out"] = nc.dram_tensor("cc_out", [128, 1536], F32, kind="Internal").ap()
        self.w2b = nc.dram_tensor("w2b", [E, 16, 128, 2048], BF16, kind="Internal").ap()
        self.w1b = nc.dram_tensor("w1b", [E, 128, 32768], BF16, kind="Internal").ap()

        self.tap_t = {}
        for nm, sh, dt in (
            ("xnb", [128, IC * TC], BF16), ("kv0", [128, 4096], BF16),
            ("scc", [128, 1536], F32), ("rcc", [128, 1536], F32),
            ("onehot", [128, NT * 8], F32), ("gbuf", [128, NT * C], BF16),
            ("qb", [128, 16384], BF16), ("mm", [128, 512], F32),
            ("y2r", [2, 512], F32), ("ynat", [128, IC * TC], BF16),
            ("ln", [128, IC * TC], BF16), ("sfm", [128, IC * TC], BF16),
            ("stm", [128, NT * C], BF16),
        ):
            if nm in self.taps:
                self.tap_t[nm] = nc.dram_tensor("tap_" + nm, sh, dt,
                                                kind="ExternalOutput").ap()

        def inl(name, arr):
            return nc.inline_tensor(arr, name).ap()
        et = np.zeros((128, 1), np.float32); et[:64] = 1.0
        ebv = np.zeros((128, 1), np.float32); ebv[64:] = 1.0
        st = np.zeros((8, 512), BF)
        for qq in range(512):
            st[qq // 64, qq] = 1.0
        self.consts = {
            "i128b": np.eye(128, dtype=BF), "i128f": np.eye(128, dtype=np.float32),
            "i8f": np.eye(8, dtype=np.float32),
            "ones_b": np.ones((128, 512), dtype=BF),
            "ones_f": np.ones((128, 128), np.float32),
            "onescol_b": np.ones((128, 1), dtype=BF),
            "etop": et, "ebot": ebv,
            "sel8": st,
        }
        self.cin = {k: inl(k, v) for k, v in self.consts.items()}

        A = nc.alloc_sbuf_tensor
        self.xnb = A("xnb", [128, IC * TC], BF16)
        self.gbuf = A("gbuf", [128, NT * C], BF16)
        self.big0 = A("big0", [128, 32768], BF16)
        self.big1 = A("big1", [128, 16384], BF16)
        self.wsl = A("wsl", [128, 6144], BF16)
        self.xa = A("xa", [128, 2048], F32)
        self.xf = A("xf", [128, 1024], F32)
        self.ob2 = A("ob2", [128, 1024], F32)
        self.ohm = A("ohm", [128, NT * 8], F32)
        self.sgs = A("sgs", [128, 1024], BF16)
        self.mfin = A("mfin", [128, 512], F32)
        self.mbuf = A("mbuf", [128, 512], BF16)
        self.csx = A("csx", [128, 512], F32)
        self.smal = A("smal", [128, 40], F32)
        self.rowsA = A("rowsA", [128, 2048], BF16)
        self.b1c = A("b1c", [128, 256], F32)
        self.b2n = A("b2n", [8, 1024], BF16)
        self.ge = A("ge", [8, 128], F32)
        self.geb = A("geb", [8, 128], BF16)
        self.gwT = A("gwT", [128, 64], F32)
        self.cb = {}
        for k, v in self.consts.items():
            self.cb[k] = A("c_" + k, list(v.shape),
                           BF16 if v.dtype == BF else F32)
        self.ps = nc.alloc_psum_tensor("PALL", [128, 4096], F32)

    def rrow(self, name, c0=0, n=512):
        p, g = ROWSA[name]
        return self.rowsA[p:p + 1, g * 512 + c0:g * 512 + c0 + n]

    def walk(self, b):
        io, ps = self.io, self.ps
        TAPS = self.taps

        for k in self.consts:
            b.dma("q", self.cb[k][:, :], self.cin[k], sem="qi")
        for i, pfx in enumerate(("q", "k1", "k2", "v1", "v2", "o")):
            for half, suf in ((0, "a"), (1, "b")):
                p, g = ROWSA[pfx + suf]
                b.dma("g", self.rowsA[p:p + 1, g * 512:g * 512 + 512],
                      io["biases"][i:i + 1, half * 512:half * 512 + 512], sem="gi")
        b.dma("g", self.b2n[:, :], io["b2"], sem="gi")
        b.dma("q", self.smal[:, 0:8], io["lng"].rearrange("(c p) -> p c", p=128),
              sem="qi", slow=True)
        b.dma("q", self.smal[:, 8:16], io["lnb"].rearrange("(c p) -> p c", p=128),
              sem="qi", slow=True)
        b.dma("q", self.smal[0:8, 16:17], io["gb"].rearrange("(a e) -> e a", a=1),
              sem="qi", slow=True)
        b.dma("q", self.smal[0:8, 17:18], io["eb"].rearrange("(a e) -> e a", a=1),
              sem="qi", slow=True)
        b.dma("q", self.smal[0:8, 18:26].rearrange("p (t g) -> p t g", t=2),
              io["gvec"].rearrange("g (h t) -> h t g", t=2), sem="qi", slow=True)
        b.dma("q", self.b1c[:, :], io["b1"].rearrange("e (c p) -> p (e c)", p=128),
              sem="qi", slow=True)
        qinit = b.dma("q", self.gwT[:, :],
                      io["gw"].rearrange("e (c p) -> p (e c)", p=128), sem="qi", slow=True)

        qi_all = b.c["qi"]
        gi_all = b.c["gi"]
        b.op("v", lambda v: v.memset(self.smal[:, 38:39], 1e-5), inc="v")
        eps_v = b.c["v"]

        wst = {"k": 0, "tr": {}}
        w_stage = (self.wsl[:, 0:4096], self.big1[:, 10240:14336])

        def wT_build(wnm, dst, cstride, first, addI=False):
            for half in range(2):
                k = wst["k"]
                wst["k"] += 1
                buf = w_stage[k % 2]
                sem = "wn" if k % 2 == 0 else "wm"
                if k >= 2:
                    b.w("g", "t", wst["tr"][k - 2])
                gv = b.dma("g", buf.rearrange("p (o i) -> p o i", o=4),
                           io[wnm][half * 512:half * 512 + 512, :].rearrange(
                               "(o p) i -> p o i", p=128), sem=sem)
                b.w("t", sem, gv)
                if first and half == 0:
                    b.w("t", "qi", qi_all)
                for ib in range(IC):
                    if not (first and half == 0 and ib == 0):
                        b.w("t", "v", b.c["v"])
                    mms = []
                    for ob4 in range(4):
                        diag = addI and (ob4 == ib - half * 4)
                        mms.append((ob4, False, not diag))
                        if diag:
                            mms.append((ob4, True, True))
                    for mi, (ob4, ident, stop) in enumerate(mms):
                        lastmm = mi == len(mms) - 1
                        lhs = (self.cb["i128b"][:, :] if ident else
                               buf[:, ob4 * 1024 + ib * 128:
                                   ob4 * 1024 + ib * 128 + 128])
                        b.op("t", lambda t, ob4=ob4, ident=ident, stop=stop,
                             lhs=lhs: t.matmul(
                                 ps[:, R0 + ob4 * 128:R0 + ob4 * 128 + 128],
                                 lhs, self.cb["i128b"][:, :],
                                 start=not ident, stop=stop),
                             inc="t" if lastmm else None)
                    b.w("v", "t", b.c["t"])
                    b.op("v", lambda v, ib=ib, half=half: v.tensor_copy(
                        self.big0[:, dst + ib * cstride + half * 512:
                                  dst + ib * cstride + half * 512 + 512],
                        ps[:, R0:R0 + 512]), inc="v")
                wst["tr"][k] = b.c["t"]

        wT_build("wk1", 0, 2048, True, addI=True)
        wT_build("wk2", 1024, 2048, False, addI=True)
        wT_build("wv1", 16384, 2048, False, addI=True)
        wT_build("wv2", 16384 + 1024, 2048, False, addI=True)
        wkv_v = b.c["v"]
        if self.init_only:
            if "kv0" in self.taps:
                b.w("q", "v", wkv_v)
                b.dma("q", self.tap_t["kv0"], self.big0[:, 0:4096], sem="tap")
            self.finish(b)
            return
        NTL = self.nt

        pc_units = ([("w1", e0, i, hf) for e0 in range(2)
                     for i in range(IC) for hf in range(2)] +
                    [("w2", e, qd) for e in range(E) for qd in range(16)])
        pcv = {}
        pc_prev = {"pa": None, "pb": None}

        def emit_precast_unit(kind, u_parity):
            sem = "pa" if u_parity == 0 else "pb"
            if pc_prev[sem] is not None:
                b.w("g", sem, pc_prev[sem])
            if kind[0] == "w1":
                _, e0, i, hf = kind
                d = b.dma("g",
                          self.w1b[e0][:, i * 4096 + hf * 2048:
                                       i * 4096 + hf * 2048 + 2048],
                          io["w1"][e0][i * 128:(i + 1) * 128,
                                       hf * 2048:(hf + 1) * 2048], sem=sem)
            else:
                _, e, qd = kind
                d = b.dma("g", self.w2b[e, qd].rearrange("p (c o) -> p c o", c=2),
                          io["w2"][e, qd * 256:(qd + 1) * 256, :].rearrange(
                              "(c p) o -> p c o", p=128), sem=sem)
            pcv[kind] = (sem, d)
            pc_prev[sem] = d

        def emit_precast(u):
            emit_precast_unit(pc_units[u], u % 2)

        xdma = {}
        ln_done = {}
        xnT_s = {}
        trv = {}
        gate_t = {}
        ge_s = {}
        maskv = {}
        zev = {}
        smm_t = {}
        xdma[0] = b.dma("q", self.xa[:, 0:1024], io["x"][0:128, :], sem="xa")
        xdma[1] = b.dma("q", self.xa[:, 1024:2048], io["x"][128:256, :], sem="xb")
        for T in range(NTL):
            sl = (T % 2) * 1024
            xt = self.xa[:, sl:sl + 1024]
            b.w("v", "xa" if T % 2 == 0 else "xb", xdma[T])
            if T == 0:
                b.w("s", "qi", qi_all)
            b.op("v", lambda v, xt=xt: v.reduce_sum(self.smal[:, 32:33], xt, AX.X),
                 inc="v")
            b.w("s", "v", b.c["v"])
            b.op("s", lambda s: s.mul(self.smal[:, 33:34], self.smal[:, 32:33],
                                      -1.0 / C), inc="s")
            b.w("v", "s", b.c["s"])
            b.op("v", lambda v, xt=xt: v.tensor_scalar(xt, xt, self.smal[:, 33:34],
                                                       None, ALU.add), inc="v")
            b.w("s", "v", b.c["v"])
            b.op("s", lambda s, xt=xt, T=T: s.activation(
                self.big1[:, 8192 + (T % 2) * 1024:8192 + (T % 2) * 1024 + 1024],
                xt, AF.Square, accum_out=self.smal[:, 33:34]))
            b.op("s", lambda s: s.activation(self.smal[:, 34:35],
                                             self.smal[:, 33:34], AF.Sqrt,
                                             bias=self.smal[:, 38:39],
                                             scale=1.0 / C), inc="s")
            b.w("v", "s", b.c["s"])
            b.op("v", lambda v: v.reciprocal(self.smal[:, 35:36], self.smal[:, 34:35]),
                 inc="v")
            b.w("s", "v", b.c["v"])
            b.op("s", lambda s, xt=xt: s.mul(xt, xt, self.smal[:, 35:36]), inc="s")
            ln_done[T] = b.c["s"]

            if self.a_stage < 1:
                xnT_s[T] = b.c["s"]
                maskv[T] = b.c["v"]
                gate_t[T] = b.c["t"]
                ge_s[T] = b.c["s"]
                zev[(T, 0, 0)] = zev[(T, 0, 1)] = b.c["s"]
                zev[(T, 1, 0)] = zev[(T, 1, 1)] = b.c["s"]
                smm_t[T] = b.c["t"]
                continue
            b.w("t", "s", ln_done[T])
            if T > 0:
                b.w("t", "s", zev[(T - 1, 1, 0)])
            for j in range(IC):
                b.op("t", lambda t, j=j, xt=xt: t.matmul(
                    ps[:, R0 + j * 128:R0 + j * 128 + 128],
                    xt[:, j * 128:j * 128 + 128], self.cb["i128f"][:, :],
                    start=True, stop=True), inc="t" if j == IC - 1 else None)
            trT = b.c["t"]
            trv[T] = trT
            b.w("s", "t", trT)
            if "noxnb" in self.dbg:
                b.op("s", lambda s: s.copy(self.xnb[:, 0:1024],
                                           ps[:, R0:R0 + 1024]), inc="s")
            else:
                b.op("s", lambda s, T=T: s.copy(
                    self.xnb[:, :].rearrange("p (c f) -> p c f", c=IC)[
                        :, :, T * 128:T * 128 + 128],
                    ps[:, R0:R0 + 1024].rearrange("p (c f) -> p c f", f=128)),
                    inc="s")
            xnT_s[T] = b.c["s"]
            b.w("v", "t", trT)
            b.w("v", "s", xnT_s[T])
            if T > 0:
                b.w("v", "t", gate_t[T - 1])
            if "noxf" in self.dbg:
                b.op("v", lambda v: v.engine_nop(), inc="v")
            else:
                b.op("v", lambda v: v.tensor_copy(self.xf[:, :], ps[:, R0:R0 + 1024]),
                     inc="v")
            xf_v = b.c["v"]

            if self.a_stage < 2:
                maskv[T] = b.c["v"]
                gate_t[T] = b.c["t"]
                ge_s[T] = b.c["s"]
                zev[(T, 0, 0)] = zev[(T, 0, 1)] = b.c["s"]
                zev[(T, 1, 0)] = zev[(T, 1, 1)] = b.c["s"]
                smm_t[T] = b.c["t"]
                continue
            b.w("t", "v", xf_v)
            b.w("t", "s", xnT_s[T])
            if T > 0:
                b.w("t", "s", ge_s[T - 1])
            for i in range(IC):
                b.op("t", lambda t, i=i: t.matmul(
                    ps[0:8, R0:R0 + 128],
                    self.gwT[:, :].rearrange("p (e c) -> p c e", c=IC)[:, i, :],
                    self.xf[:, i * 128:i * 128 + 128],
                    start=(i == 0), stop=(i == IC - 1)),
                    inc="t" if i == IC - 1 else None)
            gmm = b.c["t"]
            b.w("s", "t", gmm)
            b.op("s", lambda s: s.activation(self.ge[:, :], ps[0:8, R0:R0 + 128],
                                             AF.Identity, bias=self.smal[0:8, 16:17]))
            b.op("s", lambda s: s.add(self.ge[:, :], self.ge[:, :],
                                      self.smal[0:8, 17:18]), inc="s")
            ge_s[T] = b.c["s"]
            b.w("t", "s", ge_s[T])
            b.op("t", lambda t: t.matmul(ps[:, R0 + 128:R0 + 136], self.ge[:, :],
                                         self.cb["i8f"][:, :], start=True, stop=True),
                 inc="t")
            gate_t[T] = b.c["t"]
            b.w("v", "t", gate_t[T])
            b.op("v", lambda v: v.reduce_max(self.smal[:, 36:37],
                                             ps[:, R0 + 128:R0 + 136], AX.X),
                 inc="v")
            b.w("s", "v", b.c["v"])
            b.op("s", lambda s: s.copy(self.smal[:, 37:38], self.smal[:, 36:37]),
                 inc="s")
            b.w("v", "s", b.c["s"])
            b.op("v", lambda v, T=T: v.tensor_scalar(
                self.ohm[:, T * 8:T * 8 + 8], ps[:, R0 + 128:R0 + 136],
                self.smal[:, 37:38], None, ALU.is_ge), inc="v")
            maskv[T] = b.c["v"]

            if self.a_stage < 3:
                zev[(T, 0, 0)] = zev[(T, 0, 1)] = b.c["s"]
                zev[(T, 1, 0)] = zev[(T, 1, 1)] = b.c["s"]
                smm_t[T] = b.c["t"]
                continue
            kvs = (T % 2) * 4096
            for rnd in range(2):
                wbase = rnd * 16384
                rks = (("k1a", "k1b", "k2a", "k2b") if rnd == 0
                       else ("v1a", "v1b", "v2a", "v2b"))
                b.w("t", "s", xnT_s[T])
                b.w("t", "v", maskv[T])
                if T == 0 and rnd == 0:
                    b.w("t", "v", wkv_v)
                    b.w("t", "gi", gi_all)
                if rnd == 0 and T > 0:
                    b.w("t", "s", zev[(T - 1, 1, 1)])
                zt_g = {}
                for g4 in range(4):
                    rot = R0 + g4 * 512
                    if rnd == 1 and g4 % 2 == 0:
                        b.w("t", "s", zev[(T, 0, g4 // 2)])
                    for i in range(IC):
                        b.op("t", lambda t, i=i, g4=g4, rot=rot, wbase=wbase, T=T:
                             t.matmul(
                                 ps[:, rot:rot + 512],
                                 self.xnb[:, i * 2048 + T * 128:
                                          i * 2048 + T * 128 + 128],
                                 self.big0[:, wbase + i * 2048 + g4 * 512:
                                           wbase + i * 2048 + g4 * 512 + 512],
                                 start=(i == 0), stop=False))
                    rp = ROWSA[rks[g4]][0]
                    b.op("t", lambda t, rot=rot, rr=rks[g4], rp=rp: t.matmul(
                        ps[:, rot:rot + 512], self.cb["ones_b"][rp:rp + 1, 0:128],
                        self.rrow(rr), start=False, stop=True), inc="t")
                    zt_g[g4] = b.c["t"]
                for g2 in range(2):
                    rot = R0 + g2 * 1024
                    b.w("s", "t", zt_g[2 * g2 + 1])
                    b.op("s", lambda s, g2=g2, rot=rot, rnd=rnd, kvs=kvs: s.copy(
                        self.big1[:, kvs + rnd * 2048 + g2 * 1024:
                                  kvs + rnd * 2048 + g2 * 1024 + 1024],
                        ps[:, rot:rot + 1024]), inc="s")
                    zev[(T, rnd, g2)] = b.c["s"]

            if self.a_stage < 4:
                smm_t[T] = b.c["t"]
                if T + 2 < NTL:
                    b.w("q", "s", zev[(T, 1, 1)])
                    xdma[T + 2] = b.dma("q", self.xa[:, sl:sl + 1024],
                                        io["x"][(T + 2) * 128:(T + 3) * 128, :],
                                        sem="xa" if T % 2 == 0 else "xb")
                continue
            b.w("t", "s", zev[(T, 1, 1)])
            for sidx in range(2):
                for h in range(H):
                    r0 = (h % 2) * 64
                    koff = kvs + sidx * 1024 + h * 64
                    voff = kvs + 2048 + sidx * 1024 + h * 64
                    cc0 = sidx * 512 + (h // 2) * 64
                    b.op("t", lambda t, koff=koff, voff=voff, r0=r0, cc0=cc0, T=T, h=h:
                         t.matmul(ps[r0:r0 + 64, cc0:cc0 + 64],
                                  self.big1[:, koff:koff + 64],
                                  self.big1[:, voff:voff + 64],
                                  start=(T == 0 and h < 2),
                                  stop=(T == NTL - 1 and h >= H - 2)))
            for mi in range(4):
                vb = kvs + 2048 + (mi // 2) * 1024 + (mi % 2) * 512
                mp = (mi % 3) * 32 if mi < 3 else 0
                mcol = PS_MA if mi < 3 else PS_MB
                b.op("t", lambda t, vb=vb, mp=mp, mcol=mcol, T=T, mi=mi: t.matmul(
                    ps[mp:mp + 1, mcol:mcol + 512], self.cb["onescol_b"][:, :],
                    self.big1[:, vb:vb + 512],
                    start=(T == 0), stop=(T == NTL - 1)),
                    inc="t" if mi == 3 else None)
            smm_t[T] = b.c["t"]
            if T + 2 < NTL:
                b.w("q", "t", trv[T])
                xdma[T + 2] = b.dma("q", self.xa[:, sl:sl + 1024],
                                    io["x"][(T + 2) * 128:(T + 3) * 128, :],
                                    sem="xa" if T % 2 == 0 else "xb")
            for u in range(T * 10, min((T + 1) * 10, len(pc_units))):
                emit_precast(u)
            if T == 10:
                gw_prev = None
                for k in range(4):
                    for hf in range(2):
                        ps_, pv_ = pcv[("w1", 0, k, hf)]
                        b.w("g", ps_, pv_)
                    if gw_prev is not None:
                        b.w("g", "w1", gw_prev)
                    gw_prev = b.dma("g", self.gbuf[:, k * 4096:(k + 1) * 4096],
                                    self.w1b[0][:, k * 4096:(k + 1) * 4096],
                                    sem="w1")
                gbuf_w1 = gw_prev
        for u in range(NTL * 10, len(pc_units)):
            emit_precast(u)
        s_done_t = smm_t[NTL - 1]

        if self.a_stage < 4:
            if "xnb" in TAPS and self.a_stage >= 1:
                b.w("q", "s", xnT_s[NTL - 1])
                b.dma("q", self.tap_t["xnb"], self.xnb[:, :], sem="tap")
            if self.a_stage < 1:
                b.w("q", "s", ln_done[NTL - 1])
            if "onehot" in TAPS and self.a_stage >= 2:
                b.w("q", "v", maskv[NTL - 1])
                b.dma("q", self.tap_t["onehot"], self.ohm[:, :], sem="tap")
            if "kv0" in TAPS and self.a_stage >= 3:
                b.w("q", "s", zev[(max(0, NTL - 2), 1, 1)])
                b.dma("q", self.tap_t["kv0"], self.big1[:, 0:4096], sem="tap")
            self.finish(b)
            return
        b.w("v", "t", s_done_t)
        b.op("v", lambda v: v.tensor_copy(self.xa[:, 0:1024], ps[:, 0:1024]))
        b.op("v", lambda v: v.memset(self.xa[:, 1024:1536], 0.0))
        b.op("v", lambda v: v.tensor_copy(self.xa[0:1, 1024:1536], ps[0:1, PS_MA:PS_MA + 512]))
        b.op("v", lambda v: v.tensor_copy(self.xa[32:33, 1024:1536], ps[32:33, PS_MA:PS_MA + 512]))
        b.op("v", lambda v: v.tensor_copy(self.xa[64:65, 1024:1536], ps[64:65, PS_MA:PS_MA + 512]))
        b.op("v", lambda v: v.tensor_copy(self.xa[96:97, 1024:1536], ps[0:1, PS_MB:PS_MB + 512]),
             inc="v")
        ccv = b.c["v"]
        if "scc" in TAPS:
            b.w("q", "v", ccv)
            b.dma("q", self.tap_t["scc"], self.xa[:, 0:1536], sem="tap")
        b.w("g", "v", ccv)
        ccw = b.dma("g", io["cc_in"], self.xa[:, 0:1536], sem="cw")
        if self.stop_after != "A":
            b.w("g", "cw", ccw)
            b.op("g", lambda g: g.collective_compute(
                "AllReduce", ALU.add, replica_groups=RG,
                ins=[io["cc_in"]], outs=[io["cc_out"]]), inc="cc")

        if "xnb" in TAPS:
            b.w("q", "s", xnT_s[NTL - 1])
            b.dma("q", self.tap_t["xnb"], self.xnb[:, :], sem="tap")
        if "kv0" in TAPS:
            b.w("q", "s", zev[(max(0, NTL - 2), 1, 1)])
            b.dma("q", self.tap_t["kv0"], self.big1[:, 0:4096], sem="tap")
        if "onehot" in TAPS:
            b.w("q", "v", maskv[NTL - 1])
            b.dma("q", self.tap_t["onehot"], self.ohm[:, :], sem="tap")

        if self.stop_after == "A":
            self.finish(b)
            return

        b.w("g", "t", s_done_t)
        for i in range(4, IC):
            for hf in range(2):
                ps_, pv_ = pcv[("w1", 0, i, hf)]
                b.w("g", ps_, pv_)
        w1v = [None] * E
        w1band = []
        prev_b = gbuf_w1
        for bb in range(16):
            ilo, ihi = (4, 8) if bb < 8 else (0, 4)
            col = (bb % 8) * 512
            b.w("g", "w1", prev_b)
            prev_b = b.dma(
                "g",
                self.big0[:, :].rearrange("p (i c) -> p i c", i=IC)[
                    :, ilo:ihi, col:col + 512],
                self.w1b[0].rearrange("p (i c) -> p i c", i=IC)[
                    :, ilo:ihi, col:col + 512], sem="w1")
            w1band.append(prev_b)
        slabv = {}
        quad_t = {}
        slab_n = [0]
        rb_done = [0]
        SLAB_SEMS = ("sa", "sb", "pc")

        def emit_slab(e, ts, qd, wait=None):
            gidx = slab_n[0]
            slab_n[0] += 1
            sidx = gidx % 3
            if wait is not None:
                b.w("q", "t", wait)
            psem, pval = pcv[("w2", e, qd)]
            b.w("q", psem, pval)
            slabv[(e, ts, qd)] = (sidx, b.dma(
                "q", self.wsl[:, sidx * 2048:sidx * 2048 + 2048],
                self.w2b[e, qd], sem=SLAB_SEMS[sidx]))

        h_ev = {}
        h_last_t = {}
        yev = {}
        yev_g0 = {}

        for e in range(E):
            prev = (e - 1, 3) if e > 0 else None
            for ts in range(4):
                for qd0 in range(3):
                    wait = None
                    if (e, ts) != (0, 0):
                        pe, pts = (e, ts - 1) if ts > 0 else (e - 1, 3)
                        wait = quad_t[(pe, pts, 13 + qd0)]
                    emit_slab(e, ts, qd0, wait)
                if 1 <= e < E - 1:
                    for uu in range(4):
                        idx = ts * 4 + uu
                        emit_precast_unit(("w1", e + 1, idx // 2, idx % 2),
                                          uu % 2)
                if e == 0 and ts == 0:
                    b.w("g", "cc", 1)
                    rb_done[0] = b.dma("g", self.xa[:, 0:1536], io["cc_out"],
                                       sem="cr")
                    if "rcc" in TAPS:
                        b.w("q", "cr", rb_done[0])
                        b.dma("q", self.tap_t["rcc"], self.xa[:, 0:1536], sem="tap")
                t0 = ts * 512
                prev2 = (e, ts - 1) if ts > 0 else prev
                for fc in range(FC):
                    hb = (fc % 2) * 512
                    if e == 0 and ts == 0 and fc % 4 == 0:
                        if fc == 0:
                            b.w("t", "w1", gbuf_w1)
                        b.w("t", "w1", w1band[fc // 4])
                    elif e == 0 and ts == 1 and fc == 0:
                        b.w("t", "w1", w1band[15])
                    elif fc == 0 and ts == 0 and e > 0:
                        b.w("t", "w1", w1v[e])
                    if fc >= 2:
                        b.w("t", "s", h_ev[(e, ts, fc - 2)])
                    elif prev2 is not None:
                        b.w("t", "v", yev_g0[prev2])
                    elif fc == 0:
                        b.w("t", "v", ccv)
                    w1lo = self.gbuf if (e == 0 and ts == 0) else self.big0
                    for i in range(IC):
                        src = w1lo if i < 4 else self.big0
                        b.op("t", lambda t, i=i, fc=fc, hb=hb, t0=t0, src=src:
                             t.matmul(
                                 ps[:, hb:hb + 512],
                                 src[:, i * 4096 + fc * 128:
                                     i * 4096 + fc * 128 + 128],
                                 self.xnb[:, i * 2048 + t0:i * 2048 + t0 + 512],
                                 start=(i == 0), stop=(i == IC - 1)),
                             inc="t" if i == IC - 1 else None)
                    b.w("s", "t", b.c["t"])
                    b.op("s", lambda s, fc=fc, hb=hb, e=e: s.activation(
                        self.big1[:, fc * 512:fc * 512 + 512], ps[:, hb:hb + 512],
                        AF.Relu, bias=self.b1c[:, e * 32 + fc:e * 32 + fc + 1]),
                        inc="s")
                    h_ev[(e, ts, fc)] = b.c["s"]
                h_last_t[(e, ts)] = b.c["t"]

                b.w("t", "s", h_ev[(e, ts, FC - 1)])
                b.w("t", "s", h_ev[(e, ts, FC - 2)])
                if prev2 is not None:
                    b.w("t", "v", yev[prev2])
                else:
                    b.w("t", "v", ccv)
                for fc in range(FC):
                    if fc % 2 == 0:
                        sidx0, sv0 = slabv[(e, ts, fc // 2)]
                        b.w("t", SLAB_SEMS[sidx0], sv0)
                    sidx = slabv[(e, ts, fc // 2)][0]
                    for g in range(4):
                        for og in range(2):
                            b.op("t", lambda t, fc=fc, g=g, og=og, sidx=sidx: t.matmul(
                                ps[:, g * 1024 + og * 512:g * 1024 + og * 512 + 512],
                                self.big1[:, fc * 512 + g * 128:fc * 512 + g * 128 + 128],
                                self.wsl[:, sidx * 2048 + (fc % 2) * 1024 + og * 512:
                                         sidx * 2048 + (fc % 2) * 1024 + og * 512 + 512],
                                start=(fc == 0), stop=(fc == FC - 1)),
                                inc="t" if (fc % 2 == 1 and g == 3 and og == 1)
                                else None)
                    if fc % 2 == 1:
                        qd = fc // 2
                        quad_t[(e, ts, qd)] = b.c["t"]
                        if qd + 3 < 16:
                            emit_slab(e, ts, qd + 3, quad_t[(e, ts, qd)])
                        if ts == 3 and e + 1 < E and qd < 8:
                            if qd == 0:
                                b.w("g", "t", h_last_t[(e, 3)])
                            for hf in range(2):
                                ps_, pv_ = pcv[("w1", e + 1, qd, hf)]
                                b.w("g", ps_, pv_)
                            w1v[e + 1] = b.dma(
                                "g",
                                self.big0[:, qd * 4096:qd * 4096 + 4096],
                                self.w1b[e + 1][:, qd * 4096:qd * 4096 + 4096],
                                sem="w1")
                ymm = b.c["t"]
                b.w("v", "t", ymm)
                for g in range(4):
                    tile = ts * 4 + g
                    if e == 0:
                        b.op("v", lambda v, tile=tile, g=g, e=e: v.tensor_scalar(
                            self.gbuf[:, tile * 1024:tile * 1024 + 1024],
                            ps[:, g * 1024:g * 1024 + 1024],
                            self.ohm[:, tile * 8 + e:tile * 8 + e + 1],
                            None, ALU.mult), inc="v")
                    else:
                        b.op("v", lambda v, tile=tile, g=g, e=e: v.scalar_tensor_tensor(
                            self.gbuf[:, tile * 1024:tile * 1024 + 1024],
                            ps[:, g * 1024:g * 1024 + 1024],
                            self.ohm[:, tile * 8 + e:tile * 8 + e + 1],
                            self.gbuf[:, tile * 1024:tile * 1024 + 1024],
                            ALU.mult, ALU.add), inc="v")
                    if g == 0:
                        yev_g0[(e, ts)] = b.c["v"]
                yev[(e, ts)] = b.c["v"]

        bm_s = {}
        bm_v = {}
        for tile in range(NT):
            pslot = (tile % 2) * 1024
            if tile == 0:
                b.w("t", "v", yev[(E - 1, 3)])
            if tile >= 2:
                b.w("t", "v", bm_v[tile - 2])
            b.op("t", lambda t, tile=tile, pslot=pslot: t.matmul(
                ps[0:8, R0 + pslot:R0 + pslot + 128],
                self.ohm[:, tile * 8:tile * 8 + 8], self.cb["i128f"][:, :],
                start=True, stop=True), inc="t")
            b.w("s", "t", b.c["t"])
            if tile >= 2:
                b.w("s", "s", bm_s[tile - 2])
            b.op("s", lambda s, tile=tile, pslot=pslot: s.copy(
                self.geb[:, :], ps[0:8, R0 + pslot:R0 + pslot + 128]), inc="s")
            bm_s[tile] = b.c["s"]
            b.w("t", "s", bm_s[tile])
            for og in range(2):
                b.op("t", lambda t, og=og, pslot=pslot: t.matmul(
                    ps[:, pslot + og * 512:pslot + og * 512 + 512],
                    self.geb[:, :], self.b2n[:, og * 512:og * 512 + 512],
                    start=True, stop=True), inc="t" if og == 1 else None)
            b.w("v", "t", b.c["t"])
            b.op("v", lambda v, tile=tile, pslot=pslot: v.tensor_tensor(
                self.gbuf[:, tile * 1024:tile * 1024 + 1024],
                self.gbuf[:, tile * 1024:tile * 1024 + 1024],
                ps[:, pslot:pslot + 1024], ALU.add), inc="v")
            bm_v[tile] = b.c["v"]
            b.w("s", "v", bm_v[tile])
            b.op("s", lambda s, tile=tile: s.activation(
                self.gbuf[:, tile * 1024:tile * 1024 + 1024],
                self.gbuf[:, tile * 1024:tile * 1024 + 1024], AF.Silu), inc="s")
        silu_s = b.c["s"]
        if "gbuf" in TAPS:
            b.w("q", "s", silu_s)
            b.dma("q", self.tap_t["gbuf"], self.gbuf[:, :], sem="tap")
        if self.stop_after == "FFN":
            self.finish(b)
            return

        b.w("g", "t", quad_t[(E - 1, 3, 15)])
        first_tail = [True]
        tst = {"k": 0, "tr": {}}
        for wnm, base in (("wq", 0), ("wo", 8192)):
            addI = wnm == "wq"
            for half in range(2):
                k = tst["k"]
                tst["k"] += 1
                buf = w_stage[k % 2]
                sem = "wn" if k % 2 == 0 else "wm"
                if k >= 2:
                    b.w("g", "t", tst["tr"][k - 2])
                gv = b.dma("g", buf.rearrange("p (o i) -> p o i", o=4),
                           io[wnm][half * 512:half * 512 + 512, :].rearrange(
                               "(o p) i -> p o i", p=128), sem=sem)
                b.w("t", sem, gv)
                if first_tail[0]:
                    b.w("t", "s", silu_s)
                    first_tail[0] = False
                for ib in range(IC):
                    b.w("t", "v", b.c["v"])
                    mms = []
                    for ob4 in range(4):
                        diag = addI and (ob4 == ib - half * 4)
                        mms.append((ob4, False, not diag))
                        if diag:
                            mms.append((ob4, True, True))
                    for mi, (ob4, ident, stop) in enumerate(mms):
                        lastmm = mi == len(mms) - 1
                        lhs = (self.cb["i128b"][:, :] if ident else
                               buf[:, ob4 * 1024 + ib * 128:
                                   ob4 * 1024 + ib * 128 + 128])
                        b.op("t", lambda t, lhs=lhs, ob4=ob4, ident=ident,
                             stop=stop: t.matmul(
                                 ps[:, R0 + ob4 * 128:R0 + ob4 * 128 + 128],
                                 lhs, self.cb["i128b"][:, :],
                                 start=not ident, stop=stop),
                             inc="t" if lastmm else None)
                    b.w("v", "t", b.c["t"])
                    b.op("v", lambda v, ib=ib, base=base, half=half: v.tensor_copy(
                        self.big0[:, base + ib * 1024 + half * 512:
                                  base + ib * 1024 + half * 512 + 512],
                        ps[:, R0:R0 + 512]), inc="v")
                tst["tr"][k] = b.c["t"]
        wT_v = b.c["v"]

        qv = {}
        for cc in range(IC):
            pbase = (cc % 2) * 2048
            if cc >= 2:
                b.w("t", "s", qv[cc - 2])
            if cc == 0:
                b.w("t", "v", wT_v)
            for i in range(IC):
                for tg in range(4):
                    b.op("t", lambda t, cc=cc, i=i, tg=tg, pbase=pbase: t.matmul(
                        ps[:, pbase + tg * 512:pbase + tg * 512 + 512],
                        self.big0[:, i * 1024 + cc * 128:i * 1024 + cc * 128 + 128],
                        self.xnb[:, i * 2048 + tg * 512:i * 2048 + tg * 512 + 512],
                        start=(i == 0), stop=False))
            for tg in range(4):
                rr = "qa" if cc < 4 else "qb"
                c0 = (cc % 4) * 128
                rp = ROWSA[rr][0]
                b.op("t", lambda t, tg=tg, rr=rr, c0=c0, rp=rp, pbase=pbase: t.matmul(
                    ps[:, pbase + tg * 512:pbase + tg * 512 + 512],
                    self.rrow(rr, c0, 128), self.cb["ones_b"][rp:rp + 1, :],
                    start=False, stop=True), inc="t" if tg == 3 else None)
            b.w("s", "t", b.c["t"])
            b.op("s", lambda s, cc=cc, pbase=pbase: s.copy(
                self.big0[:, 16384 + cc * 2048:16384 + cc * 2048 + 2048],
                ps[:, pbase:pbase + 2048]), inc="s")
            qv[cc] = b.c["s"]
        q_done = b.c["s"]
        if "qb" in TAPS:
            b.w("q", "s", q_done)
            b.dma("q", self.tap_t["qb"], self.big0[:, 16384:32768], sem="tap")
        if self.stop_after == "Q":
            self.finish(b)
            return

        b.w("t", "s", q_done)
        b.w("t", "cr", rb_done[0])
        S1 = self.xa[:, 0:512]
        S2 = self.xa[:, 512:1024]

        def eslot(par, i):
            p = par * 32
            return [self.xf[p:p + 1, 0:512], self.xf[p:p + 1, 512:1024],
                    self.ob2[p:p + 1, 0:512], self.ob2[p:p + 1, 512:1024],
                    self.xa[p:p + 1, 1536:2048], self.mfin[p:p + 1, 0:512]][i]
        gr_t = {}
        gr_mm = {}
        for r8 in range(8):
            gi, par = r8 // 2, r8 % 2
            gcol = (18 + gi) if par == 0 else (22 + gi)
            pslot = R0 + (r8 % 2) * 512
            if r8 == 0:
                b.w("s", "v", bm_v[NT - 1])
            else:
                b.w("s", "t", gr_mm[r8 - 1])
            b.op("s", lambda s, gcol=gcol: s.copy(
                self.geb[0:8, 0:1], self.smal[0:8, gcol:gcol + 1]), inc="s")
            b.w("t", "s", b.c["s"])
            if r8 >= 2:
                b.w("t", "v", gr_t[r8 - 2])
            b.op("t", lambda t, pslot=pslot: t.matmul(
                ps[0:1, pslot:pslot + 512],
                self.geb[0:8, 0:1], self.cb["sel8"][:, :],
                start=True, stop=True), inc="t")
            gr_mm[r8] = b.c["t"]
            b.w("v", "t", b.c["t"])
            if r8 == 0:
                b.w("v", "cr", rb_done[0])
            b.op("v", lambda v, par=par, gi=gi, pslot=pslot: v.tensor_copy(
                eslot(par, gi), ps[0:1, pslot:pslot + 512]), inc="v")
            gr_t[r8] = b.c["v"]
        if self.stop_after == "M1":
            self.finish(b)
            return
        for vi in range(2):
            for par in range(2):
                dst = eslot(par, 4 + vi)
                for hlf in range(2):
                    srow = (vi * 2 + hlf) * 32
                    b.op("v", lambda v, dst=dst, srow=srow, par=par, hlf=hlf:
                         v.tensor_copy(
                             dst[:, hlf * 256:hlf * 256 + 256]
                             .rearrange("p (a b) -> p a b", b=64),
                             self.xa[srow:srow + 1, 1024:1536]
                             .rearrange("p (a t b) -> p a t b", t=2, b=64)
                             [:, :, par, :]),
                         inc="v" if (vi, par, hlf) == (1, 1, 1) else None)
        vpar_v = b.c["v"]
        if self.stop_after == "M2":
            self.finish(b)
            return
        b.op("v", lambda v: v.tensor_tensor(self.csx[:, :], S1, S2, ALU.mult),
             inc="v")
        s12_v = b.c["v"]
        b.w("t", "v", s12_v)
        for par in range(2):
            ev = self.cb["etop"] if par == 0 else self.cb["ebot"]
            b.op("t", lambda t, par=par, ev=ev: t.matmul(
                ps[0:1, R0 + par * 512:R0 + par * 512 + 512], ev[:, :],
                self.csx[:, :], start=True, stop=True),
                inc="t" if par == 1 else None)
        y2s_t = b.c["t"]
        b.w("v", "t", y2s_t)
        for par in range(2):
            b.op("v", lambda v, par=par: v.tensor_tensor(
                ps[0:1, R0 + par * 512:R0 + par * 512 + 512],
                ps[0:1, R0 + par * 512:R0 + par * 512 + 512],
                eslot(par, 1), ALU.mult))
        b.op("v", lambda v: v.tensor_scalar(ps[0:1, R0:R0 + 1024],
                                            ps[0:1, R0:R0 + 1024],
                                            INVN2, None, ALU.mult), inc="v")
        y2_v = b.c["v"]
        b.w("s", "v", y2_v)
        b.op("s", lambda s: s.copy(self.sgs[0:1, 0:512], ps[0:1, R0:R0 + 512]))
        b.op("s", lambda s: s.copy(self.sgs[64:65, 0:512], ps[0:1, R0 + 512:R0 + 1024]),
             inc="s")
        y2_s = b.c["s"]
        if self.stop_after == "M3":
            self.finish(b)
            return
        if "y2r" in TAPS:
            b.w("q", "v", y2_v)
            b.dma("q", self.tap_t["y2r"], ps[0:1, R0:R0 + 1024], sem="tap")
        for par in range(2):
            b.op("v", lambda v, par=par: v.tensor_tensor(
                eslot(par, 0), eslot(par, 0), eslot(par, 5), ALU.mult))
            b.op("v", lambda v, par=par: v.tensor_tensor(
                eslot(par, 2), eslot(par, 2), eslot(par, 4), ALU.mult))
            for idx in (0, 2, 3):
                b.op("v", lambda v, par=par, idx=idx: v.tensor_scalar(
                    eslot(par, idx), eslot(par, idx), INVN2, None, ALU.mult),
                    inc="v" if (par == 1 and idx == 3) else None)
        csr_v = b.c["v"]
        if self.stop_after == "M4":
            self.finish(b)
            return
        b.w("t", "v", csr_v)
        b.w("t", "s", y2_s)
        for term, sidx2 in ((0, 0), (1, 2), (2, 3)):
            for par in range(2):
                p0 = par * 32
                b.op("t", lambda t, term=term, sidx2=sidx2, par=par, p0=p0: t.matmul(
                    ps[par * 64:par * 64 + 64,
                       R1 + term * 512:R1 + term * 512 + 512],
                    self.cb["ones_f"][p0:p0 + 1, 0:64],
                    eslot(par, sidx2), start=True, stop=True),
                    inc="t" if par == 1 else None)
        cs_t = b.c["t"]
        if self.stop_after == "M4a":
            self.finish(b)
            return
        b.w("v", "t", cs_t)
        b.op("v", lambda v: v.tensor_tensor(self.mfin[:, :], S1,
                                            ps[:, R1:R1 + 512], ALU.mult))
        b.op("v", lambda v: v.tensor_tensor(self.csx[:, :], S2,
                                            ps[:, R1 + 512:R1 + 1024], ALU.mult))
        b.op("v", lambda v: v.tensor_tensor(self.mfin[:, :], self.mfin[:, :],
                                            self.csx[:, :], ALU.add))
        b.op("v", lambda v: v.tensor_tensor(self.csx[:, :], S1, S2, ALU.mult))
        b.op("v", lambda v: v.tensor_tensor(self.csx[:, :], self.csx[:, :],
                                            ps[:, R1 + 1024:R1 + 1536], ALU.mult))
        b.op("v", lambda v: v.tensor_tensor(self.mfin[:, :], self.mfin[:, :],
                                            self.csx[:, :], ALU.add), inc="v")
        mf_v = b.c["v"]
        if self.stop_after == "M4b":
            self.finish(b)
            return
        b.w("s", "v", mf_v)
        b.op("s", lambda s: s.copy(self.mbuf[:, :], self.mfin[:, :]), inc="s")
        m_s = b.c["s"]
        if "mm" in TAPS:
            b.w("q", "v", mf_v)
            b.dma("q", self.tap_t["mm"], self.mfin[:, :], sem="tap")
        if self.stop_after == "M":
            self.finish(b)
            return

        ynat_s = {}
        for j in range(IC):
            pbase = (j % 2) * 2048
            if j >= 2:
                b.w("t", "s", ynat_s[j - 2])
            if j == 0:
                b.w("t", "s", m_s)
                b.w("t", "s", y2_s)
            for hh in range(2):
                r0 = hh * 64
                for tg in range(4):
                    b.op("t", lambda t, j=j, r0=r0, tg=tg, pbase=pbase: t.matmul(
                        ps[r0:r0 + 64, pbase + tg * 512:pbase + tg * 512 + 512],
                        self.mbuf[r0:r0 + 64, j * 64:j * 64 + 64],
                        self.big0[r0:r0 + 64,
                                  16384 + j * 2048 + tg * 512:
                                  16384 + j * 2048 + tg * 512 + 512],
                        start=True, stop=False))
                for tg in range(4):
                    b.op("t", lambda t, j=j, r0=r0, tg=tg, pbase=pbase: t.matmul(
                        ps[r0:r0 + 64, pbase + tg * 512:pbase + tg * 512 + 512],
                        self.sgs[r0:r0 + 1, j * 64:j * 64 + 64],
                        self.cb["ones_b"][r0:r0 + 1, :], start=False, stop=True),
                        inc="t" if tg == 3 else None)
            b.w("s", "t", b.c["t"])
            b.op("s", lambda s, j=j, pbase=pbase: s.copy(
                self.big1[:, j * 2048:j * 2048 + 2048],
                ps[:, pbase:pbase + 2048]), inc="s")
            ynat_s[j] = b.c["s"]
        ynat_done = b.c["s"]
        if "ynat" in TAPS:
            b.w("q", "s", ynat_done)
            b.dma("q", self.tap_t["ynat"], self.big1[:, :], sem="tap")
        if self.stop_after == "Y":
            self.finish(b)
            return

        def sqsl(cc):
            return self.wsl[:, (cc % 3) * 2048:(cc % 3) * 2048 + 2048]

        def stat_sl(k):
            return ps[(k % 3) * 32:(k % 3) * 32 + 1,
                      R0 + (k // 3) * 512:R0 + (k // 3) * 512 + 512]
        sq_v = {}
        sq_eng = {}
        ln_cc_t = {}
        for cc in range(IC):
            eng = "v" if cc % 2 == 0 else "s"
            if eng == "v":
                b.w("v", "s", ynat_s[cc])
            if cc >= 3:
                b.w(eng, "t", ln_cc_t[cc - 3])
            if eng == "v":
                b.op("v", lambda v, cc=cc: v.tensor_tensor(
                    sqsl(cc), self.big1[:, cc * 2048:cc * 2048 + 2048],
                    self.big1[:, cc * 2048:cc * 2048 + 2048], ALU.mult),
                    inc="v")
                sq_v[cc] = b.c["v"]
            else:
                b.op("s", lambda s, cc=cc: s.activation(
                    sqsl(cc), self.big1[:, cc * 2048:cc * 2048 + 2048],
                    AF.Square), inc="s")
                sq_v[cc] = b.c["s"]
            sq_eng[cc] = eng
            if cc == 0:
                b.w("t", "s", ynat_done)
            b.w("t", eng, sq_v[cc])
            for gg in range(4):
                for which in range(2):
                    k = gg + 4 * which
                    src2 = (self.big1[:, cc * 2048 + gg * 512:cc * 2048 + gg * 512 + 512]
                            if which == 0 else
                            sqsl(cc)[:, gg * 512:gg * 512 + 512])
                    b.op("t", lambda t, k=k, src2=src2, cc=cc: t.matmul(
                        stat_sl(k), self.cb["onescol_b"][:, :], src2,
                        start=(cc == 0), stop=(cc == IC - 1)),
                        inc="t" if (gg == 3 and which == 1) else None)
            ln_cc_t[cc] = b.c["t"]
        lnsum_t = b.c["t"]
        b.w("v", "t", lnsum_t)

        def syr(k):
            return self.xa[0:1, k * 512:k * 512 + 512]

        def sy2r(k):
            t2 = [self.xf, self.xf, self.ob2, self.ob2][k]
            c = [0, 512, 0, 512][k]
            return t2[0:1, c:c + 512]
        musq = self.mfin[0:1, 0:512]
        for k in range(8):
            dst = syr(k) if k < 4 else sy2r(k - 4)
            b.op("v", lambda v, k=k, dst=dst: v.tensor_copy(dst, stat_sl(k)),
                 inc="v" if k == 7 else None)
        for k in range(4):
            b.op("v", lambda v, k=k: v.tensor_scalar(syr(k), syr(k), 1.0 / C,
                                                     None, ALU.mult))
            b.op("v", lambda v, k=k: v.tensor_scalar(sy2r(k), sy2r(k), 1.0 / C,
                                                     None, ALU.mult))
            b.op("v", lambda v, k=k: v.tensor_tensor(musq, syr(k), syr(k),
                                                     ALU.mult))
            b.op("v", lambda v, k=k: v.tensor_tensor(sy2r(k), sy2r(k), musq,
                                                     ALU.subtract))
            b.op("v", lambda v, k=k: v.tensor_scalar(sy2r(k), sy2r(k), 1.0, 1e-5,
                                                     ALU.mult, ALU.add),
                 inc="v" if k == 3 else None)
        b.w("s", "v", b.c["v"])
        for k in range(4):
            b.op("s", lambda s, k=k: s.sqrt(sy2r(k), sy2r(k)),
                 inc="s" if k == 3 else None)
        b.w("v", "s", b.c["s"])
        for k in range(4):
            b.op("v", lambda v, k=k: v.reciprocal(sy2r(k), sy2r(k)))
            b.op("v", lambda v, k=k: v.tensor_tensor(syr(k), syr(k), sy2r(k),
                                                     ALU.mult))
            b.op("v", lambda v, k=k: v.tensor_scalar(syr(k), syr(k), -1.0, None,
                                                     ALU.mult),
                 inc="v" if k == 3 else None)
        stat_v = b.c["v"]
        b.w("t", "v", stat_v)
        bc_slist = []
        for k in range(8):
            which, gg = k // 4, k % 4
            row = sy2r(gg) if which == 0 else syr(gg)
            slot = R1 + (k % 2) * 512
            if k >= 2:
                b.w("t", "s", bc_slist[k - 2])
            b.op("t", lambda t, row=row, slot=slot: t.matmul(
                ps[:, slot:slot + 512], self.cb["ones_f"][0:1, :],
                row, start=True, stop=True), inc="t")
            b.w("s", "t", b.c["t"])
            b.op("s", lambda s, which=which, gg=gg, slot=slot: s.copy(
                self.big0[:, 16384 + which * 2048 + gg * 512:
                          16384 + which * 2048 + gg * 512 + 512],
                ps[:, slot:slot + 512]), inc="s")
            bc_slist.append(b.c["s"])
        bc_s = b.c["s"]
        for cc in range(IC):
            if cc == 0:
                b.w("v", "s", bc_s)
            for gg in range(4):
                slc = slice(cc * 2048 + gg * 512, cc * 2048 + gg * 512 + 512)
                abc = self.big0[:, 16384 + gg * 512:16384 + gg * 512 + 512]
                bbc = self.big0[:, 18432 + gg * 512:18432 + gg * 512 + 512]
                b.op("v", lambda v, slc=slc, abc=abc: v.tensor_tensor(
                    self.big1[:, slc], self.big1[:, slc], abc, ALU.mult))
                b.op("v", lambda v, slc=slc, bbc=bbc: v.tensor_tensor(
                    self.big1[:, slc], self.big1[:, slc], bbc, ALU.add),
                    inc="v" if gg == 3 else None)
            b.w("s", "v", b.c["v"])
            b.op("s", lambda s, cc=cc: s.activation(
                self.big1[:, cc * 2048:cc * 2048 + 2048],
                self.big1[:, cc * 2048:cc * 2048 + 2048], AF.Identity,
                bias=self.smal[:, 8 + cc:9 + cc], scale=self.smal[:, cc:cc + 1]),
                inc="s")
        ln_done2 = b.c["s"]
        if "ln" in TAPS:
            b.w("q", "s", ln_done2)
            b.dma("q", self.tap_t["ln"], self.big1[:, :], sem="tap")
        if self.stop_after == "LN":
            self.finish(b)
            return

        sfm_s = {}
        for oc in range(IC):
            pbase = (oc % 2) * 2048
            if oc >= 2:
                b.w("t", "s", sfm_s[oc - 2])
            if oc == 0:
                b.w("t", "s", ln_done2)
            for i in range(IC):
                for tg in range(4):
                    b.op("t", lambda t, oc=oc, i=i, tg=tg, pbase=pbase: t.matmul(
                        ps[:, pbase + tg * 512:pbase + tg * 512 + 512],
                        self.big0[:, 8192 + i * 1024 + oc * 128:
                                  8192 + i * 1024 + oc * 128 + 128],
                        self.big1[:, i * 2048 + tg * 512:i * 2048 + tg * 512 + 512],
                        start=(i == 0), stop=False))
            for tg in range(4):
                rr = "oa" if oc < 4 else "ob"
                c0 = (oc % 4) * 128
                rp = ROWSA[rr][0]
                b.op("t", lambda t, tg=tg, rr=rr, c0=c0, rp=rp, pbase=pbase: t.matmul(
                    ps[:, pbase + tg * 512:pbase + tg * 512 + 512],
                    self.rrow(rr, c0, 128), self.cb["ones_b"][rp:rp + 1, :],
                    start=False, stop=True), inc="t" if tg == 3 else None)
            b.w("s", "t", b.c["t"])
            b.op("s", lambda s, oc=oc, pbase=pbase: s.copy(
                self.xnb[:, oc * 2048:oc * 2048 + 2048], ps[:, pbase:pbase + 2048]),
                inc="s")
            sfm_s[oc] = b.c["s"]
        sfm_done = b.c["s"]
        if "sfm" in TAPS:
            b.w("q", "s", sfm_done)
            b.dma("q", self.tap_t["sfm"], self.xnb[:, :], sem="tap")
        if self.stop_after == "SFM":
            self.finish(b)
            return

        stm_s = {}
        for T in range(NT):
            pb = R0 + (T % 2) * 1024
            if T >= 2:
                b.w("t", "s", stm_s[T - 2])
            if T == 0:
                b.w("t", "s", sfm_done)
            for j in range(IC):
                b.op("t", lambda t, T=T, j=j, pb=pb: t.matmul(
                    ps[:, pb + j * 128:pb + j * 128 + 128],
                    self.xnb[:, j * 2048 + T * 128:j * 2048 + T * 128 + 128],
                    self.cb["i128b"][:, :], start=True, stop=True),
                    inc="t" if j == IC - 1 else None)
            b.w("s", "t", b.c["t"])
            b.op("s", lambda s, T=T, pb=pb: s.copy(
                self.big1[:, T * 1024:T * 1024 + 1024], ps[:, pb:pb + 1024]),
                inc="s")
            stm_s[T] = b.c["s"]
        if "stm" in TAPS:
            b.w("q", "s", b.c["s"])
            b.dma("q", self.tap_t["stm"], self.big1[:, :], sem="tap")
        if self.stop_after == "STM":
            self.finish(b)
            return

        fin_v = {}
        od = {}
        for T in range(NTL):
            sl = (T % 2) * 1024
            if T >= 2:
                b.w("q", "v", fin_v[T - 2])
            elif T == 0:
                b.w("q", "s", bc_s)
            xd = b.dma("q", self.xa[:, sl:sl + 1024],
                       io["x"][T * 128:(T + 1) * 128, :],
                       sem="xa" if T % 2 == 0 else "xb")
            ob = self.xf[:, 0:1024] if T % 2 == 0 else self.ob2[:, 0:1024]
            b.w("v", "s", stm_s[T])
            if T == 0:
                b.w("v", "s", silu_s)
            b.w("v", "xa" if T % 2 == 0 else "xb", xd)
            if T >= 2:
                b.w("v", "oa" if T % 2 == 0 else "ob", od[T - 2])
            b.op("v", lambda v, T=T: v.tensor_tensor(
                self.sgs[:, :], self.big1[:, T * 1024:T * 1024 + 1024],
                self.gbuf[:, T * 1024:T * 1024 + 1024], ALU.mult))
            b.op("v", lambda v, T=T, sl=sl, ob=ob: v.tensor_tensor(
                ob, self.xa[:, sl:sl + 1024], self.sgs[:, :], ALU.add), inc="v")
            fin_v[T] = b.c["v"]
            b.w("q", "v", fin_v[T])
            od[T] = b.dma("q", io["out"][T * 128:(T + 1) * 128, :], ob,
                          sem="oa" if T % 2 == 0 else "ob")
        self.finish(b)

    def finish(self, b):
        if self.stop_after:
            b.w("q", "v", b.c["v"])
            b.w("q", "s", b.c["s"])
            b.w("q", "t", b.c["t"])
            b.dma("q", self.io["out"][0:128, :], self.xa[:, 0:1024], sem="oa")
        b.w("q", "oa", b.c["oa"])
        b.w("q", "ob", b.c["ob"])
        b.w("q", "tap", b.c["tap"])
        b.w("q", "pa", b.c["pa"])
        b.w("q", "pb", b.c["pb"])
        b.w("q", "pc", b.c["pc"])
        b.w("q", "wm", b.c["wm"])
        b.w("q", "w1", b.c["w1"])

    def build(self):
        nc = self.nc
        import contextlib
        names = ["t", "v", "s", "cc",
                 "qi", "gi", "wn", "xa", "xb", "w1", "sa", "sb",
                 "cw", "cr", "oa", "ob", "tap",
                 "pa", "pb", "pc", "wm"]
        with contextlib.ExitStack() as stk:
            sems = {n: stk.enter_context(nc.semaphore("sem_" + n)) for n in names}
            block = stk.enter_context(nc.Block())

            @block.sync
            def _(e):
                self.walk(Wk("q", e, sems))

            @block.gpsimd
            def _(e):
                self.walk(Wk("g", e, sems))

            @block.tensor
            def _(e):
                self.walk(Wk("t", e, sems))

            @block.vector
            def _(e):
                self.walk(Wk("v", e, sems))

            @block.scalar
            def _(e):
                self.walk(Wk("s", e, sems))
        return nc


def build_program(taps=(), stop_after=None, nt=NT, init_only=False, a_stage=4,
                  dbg=()):
    return MK(taps=taps, stop_after=stop_after, nt=nt, init_only=init_only,
              a_stage=a_stage, dbg=dbg).build()


def make_in_maps(inputs):
    x = inputs["L0_attn_x"]
    maps = []
    for c in range(8):
        bsel = c // 4
        t0 = (c % 4) * TC
        m = {
            "x": x[bsel, t0:t0 + TC, :],
            "wq": inputs["L0_attn_Wq"], "wk1": inputs["L0_attn_Wk1"],
            "wk2": inputs["L0_attn_Wk2"], "wv1": inputs["L0_attn_Wv1"],
            "wv2": inputs["L0_attn_Wv2"], "wo": inputs["L0_attn_Wo"],
            "biases": np.stack([
                inputs["L0_attn_bq"], inputs["L0_attn_bk1"],
                inputs["L0_attn_bk2"], inputs["L0_attn_bv1"],
                inputs["L0_attn_bv2"], inputs["L0_attn_bo"]]),
            "gvec": np.stack([
                inputs["L0_attn_g1"], inputs["L0_attn_g2"],
                inputs["L0_attn_g3"], inputs["L0_attn_g4"]]),
            "lng": inputs["L0_attn_ln_g"], "lnb": inputs["L0_attn_ln_b"],
            "gw": inputs["L0_ffn_gate_w"], "gb": inputs["L0_ffn_gate_b"],
            "eb": inputs["L0_ffn_expert_bias"],
            "w1": inputs["L0_ffn_w1"], "b1": inputs["L0_ffn_b1"],
            "w2": inputs["L0_ffn_w2"], "b2": inputs["L0_ffn_b2"],
        }
        maps.append({k: np.ascontiguousarray(v, dtype=np.float32)
                     for k, v in m.items()})
    return maps


def kernel(**inputs):
    nc = build_program()
    in_maps = make_in_maps(inputs)
    res = run_bass_kernel_spmd(nc, in_maps, core_ids=list(range(8)))
    out = np.empty((2, NFULL, C), dtype=np.float32)
    for c in range(8):
        out[c // 4, (c % 4) * TC:(c % 4 + 1) * TC, :] = res.results[c]["out"]
    return out


# version 06
# speedup vs baseline: 1.0090x; 1.0090x over previous
import numpy as np
import ml_dtypes
import concourse.bass as bass
import concourse.mybir as mybir
from concourse.bass_utils import run_bass_kernel_spmd

F32 = mybir.dt.float32
BF16 = mybir.dt.bfloat16
AF = mybir.ActivationFunctionType
ALU = mybir.AluOpType
AX = mybir.AxisListType

TC = 2048
NT = 16
C = 1024
IC = 8
H = 16
E = 8
FF = 4096
FC = 32
NFULL = 8192
RG = [[0, 1, 2, 3], [4, 5, 6, 7]]
BF = ml_dtypes.bfloat16
INVN2 = 1.0 / (float(NFULL) * float(NFULL))

PS_S1 = 0
PS_S2 = 512
PS_MA = 1024
PS_MB = 1536
R0, R1, R2, R3 = 2048, 2560, 3072, 3584

ROWSA = {
    "k1a": (0, 0), "k1b": (32, 0), "k2a": (64, 0),
    "k2b": (0, 1), "v1a": (32, 1), "v1b": (64, 1),
    "v2a": (0, 2), "v2b": (32, 2), "qa": (64, 2),
    "qb": (0, 3), "oa": (32, 3), "ob": (64, 3),
}


class Wk:

    def __init__(self, cur, eng, sems):
        self.cur = cur
        self.e = eng
        self.sems = sems
        self.c = {k: 0 for k in sems}

    def dma(self, qn, out, in_, sem=None, slow=False):
        sem = sem or qn
        self.c[sem] += 16
        if self.cur == qn:
            self.e.dma_start(out=out, in_=in_,
                             allow_slow_non_contiguous=slow).then_inc(
                self.sems[sem], 16)
        return self.c[sem]

    def w(self, qn, sem, val):
        if self.cur == qn and val > 0:
            self.e.wait_ge(self.sems[sem], val)

    def op(self, qn, fn, inc=None):
        if inc:
            self.c[inc] += 1
        if self.cur == qn:
            ins = fn(self.e)
            if inc:
                ins.then_inc(self.sems[inc], 1)
        return self.c[inc] if inc else None


class MK:
    def __init__(self, taps=(), stop_after=None, nt=NT, init_only=False,
                 a_stage=4, dbg=()):
        self.taps = set(taps)
        self.stop_after = stop_after
        self.nt = nt
        self.init_only = init_only
        self.a_stage = a_stage
        self.dbg = set(dbg)
        nc = bass.Bass("TRN2", target_bir_lowering=False,
                       detect_race_conditions=False)
        self.nc = nc

        self.io = {}
        for nm, sh in (
            ("x", [TC, C]), ("wq", [C, C]), ("wk1", [C, C]), ("wk2", [C, C]),
            ("wv1", [C, C]), ("wv2", [C, C]), ("wo", [C, C]),
            ("biases", [6, C]), ("gvec", [4, H]), ("lng", [C]), ("lnb", [C]),
            ("gw", [E, C]), ("gb", [E]), ("eb", [E]),
            ("w1", [E, C, FF]), ("b1", [E, FF]), ("w2", [E, FF, C]), ("b2", [E, C]),
        ):
            self.io[nm] = nc.dram_tensor(nm, sh, F32, kind="ExternalInput").ap()
        self.io["out"] = nc.dram_tensor("out", [TC, C], F32, kind="ExternalOutput").ap()
        self.io["cc_in"] = nc.dram_tensor("cc_in", [128, 1536], F32, kind="Internal").ap()
        self.io["cc_out"] = nc.dram_tensor("cc_out", [128, 1536], F32, kind="Internal").ap()
        self.w2b = nc.dram_tensor("w2b", [E, 16, 128, 2048], BF16, kind="Internal").ap()
        self.w1b = nc.dram_tensor("w1b", [E, 128, 32768], BF16, kind="Internal").ap()

        self.tap_t = {}
        for nm, sh, dt in (
            ("xnb", [128, IC * TC], BF16), ("kv0", [128, 4096], BF16),
            ("scc", [128, 1536], F32), ("rcc", [128, 1536], F32),
            ("onehot", [128, NT * 8], F32), ("gbuf", [128, NT * C], BF16),
            ("qb", [128, 16384], BF16), ("mm", [128, 512], F32),
            ("y2r", [2, 512], F32), ("ynat", [128, IC * TC], BF16),
            ("ln", [128, IC * TC], BF16), ("sfm", [128, IC * TC], BF16),
            ("stm", [128, NT * C], BF16),
        ):
            if nm in self.taps:
                self.tap_t[nm] = nc.dram_tensor("tap_" + nm, sh, dt,
                                                kind="ExternalOutput").ap()

        def inl(name, arr):
            return nc.inline_tensor(arr, name).ap()
        et = np.zeros((128, 1), np.float32); et[:64] = 1.0
        ebv = np.zeros((128, 1), np.float32); ebv[64:] = 1.0
        st = np.zeros((8, 512), BF)
        for qq in range(512):
            st[qq // 64, qq] = 1.0
        self.consts = {
            "i128b": np.eye(128, dtype=BF), "i128f": np.eye(128, dtype=np.float32),
            "i8f": np.eye(8, dtype=np.float32),
            "ones_b": np.ones((128, 512), dtype=BF),
            "ones_f": np.ones((128, 128), np.float32),
            "onescol_b": np.ones((128, 1), dtype=BF),
            "etop": et, "ebot": ebv,
            "sel8": st,
        }
        self.cin = {k: inl(k, v) for k, v in self.consts.items()}

        A = nc.alloc_sbuf_tensor
        self.xnb = A("xnb", [128, IC * TC], BF16)
        self.gbuf = A("gbuf", [128, NT * C], BF16)
        self.big0 = A("big0", [128, 32768], BF16)
        self.big1 = A("big1", [128, 16384], BF16)
        self.wsl = A("wsl", [128, 6144], BF16)
        self.xa = A("xa", [128, 2048], F32)
        self.xf = A("xf", [128, 1024], F32)
        self.ob2 = A("ob2", [128, 1024], F32)
        self.ohm = A("ohm", [128, NT * 8], F32)
        self.sgs = A("sgs", [128, 1024], BF16)
        self.mfin = A("mfin", [128, 512], F32)
        self.mbuf = A("mbuf", [128, 512], BF16)
        self.csx = A("csx", [128, 512], F32)
        self.smal = A("smal", [128, 40], F32)
        self.rowsA = A("rowsA", [128, 2048], BF16)
        self.b1c = A("b1c", [128, 256], F32)
        self.b2n = A("b2n", [8, 1024], BF16)
        self.ge = A("ge", [8, 128], F32)
        self.geb = A("geb", [8, 128], BF16)
        self.gwT = A("gwT", [128, 64], F32)
        self.cb = {}
        for k, v in self.consts.items():
            self.cb[k] = A("c_" + k, list(v.shape),
                           BF16 if v.dtype == BF else F32)
        self.ps = nc.alloc_psum_tensor("PALL", [128, 4096], F32)

    def rrow(self, name, c0=0, n=512):
        p, g = ROWSA[name]
        return self.rowsA[p:p + 1, g * 512 + c0:g * 512 + c0 + n]

    def walk(self, b):
        io, ps = self.io, self.ps
        TAPS = self.taps

        for k in self.consts:
            b.dma("q", self.cb[k][:, :], self.cin[k], sem="qi")
        for i, pfx in enumerate(("q", "k1", "k2", "v1", "v2", "o")):
            for half, suf in ((0, "a"), (1, "b")):
                p, g = ROWSA[pfx + suf]
                b.dma("g", self.rowsA[p:p + 1, g * 512:g * 512 + 512],
                      io["biases"][i:i + 1, half * 512:half * 512 + 512], sem="gi")
        b.dma("g", self.b2n[:, :], io["b2"], sem="gi")
        b.dma("q", self.smal[:, 0:8], io["lng"].rearrange("(c p) -> p c", p=128),
              sem="qi", slow=True)
        b.dma("q", self.smal[:, 8:16], io["lnb"].rearrange("(c p) -> p c", p=128),
              sem="qi", slow=True)
        b.dma("q", self.smal[0:8, 16:17], io["gb"].rearrange("(a e) -> e a", a=1),
              sem="qi", slow=True)
        b.dma("q", self.smal[0:8, 17:18], io["eb"].rearrange("(a e) -> e a", a=1),
              sem="qi", slow=True)
        b.dma("q", self.smal[0:8, 18:26].rearrange("p (t g) -> p t g", t=2),
              io["gvec"].rearrange("g (h t) -> h t g", t=2), sem="qi", slow=True)
        b.dma("q", self.b1c[:, :], io["b1"].rearrange("e (c p) -> p (e c)", p=128),
              sem="qi", slow=True)
        qinit = b.dma("q", self.gwT[:, :],
                      io["gw"].rearrange("e (c p) -> p (e c)", p=128), sem="qi", slow=True)

        qi_all = b.c["qi"]
        gi_all = b.c["gi"]
        b.op("v", lambda v: v.memset(self.smal[:, 38:39], 1e-5), inc="v")
        eps_v = b.c["v"]

        wst = {"k": 0, "tr": {}}
        w_stage = (self.wsl[:, 0:4096], self.big1[:, 10240:14336])

        def wT_build(wnm, dst, cstride, first, addI=False):
            for half in range(2):
                k = wst["k"]
                wst["k"] += 1
                buf = w_stage[k % 2]
                sem = "wn" if k % 2 == 0 else "wm"
                if k >= 2:
                    b.w("g", "t", wst["tr"][k - 2])
                gv = b.dma("g", buf.rearrange("p (o i) -> p o i", o=4),
                           io[wnm][half * 512:half * 512 + 512, :].rearrange(
                               "(o p) i -> p o i", p=128), sem=sem)
                b.w("t", sem, gv)
                if first and half == 0:
                    b.w("t", "qi", qi_all)
                for ib in range(IC):
                    if not (first and half == 0 and ib == 0):
                        b.w("t", "v", b.c["v"])
                    mms = []
                    for ob4 in range(4):
                        diag = addI and (ob4 == ib - half * 4)
                        mms.append((ob4, False, not diag))
                        if diag:
                            mms.append((ob4, True, True))
                    for mi, (ob4, ident, stop) in enumerate(mms):
                        lastmm = mi == len(mms) - 1
                        lhs = (self.cb["i128b"][:, :] if ident else
                               buf[:, ob4 * 1024 + ib * 128:
                                   ob4 * 1024 + ib * 128 + 128])
                        b.op("t", lambda t, ob4=ob4, ident=ident, stop=stop,
                             lhs=lhs: t.matmul(
                                 ps[:, R0 + ob4 * 128:R0 + ob4 * 128 + 128],
                                 lhs, self.cb["i128b"][:, :],
                                 start=not ident, stop=stop),
                             inc="t" if lastmm else None)
                    b.w("v", "t", b.c["t"])
                    b.op("v", lambda v, ib=ib, half=half: v.tensor_copy(
                        self.big0[:, dst + ib * cstride + half * 512:
                                  dst + ib * cstride + half * 512 + 512],
                        ps[:, R0:R0 + 512]), inc="v")
                wst["tr"][k] = b.c["t"]

        pc_units = ([("w1", e0, i, hf) for e0 in range(2)
                     for i in range(IC) for hf in range(2)] +
                    [("w2", e, qd) for e in range(E) for qd in range(16)])
        pcv = {}
        PC_SEMS = ("pa", "pb", "pd", "pe")
        pc_prev = {s: None for s in PC_SEMS}

        def emit_precast_unit(kind, u_parity):
            sem = PC_SEMS[u_parity % 4]
            if pc_prev[sem] is not None:
                b.w("g", sem, pc_prev[sem])
            if kind[0] == "w1":
                _, e0, i, hf = kind
                d = b.dma("g",
                          self.w1b[e0][:, i * 4096 + hf * 2048:
                                       i * 4096 + hf * 2048 + 2048],
                          io["w1"][e0][i * 128:(i + 1) * 128,
                                       hf * 2048:(hf + 1) * 2048], sem=sem)
            else:
                _, e, qd = kind
                d = b.dma("g", self.w2b[e, qd].rearrange("p (c o) -> p c o", c=2),
                          io["w2"][e, qd * 256:(qd + 1) * 256, :].rearrange(
                              "(c p) o -> p c o", p=128), sem=sem)
            pcv[kind] = (sem, d)
            pc_prev[sem] = d

        def emit_precast(u):
            emit_precast_unit(pc_units[u], u % 4)

        for u in range(4):
            emit_precast(u)

        wT_build("wk1", 0, 2048, True, addI=True)
        wT_build("wk2", 1024, 2048, False, addI=True)
        wT_build("wv1", 16384, 2048, False, addI=True)
        wT_build("wv2", 16384 + 1024, 2048, False, addI=True)
        wkv_v = b.c["v"]
        if self.init_only:
            if "kv0" in self.taps:
                b.w("q", "v", wkv_v)
                b.dma("q", self.tap_t["kv0"], self.big0[:, 0:4096], sem="tap")
            self.finish(b)
            return
        NTL = self.nt

        xdma = {}
        ln_done = {}
        xnT_s = {}
        trv = {}
        gate_t = {}
        ge_s = {}
        maskv = {}
        zev = {}
        smm_t = {}
        xdma[0] = b.dma("q", self.xa[:, 0:1024], io["x"][0:128, :], sem="xa")
        xdma[1] = b.dma("q", self.xa[:, 1024:2048], io["x"][128:256, :], sem="xb")
        for T in range(NTL):
            sl = (T % 2) * 1024
            xt = self.xa[:, sl:sl + 1024]
            b.w("v", "xa" if T % 2 == 0 else "xb", xdma[T])
            if T == 0:
                b.w("s", "qi", qi_all)
            b.op("v", lambda v, xt=xt: v.reduce_sum(self.smal[:, 32:33], xt, AX.X),
                 inc="v")
            b.w("s", "v", b.c["v"])
            b.op("s", lambda s: s.mul(self.smal[:, 33:34], self.smal[:, 32:33],
                                      -1.0 / C), inc="s")
            b.w("v", "s", b.c["s"])
            b.op("v", lambda v, xt=xt: v.tensor_scalar(xt, xt, self.smal[:, 33:34],
                                                       None, ALU.add), inc="v")
            b.w("s", "v", b.c["v"])
            b.op("s", lambda s, xt=xt, T=T: s.activation(
                self.big1[:, 8192 + (T % 2) * 1024:8192 + (T % 2) * 1024 + 1024],
                xt, AF.Square, accum_out=self.smal[:, 33:34]))
            b.op("s", lambda s: s.activation(self.smal[:, 34:35],
                                             self.smal[:, 33:34], AF.Sqrt,
                                             bias=self.smal[:, 38:39],
                                             scale=1.0 / C), inc="s")
            b.w("v", "s", b.c["s"])
            b.op("v", lambda v: v.reciprocal(self.smal[:, 35:36], self.smal[:, 34:35]),
                 inc="v")
            b.w("s", "v", b.c["v"])
            b.op("s", lambda s, xt=xt: s.mul(xt, xt, self.smal[:, 35:36]), inc="s")
            ln_done[T] = b.c["s"]

            if self.a_stage < 1:
                xnT_s[T] = b.c["s"]
                maskv[T] = b.c["v"]
                gate_t[T] = b.c["t"]
                ge_s[T] = b.c["s"]
                zev[(T, 0, 0)] = zev[(T, 0, 1)] = b.c["s"]
                zev[(T, 1, 0)] = zev[(T, 1, 1)] = b.c["s"]
                smm_t[T] = b.c["t"]
                continue
            b.w("t", "s", ln_done[T])
            if T > 0:
                b.w("t", "s", zev[(T - 1, 1, 0)])
            for j in range(IC):
                b.op("t", lambda t, j=j, xt=xt: t.matmul(
                    ps[:, R0 + j * 128:R0 + j * 128 + 128],
                    xt[:, j * 128:j * 128 + 128], self.cb["i128f"][:, :],
                    start=True, stop=True), inc="t" if j == IC - 1 else None)
            trT = b.c["t"]
            trv[T] = trT
            b.w("s", "t", trT)
            if "noxnb" in self.dbg:
                b.op("s", lambda s: s.copy(self.xnb[:, 0:1024],
                                           ps[:, R0:R0 + 1024]), inc="s")
            else:
                b.op("s", lambda s, T=T: s.copy(
                    self.xnb[:, :].rearrange("p (c f) -> p c f", c=IC)[
                        :, :, T * 128:T * 128 + 128],
                    ps[:, R0:R0 + 1024].rearrange("p (c f) -> p c f", f=128)),
                    inc="s")
            xnT_s[T] = b.c["s"]
            b.w("v", "t", trT)
            b.w("v", "s", xnT_s[T])
            if T > 0:
                b.w("v", "t", gate_t[T - 1])
            if "noxf" in self.dbg:
                b.op("v", lambda v: v.engine_nop(), inc="v")
            else:
                b.op("v", lambda v: v.tensor_copy(self.xf[:, :], ps[:, R0:R0 + 1024]),
                     inc="v")
            xf_v = b.c["v"]

            if self.a_stage < 2:
                maskv[T] = b.c["v"]
                gate_t[T] = b.c["t"]
                ge_s[T] = b.c["s"]
                zev[(T, 0, 0)] = zev[(T, 0, 1)] = b.c["s"]
                zev[(T, 1, 0)] = zev[(T, 1, 1)] = b.c["s"]
                smm_t[T] = b.c["t"]
                continue
            b.w("t", "v", xf_v)
            b.w("t", "s", xnT_s[T])
            if T > 0:
                b.w("t", "s", ge_s[T - 1])
            for i in range(IC):
                b.op("t", lambda t, i=i: t.matmul(
                    ps[0:8, R0:R0 + 128],
                    self.gwT[:, :].rearrange("p (e c) -> p c e", c=IC)[:, i, :],
                    self.xf[:, i * 128:i * 128 + 128],
                    start=(i == 0), stop=(i == IC - 1)),
                    inc="t" if i == IC - 1 else None)
            gmm = b.c["t"]
            b.w("s", "t", gmm)
            b.op("s", lambda s: s.activation(self.ge[:, :], ps[0:8, R0:R0 + 128],
                                             AF.Identity, bias=self.smal[0:8, 16:17]))
            b.op("s", lambda s: s.add(self.ge[:, :], self.ge[:, :],
                                      self.smal[0:8, 17:18]), inc="s")
            ge_s[T] = b.c["s"]
            b.w("t", "s", ge_s[T])
            b.op("t", lambda t: t.matmul(ps[:, R0 + 128:R0 + 136], self.ge[:, :],
                                         self.cb["i8f"][:, :], start=True, stop=True),
                 inc="t")
            gate_t[T] = b.c["t"]
            b.w("v", "t", gate_t[T])
            b.op("v", lambda v: v.reduce_max(self.smal[:, 36:37],
                                             ps[:, R0 + 128:R0 + 136], AX.X),
                 inc="v")
            b.w("s", "v", b.c["v"])
            b.op("s", lambda s: s.copy(self.smal[:, 37:38], self.smal[:, 36:37]),
                 inc="s")
            b.w("v", "s", b.c["s"])
            b.op("v", lambda v, T=T: v.tensor_scalar(
                self.ohm[:, T * 8:T * 8 + 8], ps[:, R0 + 128:R0 + 136],
                self.smal[:, 37:38], None, ALU.is_ge), inc="v")
            maskv[T] = b.c["v"]

            if self.a_stage < 3:
                zev[(T, 0, 0)] = zev[(T, 0, 1)] = b.c["s"]
                zev[(T, 1, 0)] = zev[(T, 1, 1)] = b.c["s"]
                smm_t[T] = b.c["t"]
                continue
            kvs = (T % 2) * 4096
            for rnd in range(2):
                wbase = rnd * 16384
                rks = (("k1a", "k1b", "k2a", "k2b") if rnd == 0
                       else ("v1a", "v1b", "v2a", "v2b"))
                b.w("t", "s", xnT_s[T])
                b.w("t", "v", maskv[T])
                if T == 0 and rnd == 0:
                    b.w("t", "v", wkv_v)
                    b.w("t", "gi", gi_all)
                if rnd == 0 and T > 0:
                    b.w("t", "s", zev[(T - 1, 1, 1)])
                zt_g = {}
                for g4 in range(4):
                    rot = R0 + g4 * 512
                    if rnd == 1 and g4 % 2 == 0:
                        b.w("t", "s", zev[(T, 0, g4 // 2)])
                    for i in range(IC):
                        b.op("t", lambda t, i=i, g4=g4, rot=rot, wbase=wbase, T=T:
                             t.matmul(
                                 ps[:, rot:rot + 512],
                                 self.xnb[:, i * 2048 + T * 128:
                                          i * 2048 + T * 128 + 128],
                                 self.big0[:, wbase + i * 2048 + g4 * 512:
                                           wbase + i * 2048 + g4 * 512 + 512],
                                 start=(i == 0), stop=False))
                    rp = ROWSA[rks[g4]][0]
                    b.op("t", lambda t, rot=rot, rr=rks[g4], rp=rp: t.matmul(
                        ps[:, rot:rot + 512], self.cb["ones_b"][rp:rp + 1, 0:128],
                        self.rrow(rr), start=False, stop=True), inc="t")
                    zt_g[g4] = b.c["t"]
                for g2 in range(2):
                    rot = R0 + g2 * 1024
                    b.w("s", "t", zt_g[2 * g2 + 1])
                    b.op("s", lambda s, g2=g2, rot=rot, rnd=rnd, kvs=kvs: s.copy(
                        self.big1[:, kvs + rnd * 2048 + g2 * 1024:
                                  kvs + rnd * 2048 + g2 * 1024 + 1024],
                        ps[:, rot:rot + 1024]), inc="s")
                    zev[(T, rnd, g2)] = b.c["s"]

            if self.a_stage < 4:
                smm_t[T] = b.c["t"]
                if T + 2 < NTL:
                    b.w("q", "s", zev[(T, 1, 1)])
                    xdma[T + 2] = b.dma("q", self.xa[:, sl:sl + 1024],
                                        io["x"][(T + 2) * 128:(T + 3) * 128, :],
                                        sem="xa" if T % 2 == 0 else "xb")
                continue
            b.w("t", "s", zev[(T, 1, 1)])
            for sidx in range(2):
                for h in range(H):
                    r0 = (h % 2) * 64
                    koff = kvs + sidx * 1024 + h * 64
                    voff = kvs + 2048 + sidx * 1024 + h * 64
                    cc0 = sidx * 512 + (h // 2) * 64
                    b.op("t", lambda t, koff=koff, voff=voff, r0=r0, cc0=cc0, T=T, h=h:
                         t.matmul(ps[r0:r0 + 64, cc0:cc0 + 64],
                                  self.big1[:, koff:koff + 64],
                                  self.big1[:, voff:voff + 64],
                                  start=(T == 0 and h < 2),
                                  stop=(T == NTL - 1 and h >= H - 2)))
            for mi in range(4):
                vb = kvs + 2048 + (mi // 2) * 1024 + (mi % 2) * 512
                mp = (mi % 3) * 32 if mi < 3 else 0
                mcol = PS_MA if mi < 3 else PS_MB
                b.op("t", lambda t, vb=vb, mp=mp, mcol=mcol, T=T, mi=mi: t.matmul(
                    ps[mp:mp + 1, mcol:mcol + 512], self.cb["onescol_b"][:, :],
                    self.big1[:, vb:vb + 512],
                    start=(T == 0), stop=(T == NTL - 1)),
                    inc="t" if mi == 3 else None)
            smm_t[T] = b.c["t"]
            if T + 2 < NTL:
                b.w("q", "t", trv[T])
                xdma[T + 2] = b.dma("q", self.xa[:, sl:sl + 1024],
                                    io["x"][(T + 2) * 128:(T + 3) * 128, :],
                                    sem="xa" if T % 2 == 0 else "xb")
            for u in range(max(4, T * 10), min((T + 1) * 10, len(pc_units))):
                emit_precast(u)
        for u in range(max(4, NTL * 10), len(pc_units)):
            emit_precast(u)
        s_done_t = smm_t[NTL - 1]

        if self.a_stage < 4:
            if "xnb" in TAPS and self.a_stage >= 1:
                b.w("q", "s", xnT_s[NTL - 1])
                b.dma("q", self.tap_t["xnb"], self.xnb[:, :], sem="tap")
            if self.a_stage < 1:
                b.w("q", "s", ln_done[NTL - 1])
            if "onehot" in TAPS and self.a_stage >= 2:
                b.w("q", "v", maskv[NTL - 1])
                b.dma("q", self.tap_t["onehot"], self.ohm[:, :], sem="tap")
            if "kv0" in TAPS and self.a_stage >= 3:
                b.w("q", "s", zev[(max(0, NTL - 2), 1, 1)])
                b.dma("q", self.tap_t["kv0"], self.big1[:, 0:4096], sem="tap")
            self.finish(b)
            return
        b.w("v", "t", s_done_t)
        b.op("v", lambda v: v.tensor_copy(self.xa[:, 0:1024], ps[:, 0:1024]))
        b.op("v", lambda v: v.memset(self.xa[:, 1024:1536], 0.0))
        b.op("v", lambda v: v.tensor_copy(self.xa[0:1, 1024:1536], ps[0:1, PS_MA:PS_MA + 512]))
        b.op("v", lambda v: v.tensor_copy(self.xa[32:33, 1024:1536], ps[32:33, PS_MA:PS_MA + 512]))
        b.op("v", lambda v: v.tensor_copy(self.xa[64:65, 1024:1536], ps[64:65, PS_MA:PS_MA + 512]))
        b.op("v", lambda v: v.tensor_copy(self.xa[96:97, 1024:1536], ps[0:1, PS_MB:PS_MB + 512]),
             inc="v")
        ccv = b.c["v"]
        if "scc" in TAPS:
            b.w("q", "v", ccv)
            b.dma("q", self.tap_t["scc"], self.xa[:, 0:1536], sem="tap")
        b.w("g", "v", ccv)
        ccw = b.dma("g", io["cc_in"], self.xa[:, 0:1536], sem="cw")
        if self.stop_after != "A":
            b.w("g", "cw", ccw)
            b.op("g", lambda g: g.collective_compute(
                "AllReduce", ALU.add, replica_groups=RG,
                ins=[io["cc_in"]], outs=[io["cc_out"]]), inc="cc")

        if "xnb" in TAPS:
            b.w("q", "s", xnT_s[NTL - 1])
            b.dma("q", self.tap_t["xnb"], self.xnb[:, :], sem="tap")
        if "kv0" in TAPS:
            b.w("q", "s", zev[(max(0, NTL - 2), 1, 1)])
            b.dma("q", self.tap_t["kv0"], self.big1[:, 0:4096], sem="tap")
        if "onehot" in TAPS:
            b.w("q", "v", maskv[NTL - 1])
            b.dma("q", self.tap_t["onehot"], self.ohm[:, :], sem="tap")

        if self.stop_after == "A":
            self.finish(b)
            return

        b.w("g", "t", s_done_t)
        for i in range(IC):
            for hf in range(2):
                ps_, pv_ = pcv[("w1", 0, i, hf)]
                b.w("g", ps_, pv_)
        w1v = [None] * E
        for i in range(IC):
            w1v[0] = b.dma("g", self.big0[:, i * 4096:(i + 1) * 4096],
                           self.w1b[0][:, i * 4096:(i + 1) * 4096], sem="w1")
        slabv = {}
        quad_t = {}
        slab_n = [0]
        rb_done = [0]
        SLAB_SEMS = ("sa", "sb", "pc")

        def emit_slab(e, ts, qd, wait=None):
            gidx = slab_n[0]
            slab_n[0] += 1
            sidx = gidx % 3
            if wait is not None:
                b.w("q", "t", wait)
            psem, pval = pcv[("w2", e, qd)]
            b.w("q", psem, pval)
            slabv[(e, ts, qd)] = (sidx, b.dma(
                "q", self.wsl[:, sidx * 2048:sidx * 2048 + 2048],
                self.w2b[e, qd], sem=SLAB_SEMS[sidx]))

        h_ev = {}
        h_last_t = {}
        yev = {}
        yev_g0 = {}
        yev_h0 = {}

        for e in range(E):
            prev = (e - 1, 3) if e > 0 else None
            for ts in range(4):
                for qd0 in range(3):
                    wait = None
                    if (e, ts) != (0, 0):
                        pe, pts = (e, ts - 1) if ts > 0 else (e - 1, 3)
                        wait = quad_t[(pe, pts, 13 + qd0)]
                    emit_slab(e, ts, qd0, wait)
                if 1 <= e < E - 1:
                    for uu in range(4):
                        idx = ts * 4 + uu
                        emit_precast_unit(("w1", e + 1, idx // 2, idx % 2), uu)
                if e == 0 and ts == 0:
                    b.w("g", "cc", 1)
                    rb_done[0] = b.dma("g", self.xa[:, 0:1536], io["cc_out"],
                                       sem="cr")
                    if "rcc" in TAPS:
                        b.w("q", "cr", rb_done[0])
                        b.dma("q", self.tap_t["rcc"], self.xa[:, 0:1536], sem="tap")
                t0 = ts * 512
                prev2 = (e, ts - 1) if ts > 0 else prev
                for fc in range(FC):
                    hb = (fc % 2) * 512
                    if fc == 0 and ts == 0:
                        b.w("t", "w1", w1v[e])
                    if fc >= 2:
                        b.w("t", "s", h_ev[(e, ts, fc - 2)])
                    elif prev2 is not None:
                        b.w("t", "v", (yev_h0 if fc == 0 else yev_g0)[prev2])
                    elif fc == 0:
                        b.w("t", "v", ccv)
                    for i in range(IC):
                        b.op("t", lambda t, i=i, fc=fc, hb=hb, t0=t0: t.matmul(
                            ps[:, hb:hb + 512],
                            self.big0[:, i * 4096 + fc * 128:
                                      i * 4096 + fc * 128 + 128],
                            self.xnb[:, i * 2048 + t0:i * 2048 + t0 + 512],
                            start=(i == 0), stop=(i == IC - 1)),
                            inc="t" if i == IC - 1 else None)
                    b.w("s", "t", b.c["t"])
                    b.op("s", lambda s, fc=fc, hb=hb, e=e: s.activation(
                        self.big1[:, fc * 512:fc * 512 + 512], ps[:, hb:hb + 512],
                        AF.Relu, bias=self.b1c[:, e * 32 + fc:e * 32 + fc + 1]),
                        inc="s")
                    h_ev[(e, ts, fc)] = b.c["s"]
                h_last_t[(e, ts)] = b.c["t"]

                if prev2 is not None:
                    b.w("t", "v", yev[prev2])
                else:
                    b.w("t", "v", ccv)
                for fc in range(FC):
                    if fc % 2 == 0:
                        sidx0, sv0 = slabv[(e, ts, fc // 2)]
                        b.w("t", SLAB_SEMS[sidx0], sv0)
                    sidx = slabv[(e, ts, fc // 2)][0]
                    gseq = (2, 3, 1, 0) if fc == 0 else (0, 1, 2, 3)
                    for g in gseq:
                        if fc == 0 and g == 0:
                            b.w("t", "s", h_ev[(e, ts, FC - 1)])
                        for og in range(2):
                            b.op("t", lambda t, fc=fc, g=g, og=og, sidx=sidx: t.matmul(
                                ps[:, g * 1024 + og * 512:g * 1024 + og * 512 + 512],
                                self.big1[:, fc * 512 + g * 128:fc * 512 + g * 128 + 128],
                                self.wsl[:, sidx * 2048 + (fc % 2) * 1024 + og * 512:
                                         sidx * 2048 + (fc % 2) * 1024 + og * 512 + 512],
                                start=(fc == 0), stop=(fc == FC - 1)),
                                inc="t" if (fc % 2 == 1 and g == 3 and og == 1)
                                else None)
                    if fc % 2 == 1:
                        qd = fc // 2
                        quad_t[(e, ts, qd)] = b.c["t"]
                        if qd + 3 < 16:
                            emit_slab(e, ts, qd + 3, quad_t[(e, ts, qd)])
                        if ts == 3 and e + 1 < E and qd < 8:
                            if qd == 0:
                                b.w("g", "t", h_last_t[(e, 3)])
                            for hf in range(2):
                                ps_, pv_ = pcv[("w1", e + 1, qd, hf)]
                                b.w("g", ps_, pv_)
                            w1v[e + 1] = b.dma(
                                "g",
                                self.big0[:, qd * 4096:qd * 4096 + 4096],
                                self.w1b[e + 1][:, qd * 4096:qd * 4096 + 4096],
                                sem="w1")
                ymm = b.c["t"]
                b.w("v", "t", ymm)
                for g in range(4):
                    tile = ts * 4 + g
                    for hh in range(2 if g == 0 else 1):
                        wid = 512 if g == 0 else 1024
                        c0 = tile * 1024 + hh * 512
                        p0 = g * 1024 + hh * 512
                        if e == 0:
                            b.op("v", lambda v, c0=c0, p0=p0, wid=wid, tile=tile,
                                 e=e: v.tensor_scalar(
                                     self.gbuf[:, c0:c0 + wid],
                                     ps[:, p0:p0 + wid],
                                     self.ohm[:, tile * 8 + e:tile * 8 + e + 1],
                                     None, ALU.mult), inc="v")
                        else:
                            b.op("v", lambda v, c0=c0, p0=p0, wid=wid, tile=tile,
                                 e=e: v.scalar_tensor_tensor(
                                     self.gbuf[:, c0:c0 + wid],
                                     ps[:, p0:p0 + wid],
                                     self.ohm[:, tile * 8 + e:tile * 8 + e + 1],
                                     self.gbuf[:, c0:c0 + wid],
                                     ALU.mult, ALU.add), inc="v")
                        if g == 0 and hh == 0:
                            yev_h0[(e, ts)] = b.c["v"]
                    if g == 0:
                        yev_g0[(e, ts)] = b.c["v"]
                yev[(e, ts)] = b.c["v"]

        bm_s = {}
        bm_v = {}

        def emit_b2mix(tile):
            pslot = (tile % 2) * 1024
            if tile == 0:
                b.w("t", "v", yev[(E - 1, 3)])
            if tile >= 2:
                b.w("t", "v", bm_v[tile - 2])
            b.op("t", lambda t, tile=tile, pslot=pslot: t.matmul(
                ps[0:8, pslot + 512:pslot + 640],
                self.ohm[:, tile * 8:tile * 8 + 8], self.cb["i128f"][:, :],
                start=True, stop=True), inc="t")
            b.w("s", "t", b.c["t"])
            if tile >= 2:
                b.w("s", "s", bm_s[tile - 2])
            b.op("s", lambda s, tile=tile, pslot=pslot: s.copy(
                self.geb[:, :], ps[0:8, pslot + 512:pslot + 640]), inc="s")
            bm_s[tile] = b.c["s"]
            b.w("t", "s", bm_s[tile])
            for og in range(2):
                b.op("t", lambda t, og=og, pslot=pslot: t.matmul(
                    ps[:, pslot + og * 512:pslot + og * 512 + 512],
                    self.geb[:, :], self.b2n[:, og * 512:og * 512 + 512],
                    start=True, stop=True), inc="t" if og == 1 else None)
            b.w("v", "t", b.c["t"])
            b.op("v", lambda v, tile=tile, pslot=pslot: v.tensor_tensor(
                self.gbuf[:, tile * 1024:tile * 1024 + 1024],
                self.gbuf[:, tile * 1024:tile * 1024 + 1024],
                ps[:, pslot:pslot + 1024], ALU.add), inc="v")
            bm_v[tile] = b.c["v"]
            b.w("s", "v", bm_v[tile])
            b.op("s", lambda s, tile=tile: s.activation(
                self.gbuf[:, tile * 1024:tile * 1024 + 1024],
                self.gbuf[:, tile * 1024:tile * 1024 + 1024], AF.Silu), inc="s")

        if self.stop_after == "FFN":
            for tile in range(NT):
                emit_b2mix(tile)
            silu_s = b.c["s"]
            if "gbuf" in TAPS:
                b.w("q", "s", silu_s)
                b.dma("q", self.tap_t["gbuf"], self.gbuf[:, :], sem="tap")
            self.finish(b)
            return

        b.w("g", "t", quad_t[(E - 1, 3, 15)])
        tst = {"k": 0, "tr": {}}
        for wnm, base in (("wq", 0), ("wo", 8192)):
            addI = wnm == "wq"
            for half in range(2):
                k = tst["k"]
                tst["k"] += 1
                buf = w_stage[k % 2]
                sem = "wn" if k % 2 == 0 else "wm"
                if k >= 2:
                    b.w("g", "t", tst["tr"][k - 2])
                gv = b.dma("g", buf.rearrange("p (o i) -> p o i", o=4),
                           io[wnm][half * 512:half * 512 + 512, :].rearrange(
                               "(o p) i -> p o i", p=128), sem=sem)
                for tile in range(k * 4, k * 4 + 4):
                    emit_b2mix(tile)
                b.w("t", sem, gv)
                for ib in range(IC):
                    b.w("t", "v", b.c["v"])
                    mms = []
                    for ob4 in range(4):
                        diag = addI and (ob4 == ib - half * 4)
                        mms.append((ob4, False, not diag))
                        if diag:
                            mms.append((ob4, True, True))
                    for mi, (ob4, ident, stop) in enumerate(mms):
                        lastmm = mi == len(mms) - 1
                        lhs = (self.cb["i128b"][:, :] if ident else
                               buf[:, ob4 * 1024 + ib * 128:
                                   ob4 * 1024 + ib * 128 + 128])
                        b.op("t", lambda t, lhs=lhs, ob4=ob4, ident=ident,
                             stop=stop: t.matmul(
                                 ps[:, R0 + ob4 * 128:R0 + ob4 * 128 + 128],
                                 lhs, self.cb["i128b"][:, :],
                                 start=not ident, stop=stop),
                             inc="t" if lastmm else None)
                    b.w("v", "t", b.c["t"])
                    b.op("v", lambda v, ib=ib, base=base, half=half: v.tensor_copy(
                        self.big0[:, base + ib * 1024 + half * 512:
                                  base + ib * 1024 + half * 512 + 512],
                        ps[:, R0:R0 + 512]), inc="v")
                tst["tr"][k] = b.c["t"]
        wT_v = b.c["v"]
        silu_s = b.c["s"]
        if "gbuf" in TAPS:
            b.w("q", "s", silu_s)
            b.dma("q", self.tap_t["gbuf"], self.gbuf[:, :], sem="tap")

        qv = {}
        for cc in range(IC):
            pbase = (cc % 2) * 2048
            if cc >= 2:
                b.w("t", "s", qv[cc - 2])
            if cc == 0:
                b.w("t", "v", wT_v)
                b.w("t", "v", bm_v[NT - 1])
            for i in range(IC):
                for tg in range(4):
                    b.op("t", lambda t, cc=cc, i=i, tg=tg, pbase=pbase: t.matmul(
                        ps[:, pbase + tg * 512:pbase + tg * 512 + 512],
                        self.big0[:, i * 1024 + cc * 128:i * 1024 + cc * 128 + 128],
                        self.xnb[:, i * 2048 + tg * 512:i * 2048 + tg * 512 + 512],
                        start=(i == 0), stop=False))
            for tg in range(4):
                rr = "qa" if cc < 4 else "qb"
                c0 = (cc % 4) * 128
                rp = ROWSA[rr][0]
                b.op("t", lambda t, tg=tg, rr=rr, c0=c0, rp=rp, pbase=pbase: t.matmul(
                    ps[:, pbase + tg * 512:pbase + tg * 512 + 512],
                    self.rrow(rr, c0, 128), self.cb["ones_b"][rp:rp + 1, :],
                    start=False, stop=True), inc="t" if tg == 3 else None)
            b.w("s", "t", b.c["t"])
            b.op("s", lambda s, cc=cc, pbase=pbase: s.copy(
                self.big0[:, 16384 + cc * 2048:16384 + cc * 2048 + 2048],
                ps[:, pbase:pbase + 2048]), inc="s")
            qv[cc] = b.c["s"]
        q_done = b.c["s"]
        if "qb" in TAPS:
            b.w("q", "s", q_done)
            b.dma("q", self.tap_t["qb"], self.big0[:, 16384:32768], sem="tap")
        if self.stop_after == "Q":
            self.finish(b)
            return

        b.w("t", "s", q_done)
        b.w("t", "cr", rb_done[0])
        S1 = self.xa[:, 0:512]
        S2 = self.xa[:, 512:1024]

        def eslot(par, i):
            p = par * 32
            return [self.xf[p:p + 1, 0:512], self.xf[p:p + 1, 512:1024],
                    self.ob2[p:p + 1, 0:512], self.ob2[p:p + 1, 512:1024],
                    self.xa[p:p + 1, 1536:2048], self.mfin[p:p + 1, 0:512]][i]
        gr_t = {}
        gr_mm = {}
        for r8 in range(8):
            gi, par = r8 // 2, r8 % 2
            gcol = (18 + gi) if par == 0 else (22 + gi)
            pslot = R0 + (r8 % 2) * 512
            if r8 == 0:
                b.w("s", "v", bm_v[NT - 1])
            else:
                b.w("s", "t", gr_mm[r8 - 1])
            b.op("s", lambda s, gcol=gcol: s.copy(
                self.geb[0:8, 0:1], self.smal[0:8, gcol:gcol + 1]), inc="s")
            b.w("t", "s", b.c["s"])
            if r8 >= 2:
                b.w("t", "v", gr_t[r8 - 2])
            b.op("t", lambda t, pslot=pslot: t.matmul(
                ps[0:1, pslot:pslot + 512],
                self.geb[0:8, 0:1], self.cb["sel8"][:, :],
                start=True, stop=True), inc="t")
            gr_mm[r8] = b.c["t"]
            b.w("v", "t", b.c["t"])
            if r8 == 0:
                b.w("v", "cr", rb_done[0])
            b.op("v", lambda v, par=par, gi=gi, pslot=pslot: v.tensor_copy(
                eslot(par, gi), ps[0:1, pslot:pslot + 512]), inc="v")
            gr_t[r8] = b.c["v"]
        if self.stop_after == "M1":
            self.finish(b)
            return
        for vi in range(2):
            for par in range(2):
                dst = eslot(par, 4 + vi)
                for hlf in range(2):
                    srow = (vi * 2 + hlf) * 32
                    b.op("v", lambda v, dst=dst, srow=srow, par=par, hlf=hlf:
                         v.tensor_copy(
                             dst[:, hlf * 256:hlf * 256 + 256]
                             .rearrange("p (a b) -> p a b", b=64),
                             self.xa[srow:srow + 1, 1024:1536]
                             .rearrange("p (a t b) -> p a t b", t=2, b=64)
                             [:, :, par, :]),
                         inc="v" if (vi, par, hlf) == (1, 1, 1) else None)
        vpar_v = b.c["v"]
        if self.stop_after == "M2":
            self.finish(b)
            return
        b.op("v", lambda v: v.tensor_tensor(self.csx[:, :], S1, S2, ALU.mult),
             inc="v")
        s12_v = b.c["v"]
        b.w("t", "v", s12_v)
        for par in range(2):
            ev = self.cb["etop"] if par == 0 else self.cb["ebot"]
            b.op("t", lambda t, par=par, ev=ev: t.matmul(
                ps[0:1, R0 + par * 512:R0 + par * 512 + 512], ev[:, :],
                self.csx[:, :], start=True, stop=True),
                inc="t" if par == 1 else None)
        y2s_t = b.c["t"]
        b.w("v", "t", y2s_t)
        for par in range(2):
            b.op("v", lambda v, par=par: v.tensor_tensor(
                ps[0:1, R0 + par * 512:R0 + par * 512 + 512],
                ps[0:1, R0 + par * 512:R0 + par * 512 + 512],
                eslot(par, 1), ALU.mult))
        b.op("v", lambda v: v.tensor_scalar(ps[0:1, R0:R0 + 1024],
                                            ps[0:1, R0:R0 + 1024],
                                            INVN2, None, ALU.mult), inc="v")
        y2_v = b.c["v"]
        b.w("s", "v", y2_v)
        b.op("s", lambda s: s.copy(self.sgs[0:1, 0:512], ps[0:1, R0:R0 + 512]))
        b.op("s", lambda s: s.copy(self.sgs[64:65, 0:512], ps[0:1, R0 + 512:R0 + 1024]),
             inc="s")
        y2_s = b.c["s"]
        if self.stop_after == "M3":
            self.finish(b)
            return
        if "y2r" in TAPS:
            b.w("q", "v", y2_v)
            b.dma("q", self.tap_t["y2r"], ps[0:1, R0:R0 + 1024], sem="tap")
        for par in range(2):
            b.op("v", lambda v, par=par: v.tensor_tensor(
                eslot(par, 0), eslot(par, 0), eslot(par, 5), ALU.mult))
            b.op("v", lambda v, par=par: v.tensor_tensor(
                eslot(par, 2), eslot(par, 2), eslot(par, 4), ALU.mult))
            for idx in (0, 2, 3):
                b.op("v", lambda v, par=par, idx=idx: v.tensor_scalar(
                    eslot(par, idx), eslot(par, idx), INVN2, None, ALU.mult),
                    inc="v" if (par == 1 and idx == 3) else None)
        csr_v = b.c["v"]
        if self.stop_after == "M4":
            self.finish(b)
            return
        b.w("t", "v", csr_v)
        b.w("t", "s", y2_s)
        for term, sidx2 in ((0, 0), (1, 2), (2, 3)):
            for par in range(2):
                p0 = par * 32
                b.op("t", lambda t, term=term, sidx2=sidx2, par=par, p0=p0: t.matmul(
                    ps[par * 64:par * 64 + 64,
                       R1 + term * 512:R1 + term * 512 + 512],
                    self.cb["ones_f"][p0:p0 + 1, 0:64],
                    eslot(par, sidx2), start=True, stop=True),
                    inc="t" if par == 1 else None)
        cs_t = b.c["t"]
        if self.stop_after == "M4a":
            self.finish(b)
            return
        b.w("v", "t", cs_t)
        b.op("v", lambda v: v.tensor_tensor(self.mfin[:, :], S1,
                                            ps[:, R1:R1 + 512], ALU.mult))
        b.op("v", lambda v: v.tensor_tensor(self.csx[:, :], S2,
                                            ps[:, R1 + 512:R1 + 1024], ALU.mult))
        b.op("v", lambda v: v.tensor_tensor(self.mfin[:, :], self.mfin[:, :],
                                            self.csx[:, :], ALU.add))
        b.op("v", lambda v: v.tensor_tensor(self.csx[:, :], S1, S2, ALU.mult))
        b.op("v", lambda v: v.tensor_tensor(self.csx[:, :], self.csx[:, :],
                                            ps[:, R1 + 1024:R1 + 1536], ALU.mult))
        b.op("v", lambda v: v.tensor_tensor(self.mfin[:, :], self.mfin[:, :],
                                            self.csx[:, :], ALU.add), inc="v")
        mf_v = b.c["v"]
        if self.stop_after == "M4b":
            self.finish(b)
            return
        b.w("s", "v", mf_v)
        b.op("s", lambda s: s.copy(self.mbuf[:, :], self.mfin[:, :]), inc="s")
        m_s = b.c["s"]
        if "mm" in TAPS:
            b.w("q", "v", mf_v)
            b.dma("q", self.tap_t["mm"], self.mfin[:, :], sem="tap")
        if self.stop_after == "M":
            self.finish(b)
            return

        ynat_s = {}
        for j in range(IC):
            pbase = (j % 2) * 2048
            if j >= 2:
                b.w("t", "s", ynat_s[j - 2])
            if j == 0:
                b.w("t", "s", m_s)
                b.w("t", "s", y2_s)
            for hh in range(2):
                r0 = hh * 64
                for tg in range(4):
                    b.op("t", lambda t, j=j, r0=r0, tg=tg, pbase=pbase: t.matmul(
                        ps[r0:r0 + 64, pbase + tg * 512:pbase + tg * 512 + 512],
                        self.mbuf[r0:r0 + 64, j * 64:j * 64 + 64],
                        self.big0[r0:r0 + 64,
                                  16384 + j * 2048 + tg * 512:
                                  16384 + j * 2048 + tg * 512 + 512],
                        start=True, stop=False))
                for tg in range(4):
                    b.op("t", lambda t, j=j, r0=r0, tg=tg, pbase=pbase: t.matmul(
                        ps[r0:r0 + 64, pbase + tg * 512:pbase + tg * 512 + 512],
                        self.sgs[r0:r0 + 1, j * 64:j * 64 + 64],
                        self.cb["ones_b"][r0:r0 + 1, :], start=False, stop=True),
                        inc="t" if tg == 3 else None)
            b.w("s", "t", b.c["t"])
            b.op("s", lambda s, j=j, pbase=pbase: s.copy(
                self.big1[:, j * 2048:j * 2048 + 2048],
                ps[:, pbase:pbase + 2048]), inc="s")
            ynat_s[j] = b.c["s"]
        ynat_done = b.c["s"]
        if "ynat" in TAPS:
            b.w("q", "s", ynat_done)
            b.dma("q", self.tap_t["ynat"], self.big1[:, :], sem="tap")
        if self.stop_after == "Y":
            self.finish(b)
            return

        def sqsl(cc):
            return self.wsl[:, (cc % 3) * 2048:(cc % 3) * 2048 + 2048]

        def stat_sl(k):
            return ps[(k % 3) * 32:(k % 3) * 32 + 1,
                      R0 + (k // 3) * 512:R0 + (k // 3) * 512 + 512]
        sq_v = {}
        sq_eng = {}
        ln_cc_t = {}
        for cc in range(IC):
            eng = "v" if cc % 2 == 0 else "s"
            if eng == "v":
                b.w("v", "s", ynat_s[cc])
            if cc >= 3:
                b.w(eng, "t", ln_cc_t[cc - 3])
            if eng == "v":
                b.op("v", lambda v, cc=cc: v.tensor_tensor(
                    sqsl(cc), self.big1[:, cc * 2048:cc * 2048 + 2048],
                    self.big1[:, cc * 2048:cc * 2048 + 2048], ALU.mult),
                    inc="v")
                sq_v[cc] = b.c["v"]
            else:
                b.op("s", lambda s, cc=cc: s.activation(
                    sqsl(cc), self.big1[:, cc * 2048:cc * 2048 + 2048],
                    AF.Square), inc="s")
                sq_v[cc] = b.c["s"]
            sq_eng[cc] = eng
            if cc == 0:
                b.w("t", "s", ynat_done)
            b.w("t", eng, sq_v[cc])
            for gg in range(4):
                for which in range(2):
                    k = gg + 4 * which
                    src2 = (self.big1[:, cc * 2048 + gg * 512:cc * 2048 + gg * 512 + 512]
                            if which == 0 else
                            sqsl(cc)[:, gg * 512:gg * 512 + 512])
                    b.op("t", lambda t, k=k, src2=src2, cc=cc: t.matmul(
                        stat_sl(k), self.cb["onescol_b"][:, :], src2,
                        start=(cc == 0), stop=(cc == IC - 1)),
                        inc="t" if (gg == 3 and which == 1) else None)
            ln_cc_t[cc] = b.c["t"]
        lnsum_t = b.c["t"]
        b.w("v", "t", lnsum_t)

        def syr(k):
            return self.xa[0:1, k * 512:k * 512 + 512]

        def sy2r(k):
            t2 = [self.xf, self.xf, self.ob2, self.ob2][k]
            c = [0, 512, 0, 512][k]
            return t2[0:1, c:c + 512]
        musq = self.mfin[0:1, 0:512]
        for k in range(8):
            dst = syr(k) if k < 4 else sy2r(k - 4)
            b.op("v", lambda v, k=k, dst=dst: v.tensor_copy(dst, stat_sl(k)),
                 inc="v" if k == 7 else None)
        for k in range(4):
            b.op("v", lambda v, k=k: v.tensor_scalar(syr(k), syr(k), 1.0 / C,
                                                     None, ALU.mult))
            b.op("v", lambda v, k=k: v.tensor_scalar(sy2r(k), sy2r(k), 1.0 / C,
                                                     None, ALU.mult))
            b.op("v", lambda v, k=k: v.tensor_tensor(musq, syr(k), syr(k),
                                                     ALU.mult))
            b.op("v", lambda v, k=k: v.tensor_tensor(sy2r(k), sy2r(k), musq,
                                                     ALU.subtract))
            b.op("v", lambda v, k=k: v.tensor_scalar(sy2r(k), sy2r(k), 1.0, 1e-5,
                                                     ALU.mult, ALU.add),
                 inc="v" if k == 3 else None)
        b.w("s", "v", b.c["v"])
        for k in range(4):
            b.op("s", lambda s, k=k: s.sqrt(sy2r(k), sy2r(k)),
                 inc="s" if k == 3 else None)
        b.w("v", "s", b.c["s"])
        for k in range(4):
            b.op("v", lambda v, k=k: v.reciprocal(sy2r(k), sy2r(k)))
            b.op("v", lambda v, k=k: v.tensor_tensor(syr(k), syr(k), sy2r(k),
                                                     ALU.mult))
            b.op("v", lambda v, k=k: v.tensor_scalar(syr(k), syr(k), -1.0, None,
                                                     ALU.mult),
                 inc="v" if k == 3 else None)
        stat_v = b.c["v"]
        b.w("t", "v", stat_v)
        bc_slist = []
        for k in range(8):
            which, gg = k // 4, k % 4
            row = sy2r(gg) if which == 0 else syr(gg)
            slot = R1 + (k % 2) * 512
            if k >= 2:
                b.w("t", "s", bc_slist[k - 2])
            b.op("t", lambda t, row=row, slot=slot: t.matmul(
                ps[:, slot:slot + 512], self.cb["ones_f"][0:1, :],
                row, start=True, stop=True), inc="t")
            b.w("s", "t", b.c["t"])
            b.op("s", lambda s, which=which, gg=gg, slot=slot: s.copy(
                self.big0[:, 16384 + which * 2048 + gg * 512:
                          16384 + which * 2048 + gg * 512 + 512],
                ps[:, slot:slot + 512]), inc="s")
            bc_slist.append(b.c["s"])
        bc_s = b.c["s"]
        for cc in range(IC):
            if cc == 0:
                b.w("v", "s", bc_s)
            for gg in range(4):
                slc = slice(cc * 2048 + gg * 512, cc * 2048 + gg * 512 + 512)
                abc = self.big0[:, 16384 + gg * 512:16384 + gg * 512 + 512]
                bbc = self.big0[:, 18432 + gg * 512:18432 + gg * 512 + 512]
                b.op("v", lambda v, slc=slc, abc=abc: v.tensor_tensor(
                    self.big1[:, slc], self.big1[:, slc], abc, ALU.mult))
                b.op("v", lambda v, slc=slc, bbc=bbc: v.tensor_tensor(
                    self.big1[:, slc], self.big1[:, slc], bbc, ALU.add),
                    inc="v" if gg == 3 else None)
            b.w("s", "v", b.c["v"])
            b.op("s", lambda s, cc=cc: s.activation(
                self.big1[:, cc * 2048:cc * 2048 + 2048],
                self.big1[:, cc * 2048:cc * 2048 + 2048], AF.Identity,
                bias=self.smal[:, 8 + cc:9 + cc], scale=self.smal[:, cc:cc + 1]),
                inc="s")
        ln_done2 = b.c["s"]
        if "ln" in TAPS:
            b.w("q", "s", ln_done2)
            b.dma("q", self.tap_t["ln"], self.big1[:, :], sem="tap")
        if self.stop_after == "LN":
            self.finish(b)
            return

        sfm_s = {}
        for oc in range(IC):
            pbase = (oc % 2) * 2048
            if oc >= 2:
                b.w("t", "s", sfm_s[oc - 2])
            if oc == 0:
                b.w("t", "s", ln_done2)
            for i in range(IC):
                for tg in range(4):
                    b.op("t", lambda t, oc=oc, i=i, tg=tg, pbase=pbase: t.matmul(
                        ps[:, pbase + tg * 512:pbase + tg * 512 + 512],
                        self.big0[:, 8192 + i * 1024 + oc * 128:
                                  8192 + i * 1024 + oc * 128 + 128],
                        self.big1[:, i * 2048 + tg * 512:i * 2048 + tg * 512 + 512],
                        start=(i == 0), stop=False))
            for tg in range(4):
                rr = "oa" if oc < 4 else "ob"
                c0 = (oc % 4) * 128
                rp = ROWSA[rr][0]
                b.op("t", lambda t, tg=tg, rr=rr, c0=c0, rp=rp, pbase=pbase: t.matmul(
                    ps[:, pbase + tg * 512:pbase + tg * 512 + 512],
                    self.rrow(rr, c0, 128), self.cb["ones_b"][rp:rp + 1, :],
                    start=False, stop=True), inc="t" if tg == 3 else None)
            b.w("s", "t", b.c["t"])
            b.op("s", lambda s, oc=oc, pbase=pbase: s.copy(
                self.xnb[:, oc * 2048:oc * 2048 + 2048], ps[:, pbase:pbase + 2048]),
                inc="s")
            sfm_s[oc] = b.c["s"]
        sfm_done = b.c["s"]
        if "sfm" in TAPS:
            b.w("q", "s", sfm_done)
            b.dma("q", self.tap_t["sfm"], self.xnb[:, :], sem="tap")
        if self.stop_after == "SFM":
            self.finish(b)
            return

        stm_s = {}
        for T in range(NT):
            pb = R0 + (T % 2) * 1024
            if T >= 2:
                b.w("t", "s", stm_s[T - 2])
            if T == 0:
                b.w("t", "s", sfm_done)
            for j in range(IC):
                b.op("t", lambda t, T=T, j=j, pb=pb: t.matmul(
                    ps[:, pb + j * 128:pb + j * 128 + 128],
                    self.xnb[:, j * 2048 + T * 128:j * 2048 + T * 128 + 128],
                    self.cb["i128b"][:, :], start=True, stop=True),
                    inc="t" if j == IC - 1 else None)
            b.w("s", "t", b.c["t"])
            b.op("s", lambda s, T=T, pb=pb: s.copy(
                self.big1[:, T * 1024:T * 1024 + 1024], ps[:, pb:pb + 1024]),
                inc="s")
            stm_s[T] = b.c["s"]
        if "stm" in TAPS:
            b.w("q", "s", b.c["s"])
            b.dma("q", self.tap_t["stm"], self.big1[:, :], sem="tap")
        if self.stop_after == "STM":
            self.finish(b)
            return

        fin_v = {}
        od = {}
        for T in range(NTL):
            sl = (T % 2) * 1024
            if T >= 2:
                b.w("q", "v", fin_v[T - 2])
            elif T == 0:
                b.w("q", "s", bc_s)
            xd = b.dma("q", self.xa[:, sl:sl + 1024],
                       io["x"][T * 128:(T + 1) * 128, :],
                       sem="xa" if T % 2 == 0 else "xb")
            ob = self.xf[:, 0:1024] if T % 2 == 0 else self.ob2[:, 0:1024]
            b.w("v", "s", stm_s[T])
            if T == 0:
                b.w("v", "s", silu_s)
            b.w("v", "xa" if T % 2 == 0 else "xb", xd)
            if T >= 2:
                b.w("v", "oa" if T % 2 == 0 else "ob", od[T - 2])
            b.op("v", lambda v, T=T: v.tensor_tensor(
                self.sgs[:, :], self.big1[:, T * 1024:T * 1024 + 1024],
                self.gbuf[:, T * 1024:T * 1024 + 1024], ALU.mult))
            b.op("v", lambda v, T=T, sl=sl, ob=ob: v.tensor_tensor(
                ob, self.xa[:, sl:sl + 1024], self.sgs[:, :], ALU.add), inc="v")
            fin_v[T] = b.c["v"]
            b.w("q", "v", fin_v[T])
            od[T] = b.dma("q", io["out"][T * 128:(T + 1) * 128, :], ob,
                          sem="oa" if T % 2 == 0 else "ob")
        self.finish(b)

    def finish(self, b):
        if self.stop_after:
            b.w("q", "v", b.c["v"])
            b.w("q", "s", b.c["s"])
            b.w("q", "t", b.c["t"])
            b.dma("q", self.io["out"][0:128, :], self.xa[:, 0:1024], sem="oa")
        b.w("q", "oa", b.c["oa"])
        b.w("q", "ob", b.c["ob"])
        b.w("q", "tap", b.c["tap"])
        b.w("q", "pa", b.c["pa"])
        b.w("q", "pb", b.c["pb"])
        b.w("q", "pc", b.c["pc"])
        b.w("q", "wm", b.c["wm"])
        b.w("q", "w1", b.c["w1"])
        b.w("q", "pd", b.c["pd"])
        b.w("q", "pe", b.c["pe"])

    def build(self):
        nc = self.nc
        import contextlib
        names = ["t", "v", "s", "cc",
                 "qi", "gi", "wn", "xa", "xb", "w1", "sa", "sb",
                 "cw", "cr", "oa", "ob", "tap",
                 "pa", "pb", "pc", "wm", "pd", "pe"]
        with contextlib.ExitStack() as stk:
            sems = {n: stk.enter_context(nc.semaphore("sem_" + n)) for n in names}
            block = stk.enter_context(nc.Block())

            @block.sync
            def _(e):
                self.walk(Wk("q", e, sems))

            @block.gpsimd
            def _(e):
                self.walk(Wk("g", e, sems))

            @block.tensor
            def _(e):
                self.walk(Wk("t", e, sems))

            @block.vector
            def _(e):
                self.walk(Wk("v", e, sems))

            @block.scalar
            def _(e):
                self.walk(Wk("s", e, sems))
        return nc


def build_program(taps=(), stop_after=None, nt=NT, init_only=False, a_stage=4,
                  dbg=()):
    return MK(taps=taps, stop_after=stop_after, nt=nt, init_only=init_only,
              a_stage=a_stage, dbg=dbg).build()


def make_in_maps(inputs):
    x = inputs["L0_attn_x"]
    maps = []
    for c in range(8):
        bsel = c // 4
        t0 = (c % 4) * TC
        m = {
            "x": x[bsel, t0:t0 + TC, :],
            "wq": inputs["L0_attn_Wq"], "wk1": inputs["L0_attn_Wk1"],
            "wk2": inputs["L0_attn_Wk2"], "wv1": inputs["L0_attn_Wv1"],
            "wv2": inputs["L0_attn_Wv2"], "wo": inputs["L0_attn_Wo"],
            "biases": np.stack([
                inputs["L0_attn_bq"], inputs["L0_attn_bk1"],
                inputs["L0_attn_bk2"], inputs["L0_attn_bv1"],
                inputs["L0_attn_bv2"], inputs["L0_attn_bo"]]),
            "gvec": np.stack([
                inputs["L0_attn_g1"], inputs["L0_attn_g2"],
                inputs["L0_attn_g3"], inputs["L0_attn_g4"]]),
            "lng": inputs["L0_attn_ln_g"], "lnb": inputs["L0_attn_ln_b"],
            "gw": inputs["L0_ffn_gate_w"], "gb": inputs["L0_ffn_gate_b"],
            "eb": inputs["L0_ffn_expert_bias"],
            "w1": inputs["L0_ffn_w1"], "b1": inputs["L0_ffn_b1"],
            "w2": inputs["L0_ffn_w2"], "b2": inputs["L0_ffn_b2"],
        }
        maps.append({k: np.ascontiguousarray(v, dtype=np.float32)
                     for k, v in m.items()})
    return maps


def kernel(**inputs):
    nc = build_program()
    in_maps = make_in_maps(inputs)
    res = run_bass_kernel_spmd(nc, in_maps, core_ids=list(range(8)))
    out = np.empty((2, NFULL, C), dtype=np.float32)
    for c in range(8):
        out[c // 4, (c % 4) * TC:(c % 4 + 1) * TC, :] = res.results[c]["out"]
    return out


# version 08
# speedup vs baseline: 1.0246x; 1.0156x over previous
import numpy as np
import ml_dtypes
import concourse.bass as bass
import concourse.mybir as mybir
from concourse.bass_utils import run_bass_kernel_spmd

F32 = mybir.dt.float32
BF16 = mybir.dt.bfloat16
AF = mybir.ActivationFunctionType
ALU = mybir.AluOpType
AX = mybir.AxisListType

TC = 2048
NT = 16
C = 1024
IC = 8
H = 16
E = 8
FF = 4096
FC = 32
NFULL = 8192
RG = [[0, 1, 2, 3], [4, 5, 6, 7]]
BF = ml_dtypes.bfloat16
INVN2 = 1.0 / (float(NFULL) * float(NFULL))

PS_S1 = 0
PS_S2 = 512
PS_MA = 1024
PS_MB = 1536
R0, R1, R2, R3 = 2048, 2560, 3072, 3584

ROWSA = {
    "k1a": (0, 0), "k1b": (32, 0), "k2a": (64, 0),
    "k2b": (0, 1), "v1a": (32, 1), "v1b": (64, 1),
    "v2a": (0, 2), "v2b": (32, 2), "qa": (64, 2),
    "qb": (0, 3), "oa": (32, 3), "ob": (64, 3),
}


class Wk:

    def __init__(self, cur, eng, sems):
        self.cur = cur
        self.e = eng
        self.sems = sems
        self.c = {k: 0 for k in sems}

    def dma(self, qn, out, in_, sem=None, slow=False):
        sem = sem or qn
        self.c[sem] += 16
        if self.cur == qn:
            self.e.dma_start(out=out, in_=in_,
                             allow_slow_non_contiguous=slow).then_inc(
                self.sems[sem], 16)
        return self.c[sem]

    def w(self, qn, sem, val):
        if self.cur == qn and val > 0:
            self.e.wait_ge(self.sems[sem], val)

    def op(self, qn, fn, inc=None):
        if inc:
            self.c[inc] += 1
        if self.cur == qn:
            ins = fn(self.e)
            if inc:
                ins.then_inc(self.sems[inc], 1)
        return self.c[inc] if inc else None


class MK:
    def __init__(self, taps=(), stop_after=None, nt=NT, init_only=False,
                 a_stage=4, dbg=()):
        self.taps = set(taps)
        self.stop_after = stop_after
        self.nt = nt
        self.init_only = init_only
        self.a_stage = a_stage
        self.dbg = set(dbg)
        nc = bass.Bass("TRN2", target_bir_lowering=False,
                       detect_race_conditions=False)
        self.nc = nc

        self.io = {}
        for nm, sh in (
            ("x", [TC, C]), ("wq", [C, C]), ("wk1", [C, C]), ("wk2", [C, C]),
            ("wv1", [C, C]), ("wv2", [C, C]), ("wo", [C, C]),
            ("biases", [6, C]), ("gvec", [4, H]), ("lng", [C]), ("lnb", [C]),
            ("gw", [E, C]), ("gb", [E]), ("eb", [E]),
            ("w1", [E, C, FF]), ("b1", [E, FF]), ("w2", [E, FF, C]), ("b2", [E, C]),
        ):
            self.io[nm] = nc.dram_tensor(nm, sh, F32, kind="ExternalInput").ap()
        self.io["out"] = nc.dram_tensor("out", [TC, C], F32, kind="ExternalOutput").ap()
        self.io["cc_in"] = nc.dram_tensor("cc_in", [128, 1536], F32, kind="Internal").ap()
        self.io["cc_out"] = nc.dram_tensor("cc_out", [128, 1536], F32, kind="Internal").ap()
        self.w2b = nc.dram_tensor("w2b", [E, 16, 128, 2048], BF16, kind="Internal").ap()
        self.w1b = nc.dram_tensor("w1b", [E, 128, 32768], BF16, kind="Internal").ap()

        self.tap_t = {}
        for nm, sh, dt in (
            ("xnb", [128, IC * TC], BF16), ("kv0", [128, 4096], BF16),
            ("scc", [128, 1536], F32), ("rcc", [128, 1536], F32),
            ("onehot", [128, NT * 8], F32), ("gbuf", [128, NT * C], BF16),
            ("qb", [128, 16384], BF16), ("mm", [128, 512], F32),
            ("y2r", [2, 512], F32), ("ynat", [128, IC * TC], BF16),
            ("ln", [128, IC * TC], BF16), ("sfm", [128, IC * TC], BF16),
            ("stm", [128, NT * C], BF16),
        ):
            if nm in self.taps:
                self.tap_t[nm] = nc.dram_tensor("tap_" + nm, sh, dt,
                                                kind="ExternalOutput").ap()

        def inl(name, arr):
            return nc.inline_tensor(arr, name).ap()
        et = np.zeros((128, 1), np.float32); et[:64] = 1.0
        ebv = np.zeros((128, 1), np.float32); ebv[64:] = 1.0
        st = np.zeros((8, 512), BF)
        for qq in range(512):
            st[qq // 64, qq] = 1.0
        self.consts = {
            "i128b": np.eye(128, dtype=BF), "i128f": np.eye(128, dtype=np.float32),
            "i8f": np.eye(8, dtype=np.float32),
            "ones_b": np.ones((128, 512), dtype=BF),
            "ones_f": np.ones((128, 128), np.float32),
            "onescol_b": np.ones((128, 1), dtype=BF),
            "etop": et, "ebot": ebv,
            "sel8": st,
        }
        self.cin = {k: inl(k, v) for k, v in self.consts.items()}

        A = nc.alloc_sbuf_tensor
        self.xnb = A("xnb", [128, IC * TC], BF16)
        self.gbuf = A("gbuf", [128, NT * C], BF16)
        self.big0 = A("big0", [128, 32768], BF16)
        self.big1 = A("big1", [128, 16384], BF16)
        self.wsl = A("wsl", [128, 6144], BF16)
        self.xa = A("xa", [128, 2048], F32)
        self.xf = A("xf", [128, 1024], F32)
        self.ob2 = A("ob2", [128, 1024], F32)
        self.ohm = A("ohm", [128, NT * 8], F32)
        self.sgs = A("sgs", [128, 1024], BF16)
        self.mfin = A("mfin", [128, 512], F32)
        self.mbuf = A("mbuf", [128, 512], BF16)
        self.csx = A("csx", [128, 512], F32)
        self.smal = A("smal", [128, 40], F32)
        self.rowsA = A("rowsA", [128, 2048], BF16)
        self.b1c = A("b1c", [128, 256], F32)
        self.b2n = A("b2n", [8, 1024], BF16)
        self.ge = A("ge", [8, 128], F32)
        self.geb = A("geb", [8, 128], BF16)
        self.gwT = A("gwT", [128, 64], F32)
        self.cb = {}
        for k, v in self.consts.items():
            self.cb[k] = A("c_" + k, list(v.shape),
                           BF16 if v.dtype == BF else F32)
        self.ps = nc.alloc_psum_tensor("PALL", [128, 4096], F32)

    def rrow(self, name, c0=0, n=512):
        p, g = ROWSA[name]
        return self.rowsA[p:p + 1, g * 512 + c0:g * 512 + c0 + n]

    def walk(self, b):
        io, ps = self.io, self.ps
        TAPS = self.taps

        for k in self.consts:
            b.dma("q", self.cb[k][:, :], self.cin[k], sem="qi")
        for i, pfx in enumerate(("q", "k1", "k2", "v1", "v2", "o")):
            for half, suf in ((0, "a"), (1, "b")):
                p, g = ROWSA[pfx + suf]
                b.dma("g", self.rowsA[p:p + 1, g * 512:g * 512 + 512],
                      io["biases"][i:i + 1, half * 512:half * 512 + 512], sem="gi")
        b.dma("g", self.b2n[:, :], io["b2"], sem="gi")
        b.dma("q", self.smal[:, 0:8], io["lng"].rearrange("(c p) -> p c", p=128),
              sem="qi", slow=True)
        b.dma("q", self.smal[:, 8:16], io["lnb"].rearrange("(c p) -> p c", p=128),
              sem="qi", slow=True)
        b.dma("q", self.smal[0:8, 16:17], io["gb"].rearrange("(a e) -> e a", a=1),
              sem="qi", slow=True)
        b.dma("q", self.smal[0:8, 17:18], io["eb"].rearrange("(a e) -> e a", a=1),
              sem="qi", slow=True)
        b.dma("q", self.smal[0:8, 18:26].rearrange("p (t g) -> p t g", t=2),
              io["gvec"].rearrange("g (h t) -> h t g", t=2), sem="qi", slow=True)
        b.dma("q", self.b1c[:, :], io["b1"].rearrange("e (c p) -> p (e c)", p=128),
              sem="qi", slow=True)
        qinit = b.dma("q", self.gwT[:, :],
                      io["gw"].rearrange("e (c p) -> p (e c)", p=128), sem="qi", slow=True)

        qi_all = b.c["qi"]
        gi_all = b.c["gi"]
        b.op("v", lambda v: v.memset(self.smal[:, 38:39], 1e-5), inc="v")
        eps_v = b.c["v"]

        wst = {"k": 0, "tr": {}, "ev": []}
        w_stage = (self.wsl[:, 0:4096], self.big1[:, 10240:14336])

        def wT_build(wnm, dst, cstride, first, addI=False):
            for half in range(2):
                k = wst["k"]
                wst["k"] += 1
                buf = w_stage[k % 2]
                sem = "wn" if k % 2 == 0 else "wm"
                if k >= 2:
                    b.w("g", "t", wst["tr"][k - 2])
                gv = b.dma("g", buf.rearrange("p (o i) -> p o i", o=4),
                           io[wnm][half * 512:half * 512 + 512, :].rearrange(
                               "(o p) i -> p o i", p=128), sem=sem)
                b.w("t", sem, gv)
                if first and half == 0:
                    b.w("t", "qi", qi_all)
                for ib in range(IC):
                    bk = R0 + (ib % 2) * 512
                    if len(wst["ev"]) >= 2:
                        b.w("t", "v", wst["ev"][-2])
                    mms = []
                    for ob4 in range(4):
                        diag = addI and (ob4 == ib - half * 4)
                        mms.append((ob4, False, not diag))
                        if diag:
                            mms.append((ob4, True, True))
                    for mi, (ob4, ident, stop) in enumerate(mms):
                        lastmm = mi == len(mms) - 1
                        lhs = (self.cb["i128b"][:, :] if ident else
                               buf[:, ob4 * 1024 + ib * 128:
                                   ob4 * 1024 + ib * 128 + 128])
                        b.op("t", lambda t, ob4=ob4, ident=ident, stop=stop,
                             lhs=lhs, bk=bk: t.matmul(
                                 ps[:, bk + ob4 * 128:bk + ob4 * 128 + 128],
                                 lhs, self.cb["i128b"][:, :],
                                 start=not ident, stop=stop),
                             inc="t" if lastmm else None)
                    b.w("v", "t", b.c["t"])
                    b.op("v", lambda v, ib=ib, half=half, bk=bk: v.tensor_copy(
                        self.big0[:, dst + ib * cstride + half * 512:
                                  dst + ib * cstride + half * 512 + 512],
                        ps[:, bk:bk + 512]), inc="v")
                    wst["ev"].append(b.c["v"])
                wst["tr"][k] = b.c["t"]

        pc_units = ([("w1", e0, i, hf) for e0 in range(2)
                     for i in range(IC) for hf in range(2)] +
                    [("w2", e, qd) for e in range(2) for qd in range(16)])
        pcv = {}
        PC_SEMS = ("pa", "pb", "pd", "pe")
        pc_prev = {s: None for s in PC_SEMS}

        def emit_precast_unit(kind, u_parity):
            sem = PC_SEMS[u_parity % 4]
            if pc_prev[sem] is not None:
                b.w("g", sem, pc_prev[sem])
            if kind[0] == "w1":
                _, e0, i, hf = kind
                d = b.dma("g",
                          self.w1b[e0][:, i * 4096 + hf * 2048:
                                       i * 4096 + hf * 2048 + 2048],
                          io["w1"][e0][i * 128:(i + 1) * 128,
                                       hf * 2048:(hf + 1) * 2048], sem=sem)
            else:
                _, e, qd = kind
                d = b.dma("g", self.w2b[e, qd].rearrange("p (c o) -> p c o", c=2),
                          io["w2"][e, qd * 256:(qd + 1) * 256, :].rearrange(
                              "(c p) o -> p c o", p=128), sem=sem)
            pcv[kind] = (sem, d)
            pc_prev[sem] = d

        def emit_precast(u):
            emit_precast_unit(pc_units[u], u % 4)

        for u in range(4):
            emit_precast(u)

        wT_build("wk1", 0, 2048, True, addI=True)
        wT_build("wk2", 1024, 2048, False, addI=True)
        wT_build("wv1", 16384, 2048, False, addI=True)
        wT_build("wv2", 16384 + 1024, 2048, False, addI=True)
        wkv_v = b.c["v"]
        if self.init_only:
            if "kv0" in self.taps:
                b.w("q", "v", wkv_v)
                b.dma("q", self.tap_t["kv0"], self.big0[:, 0:4096], sem="tap")
            self.finish(b)
            return
        NTL = self.nt

        xdma = {}
        ln_done = {}
        xnT_s = {}
        trv = {}
        gate_t = {}
        ge_s = {}
        maskv = {}
        zev = {}
        smm_t = {}
        xdma[0] = b.dma("q", self.xa[:, 0:1024], io["x"][0:128, :], sem="xa")
        xdma[1] = b.dma("q", self.xa[:, 1024:2048], io["x"][128:256, :], sem="xb")
        for T in range(NTL):
            sl = (T % 2) * 1024
            xt = self.xa[:, sl:sl + 1024]
            b.w("v", "xa" if T % 2 == 0 else "xb", xdma[T])
            if T == 0:
                b.w("s", "qi", qi_all)
            b.op("v", lambda v, xt=xt: v.reduce_sum(self.smal[:, 32:33], xt, AX.X),
                 inc="v")
            b.w("s", "v", b.c["v"])
            b.op("s", lambda s: s.mul(self.smal[:, 33:34], self.smal[:, 32:33],
                                      -1.0 / C), inc="s")
            b.w("v", "s", b.c["s"])
            b.op("v", lambda v, xt=xt: v.tensor_scalar(xt, xt, self.smal[:, 33:34],
                                                       None, ALU.add), inc="v")
            b.w("s", "v", b.c["v"])
            b.op("s", lambda s, xt=xt, T=T: s.activation(
                self.big1[:, 8192 + (T % 2) * 1024:8192 + (T % 2) * 1024 + 1024],
                xt, AF.Square, accum_out=self.smal[:, 33:34]))
            b.op("s", lambda s: s.activation(self.smal[:, 34:35],
                                             self.smal[:, 33:34], AF.Sqrt,
                                             bias=self.smal[:, 38:39],
                                             scale=1.0 / C), inc="s")
            b.w("v", "s", b.c["s"])
            b.op("v", lambda v: v.reciprocal(self.smal[:, 35:36], self.smal[:, 34:35]),
                 inc="v")
            b.w("s", "v", b.c["v"])
            b.op("s", lambda s, xt=xt: s.mul(xt, xt, self.smal[:, 35:36]), inc="s")
            ln_done[T] = b.c["s"]

            if self.a_stage < 1:
                xnT_s[T] = b.c["s"]
                maskv[T] = b.c["v"]
                gate_t[T] = b.c["t"]
                ge_s[T] = b.c["s"]
                zev[(T, 0, 0)] = zev[(T, 0, 1)] = b.c["s"]
                zev[(T, 1, 0)] = zev[(T, 1, 1)] = b.c["s"]
                smm_t[T] = b.c["t"]
                continue
            b.w("t", "s", ln_done[T])
            if T > 0:
                b.w("t", "s", zev[(T - 1, 1, 0)])
            for j in range(IC):
                b.op("t", lambda t, j=j, xt=xt: t.matmul(
                    ps[:, R0 + j * 128:R0 + j * 128 + 128],
                    xt[:, j * 128:j * 128 + 128], self.cb["i128f"][:, :],
                    start=True, stop=True), inc="t" if j == IC - 1 else None)
            trT = b.c["t"]
            trv[T] = trT
            b.w("v", "t", trT)
            if "noxf" in self.dbg:
                b.op("v", lambda v: v.engine_nop(), inc="v")
            else:
                b.op("v", lambda v: v.tensor_copy(self.xf[:, :], ps[:, R0:R0 + 1024]),
                     inc="v")
            xf_v = b.c["v"]
            b.w("s", "t", trT)
            b.w("s", "v", xf_v)
            if "noxnb" in self.dbg:
                b.op("s", lambda s: s.copy(self.xnb[:, 0:1024],
                                           ps[:, R0:R0 + 1024]), inc="s")
            else:
                b.op("s", lambda s, T=T: s.copy(
                    self.xnb[:, :].rearrange("p (c f) -> p c f", c=IC)[
                        :, :, T * 128:T * 128 + 128],
                    ps[:, R0:R0 + 1024].rearrange("p (c f) -> p c f", f=128)),
                    inc="s")
            xnT_s[T] = b.c["s"]

            if self.a_stage < 2:
                maskv[T] = b.c["v"]
                gate_t[T] = b.c["t"]
                ge_s[T] = b.c["s"]
                zev[(T, 0, 0)] = zev[(T, 0, 1)] = b.c["s"]
                zev[(T, 1, 0)] = zev[(T, 1, 1)] = b.c["s"]
                smm_t[T] = b.c["t"]
                continue
            b.w("t", "v", xf_v)
            if T > 0:
                b.w("t", "s", ge_s[T - 1])
                b.w("t", "s", zev[(T - 1, 1, 1)])
            for i in range(IC):
                b.op("t", lambda t, i=i: t.matmul(
                    ps[0:8, R2:R2 + 128],
                    self.gwT[:, :].rearrange("p (e c) -> p c e", c=IC)[:, i, :],
                    self.xf[:, i * 128:i * 128 + 128],
                    start=(i == 0), stop=(i == IC - 1)),
                    inc="t" if i == IC - 1 else None)
            gmm = b.c["t"]
            b.w("s", "t", gmm)
            b.op("s", lambda s: s.activation(self.ge[:, :], ps[0:8, R2:R2 + 128],
                                             AF.Identity, bias=self.smal[0:8, 16:17]))
            b.op("s", lambda s: s.add(self.ge[:, :], self.ge[:, :],
                                      self.smal[0:8, 17:18]), inc="s")
            ge_s[T] = b.c["s"]
            b.w("t", "s", ge_s[T])
            b.op("t", lambda t: t.matmul(ps[:, R2 + 128:R2 + 136], self.ge[:, :],
                                         self.cb["i8f"][:, :], start=True, stop=True),
                 inc="t")
            gate_t[T] = b.c["t"]
            b.w("v", "t", gate_t[T])
            b.op("v", lambda v: v.reduce_max(self.smal[:, 36:37],
                                             ps[:, R2 + 128:R2 + 136], AX.X),
                 inc="v")
            b.w("s", "v", b.c["v"])
            b.op("s", lambda s: s.copy(self.smal[:, 37:38], self.smal[:, 36:37]),
                 inc="s")
            b.w("v", "s", b.c["s"])
            b.op("v", lambda v, T=T: v.tensor_scalar(
                self.ohm[:, T * 8:T * 8 + 8], ps[:, R2 + 128:R2 + 136],
                self.smal[:, 37:38], None, ALU.is_ge), inc="v")
            maskv[T] = b.c["v"]

            if self.a_stage < 3:
                zev[(T, 0, 0)] = zev[(T, 0, 1)] = b.c["s"]
                zev[(T, 1, 0)] = zev[(T, 1, 1)] = b.c["s"]
                smm_t[T] = b.c["t"]
                continue
            kvs = (T % 2) * 4096
            for rnd in range(2):
                wbase = rnd * 16384
                rks = (("k1a", "k1b", "k2a", "k2b") if rnd == 0
                       else ("v1a", "v1b", "v2a", "v2b"))
                b.w("t", "s", xnT_s[T])
                b.w("t", "v", maskv[T])
                if T == 0 and rnd == 0:
                    b.w("t", "v", wkv_v)
                    b.w("t", "gi", gi_all)
                if rnd == 0 and T > 0:
                    b.w("t", "s", zev[(T - 1, 1, 1)])
                zt_g = {}
                for g4 in range(4):
                    rot = R0 + g4 * 512
                    if rnd == 1 and g4 % 2 == 0:
                        b.w("t", "s", zev[(T, 0, g4 // 2)])
                    for i in range(IC):
                        b.op("t", lambda t, i=i, g4=g4, rot=rot, wbase=wbase, T=T:
                             t.matmul(
                                 ps[:, rot:rot + 512],
                                 self.xnb[:, i * 2048 + T * 128:
                                          i * 2048 + T * 128 + 128],
                                 self.big0[:, wbase + i * 2048 + g4 * 512:
                                           wbase + i * 2048 + g4 * 512 + 512],
                                 start=(i == 0), stop=False))
                    rp = ROWSA[rks[g4]][0]
                    b.op("t", lambda t, rot=rot, rr=rks[g4], rp=rp: t.matmul(
                        ps[:, rot:rot + 512], self.cb["ones_b"][rp:rp + 1, 0:128],
                        self.rrow(rr), start=False, stop=True), inc="t")
                    zt_g[g4] = b.c["t"]
                for g2 in range(2):
                    rot = R0 + g2 * 1024
                    b.w("s", "t", zt_g[2 * g2 + 1])
                    b.op("s", lambda s, g2=g2, rot=rot, rnd=rnd, kvs=kvs: s.copy(
                        self.big1[:, kvs + rnd * 2048 + g2 * 1024:
                                  kvs + rnd * 2048 + g2 * 1024 + 1024],
                        ps[:, rot:rot + 1024]), inc="s")
                    zev[(T, rnd, g2)] = b.c["s"]

            if self.a_stage < 4:
                smm_t[T] = b.c["t"]
                if T + 2 < NTL:
                    b.w("q", "s", zev[(T, 1, 1)])
                    xdma[T + 2] = b.dma("q", self.xa[:, sl:sl + 1024],
                                        io["x"][(T + 2) * 128:(T + 3) * 128, :],
                                        sem="xa" if T % 2 == 0 else "xb")
                continue
            b.w("t", "s", zev[(T, 1, 1)])
            for sidx in range(2):
                for h in range(H):
                    r0 = (h % 2) * 64
                    koff = kvs + sidx * 1024 + h * 64
                    voff = kvs + 2048 + sidx * 1024 + h * 64
                    cc0 = sidx * 512 + (h // 2) * 64
                    b.op("t", lambda t, koff=koff, voff=voff, r0=r0, cc0=cc0, T=T, h=h:
                         t.matmul(ps[r0:r0 + 64, cc0:cc0 + 64],
                                  self.big1[:, koff:koff + 64],
                                  self.big1[:, voff:voff + 64],
                                  start=(T == 0 and h < 2),
                                  stop=(T == NTL - 1 and h >= H - 2)))
            for mi in range(4):
                vb = kvs + 2048 + (mi // 2) * 1024 + (mi % 2) * 512
                mp = (mi % 3) * 32 if mi < 3 else 0
                mcol = PS_MA if mi < 3 else PS_MB
                b.op("t", lambda t, vb=vb, mp=mp, mcol=mcol, T=T, mi=mi: t.matmul(
                    ps[mp:mp + 1, mcol:mcol + 512], self.cb["onescol_b"][:, :],
                    self.big1[:, vb:vb + 512],
                    start=(T == 0), stop=(T == NTL - 1)),
                    inc="t" if mi == 3 else None)
            smm_t[T] = b.c["t"]
            if T + 2 < NTL:
                b.w("q", "t", trv[T])
                xdma[T + 2] = b.dma("q", self.xa[:, sl:sl + 1024],
                                    io["x"][(T + 2) * 128:(T + 3) * 128, :],
                                    sem="xa" if T % 2 == 0 else "xb")
            for u in range(max(4, T * 4), min((T + 1) * 4, len(pc_units))):
                emit_precast(u)
            if T == 10:
                for k in range(4):
                    for hf in range(2):
                        ps_, pv_ = pcv[("w1", 0, k, hf)]
                        b.w("g", ps_, pv_)
                for k in range(4):
                    gbuf_w1 = b.dma("g", self.gbuf[:, k * 4096:(k + 1) * 4096],
                                    self.w1b[0][:, k * 4096:(k + 1) * 4096],
                                    sem="w1")
        for u in range(max(4, NTL * 4), len(pc_units)):
            emit_precast(u)
        s_done_t = smm_t[NTL - 1]

        if self.a_stage < 4:
            if "xnb" in TAPS and self.a_stage >= 1:
                b.w("q", "s", xnT_s[NTL - 1])
                b.dma("q", self.tap_t["xnb"], self.xnb[:, :], sem="tap")
            if self.a_stage < 1:
                b.w("q", "s", ln_done[NTL - 1])
            if "onehot" in TAPS and self.a_stage >= 2:
                b.w("q", "v", maskv[NTL - 1])
                b.dma("q", self.tap_t["onehot"], self.ohm[:, :], sem="tap")
            if "kv0" in TAPS and self.a_stage >= 3:
                b.w("q", "s", zev[(max(0, NTL - 2), 1, 1)])
                b.dma("q", self.tap_t["kv0"], self.big1[:, 0:4096], sem="tap")
            self.finish(b)
            return
        b.w("v", "t", s_done_t)
        b.op("v", lambda v: v.tensor_copy(self.xa[:, 0:1024], ps[:, 0:1024]))
        b.op("v", lambda v: v.memset(self.xa[:, 1024:1536], 0.0))
        b.op("v", lambda v: v.tensor_copy(self.xa[0:1, 1024:1536], ps[0:1, PS_MA:PS_MA + 512]))
        b.op("v", lambda v: v.tensor_copy(self.xa[32:33, 1024:1536], ps[32:33, PS_MA:PS_MA + 512]))
        b.op("v", lambda v: v.tensor_copy(self.xa[64:65, 1024:1536], ps[64:65, PS_MA:PS_MA + 512]))
        b.op("v", lambda v: v.tensor_copy(self.xa[96:97, 1024:1536], ps[0:1, PS_MB:PS_MB + 512]),
             inc="v")
        ccv = b.c["v"]
        if "scc" in TAPS:
            b.w("q", "v", ccv)
            b.dma("q", self.tap_t["scc"], self.xa[:, 0:1536], sem="tap")
        b.w("g", "v", ccv)
        ccw = b.dma("g", io["cc_in"], self.xa[:, 0:1536], sem="cw")
        if self.stop_after != "A":
            b.w("g", "cw", ccw)
            b.op("g", lambda g: g.collective_compute(
                "AllReduce", ALU.add, replica_groups=RG,
                ins=[io["cc_in"]], outs=[io["cc_out"]]), inc="cc")

        if "xnb" in TAPS:
            b.w("q", "s", xnT_s[NTL - 1])
            b.dma("q", self.tap_t["xnb"], self.xnb[:, :], sem="tap")
        if "kv0" in TAPS:
            b.w("q", "s", zev[(max(0, NTL - 2), 1, 1)])
            b.dma("q", self.tap_t["kv0"], self.big1[:, 0:4096], sem="tap")
        if "onehot" in TAPS:
            b.w("q", "v", maskv[NTL - 1])
            b.dma("q", self.tap_t["onehot"], self.ohm[:, :], sem="tap")

        if self.stop_after == "A":
            self.finish(b)
            return

        b.w("g", "t", s_done_t)
        for i in range(4, IC):
            for hf in range(2):
                ps_, pv_ = pcv[("w1", 0, i, hf)]
                b.w("g", ps_, pv_)
        w1v = [None] * E
        for i in range(4, IC):
            w1hi = b.dma("g", self.big0[:, i * 4096:(i + 1) * 4096],
                         self.w1b[0][:, i * 4096:(i + 1) * 4096], sem="w1")
        b.w("g", "w1", w1hi)
        for i in range(4):
            w1v[0] = b.dma("g", self.big0[:, i * 4096:(i + 1) * 4096],
                           self.w1b[0][:, i * 4096:(i + 1) * 4096], sem="w1")
        slabv = {}
        quad_t = {}
        slab_n = [0]
        rb_done = [0]
        SLAB_SEMS = ("sa", "sb", "pc")

        def emit_slab(e, ts, qd, wait=None):
            gidx = slab_n[0]
            slab_n[0] += 1
            sidx = gidx % 3
            if wait is not None:
                b.w("q", "t", wait)
            psem, pval = pcv[("w2", e, qd)]
            b.w("q", psem, pval)
            slabv[(e, ts, qd)] = (sidx, b.dma(
                "q", self.wsl[:, sidx * 2048:sidx * 2048 + 2048],
                self.w2b[e, qd], sem=SLAB_SEMS[sidx]))

        h_ev = {}
        h_last_t = {}
        yev = {}
        yev_g0 = {}
        yev_h0 = {}

        for e in range(E):
            prev = (e - 1, 3) if e > 0 else None
            for ts in range(4):
                for qd0 in range(3):
                    wait = None
                    if (e, ts) != (0, 0):
                        pe, pts = (e, ts - 1) if ts > 0 else (e - 1, 3)
                        wait = quad_t[(pe, pts, 13 + qd0)]
                    emit_slab(e, ts, qd0, wait)
                if e + 2 < E:
                    for uu in range(4):
                        idx = ts * 4 + uu
                        emit_precast_unit(("w1", e + 2, idx // 2, idx % 2), uu)
                        emit_precast_unit(("w2", e + 2, idx), uu + 1)
                if e == 0 and ts == 0:
                    b.w("g", "cc", 1)
                    rb_done[0] = b.dma("g", self.xa[:, 0:1536], io["cc_out"],
                                       sem="cr")
                    if "rcc" in TAPS:
                        b.w("q", "cr", rb_done[0])
                        b.dma("q", self.tap_t["rcc"], self.xa[:, 0:1536], sem="tap")
                t0 = ts * 512
                prev2 = (e, ts - 1) if ts > 0 else prev
                for fc in range(FC):
                    hb = (fc % 2) * 512
                    if fc == 0 and ts == 0:
                        b.w("t", "w1", w1hi if e == 0 else w1v[e])
                    elif e == 0 and ts == 1 and fc == 0:
                        b.w("t", "w1", w1v[0])
                    if fc >= 2:
                        b.w("t", "s", h_ev[(e, ts, fc - 2)])
                    elif prev2 is not None:
                        b.w("t", "v", (yev_h0 if fc == 0 else yev_g0)[prev2])
                    elif fc == 0:
                        b.w("t", "v", ccv)
                    w1lo = self.gbuf if (e == 0 and ts == 0) else self.big0
                    for i in range(IC):
                        src = w1lo if i < 4 else self.big0
                        b.op("t", lambda t, i=i, fc=fc, hb=hb, t0=t0, src=src:
                             t.matmul(
                                 ps[:, hb:hb + 512],
                                 src[:, i * 4096 + fc * 128:
                                     i * 4096 + fc * 128 + 128],
                                 self.xnb[:, i * 2048 + t0:i * 2048 + t0 + 512],
                                 start=(i == 0), stop=(i == IC - 1)),
                             inc="t" if i == IC - 1 else None)
                    b.w("s", "t", b.c["t"])
                    b.op("s", lambda s, fc=fc, hb=hb, e=e: s.activation(
                        self.big1[:, fc * 512:fc * 512 + 512], ps[:, hb:hb + 512],
                        AF.Relu, bias=self.b1c[:, e * 32 + fc:e * 32 + fc + 1]),
                        inc="s")
                    h_ev[(e, ts, fc)] = b.c["s"]
                h_last_t[(e, ts)] = b.c["t"]

                if prev2 is not None:
                    b.w("t", "v", yev[prev2])
                else:
                    b.w("t", "v", ccv)
                for fc in range(FC):
                    if fc % 2 == 0:
                        sidx0, sv0 = slabv[(e, ts, fc // 2)]
                        b.w("t", SLAB_SEMS[sidx0], sv0)
                    sidx = slabv[(e, ts, fc // 2)][0]
                    gseq = (2, 3, 1, 0) if fc == 0 else (0, 1, 2, 3)
                    for g in gseq:
                        if fc == 0 and g == 0:
                            b.w("t", "s", h_ev[(e, ts, FC - 1)])
                        for og in range(2):
                            b.op("t", lambda t, fc=fc, g=g, og=og, sidx=sidx: t.matmul(
                                ps[:, g * 1024 + og * 512:g * 1024 + og * 512 + 512],
                                self.big1[:, fc * 512 + g * 128:fc * 512 + g * 128 + 128],
                                self.wsl[:, sidx * 2048 + (fc % 2) * 1024 + og * 512:
                                         sidx * 2048 + (fc % 2) * 1024 + og * 512 + 512],
                                start=(fc == 0), stop=(fc == FC - 1)),
                                inc="t" if (fc % 2 == 1 and g == 3 and og == 1)
                                else None)
                    if fc % 2 == 1:
                        qd = fc // 2
                        quad_t[(e, ts, qd)] = b.c["t"]
                        if qd + 3 < 16:
                            emit_slab(e, ts, qd + 3, quad_t[(e, ts, qd)])
                        if ts == 3 and e + 1 < E and qd < 8:
                            if qd == 0:
                                b.w("g", "t", h_last_t[(e, 3)])
                            for hf in range(2):
                                ps_, pv_ = pcv[("w1", e + 1, qd, hf)]
                                b.w("g", ps_, pv_)
                            w1v[e + 1] = b.dma(
                                "g",
                                self.big0[:, qd * 4096:qd * 4096 + 4096],
                                self.w1b[e + 1][:, qd * 4096:qd * 4096 + 4096],
                                sem="w1")
                ymm = b.c["t"]
                b.w("v", "t", ymm)
                for g in range(4):
                    tile = ts * 4 + g
                    for hh in range(2 if g == 0 else 1):
                        wid = 512 if g == 0 else 1024
                        c0 = tile * 1024 + hh * 512
                        p0 = g * 1024 + hh * 512
                        if e == 0:
                            b.op("v", lambda v, c0=c0, p0=p0, wid=wid, tile=tile,
                                 e=e: v.tensor_scalar(
                                     self.gbuf[:, c0:c0 + wid],
                                     ps[:, p0:p0 + wid],
                                     self.ohm[:, tile * 8 + e:tile * 8 + e + 1],
                                     None, ALU.mult), inc="v")
                        else:
                            b.op("v", lambda v, c0=c0, p0=p0, wid=wid, tile=tile,
                                 e=e: v.scalar_tensor_tensor(
                                     self.gbuf[:, c0:c0 + wid],
                                     ps[:, p0:p0 + wid],
                                     self.ohm[:, tile * 8 + e:tile * 8 + e + 1],
                                     self.gbuf[:, c0:c0 + wid],
                                     ALU.mult, ALU.add), inc="v")
                        if g == 0 and hh == 0:
                            yev_h0[(e, ts)] = b.c["v"]
                    if g == 0:
                        yev_g0[(e, ts)] = b.c["v"]
                yev[(e, ts)] = b.c["v"]

        bm_s = {}
        bm_v = {}

        def emit_b2mix(tile):
            pslot = (tile % 2) * 1024
            if tile == 0:
                b.w("t", "v", yev[(E - 1, 3)])
            if tile >= 2:
                b.w("t", "v", bm_v[tile - 2])
            b.op("t", lambda t, tile=tile, pslot=pslot: t.matmul(
                ps[0:8, pslot + 512:pslot + 640],
                self.ohm[:, tile * 8:tile * 8 + 8], self.cb["i128f"][:, :],
                start=True, stop=True), inc="t")
            b.w("s", "t", b.c["t"])
            if tile >= 2:
                b.w("s", "s", bm_s[tile - 2])
            b.op("s", lambda s, tile=tile, pslot=pslot: s.copy(
                self.geb[:, :], ps[0:8, pslot + 512:pslot + 640]), inc="s")
            bm_s[tile] = b.c["s"]
            b.w("t", "s", bm_s[tile])
            for og in range(2):
                b.op("t", lambda t, og=og, pslot=pslot: t.matmul(
                    ps[:, pslot + og * 512:pslot + og * 512 + 512],
                    self.geb[:, :], self.b2n[:, og * 512:og * 512 + 512],
                    start=True, stop=True), inc="t" if og == 1 else None)
            b.w("v", "t", b.c["t"])
            b.op("v", lambda v, tile=tile, pslot=pslot: v.tensor_tensor(
                self.gbuf[:, tile * 1024:tile * 1024 + 1024],
                self.gbuf[:, tile * 1024:tile * 1024 + 1024],
                ps[:, pslot:pslot + 1024], ALU.add), inc="v")
            bm_v[tile] = b.c["v"]
            b.w("s", "v", bm_v[tile])
            b.op("s", lambda s, tile=tile: s.activation(
                self.gbuf[:, tile * 1024:tile * 1024 + 1024],
                self.gbuf[:, tile * 1024:tile * 1024 + 1024], AF.Silu), inc="s")

        if self.stop_after == "FFN":
            for tile in range(NT):
                emit_b2mix(tile)
            silu_s = b.c["s"]
            if "gbuf" in TAPS:
                b.w("q", "s", silu_s)
                b.dma("q", self.tap_t["gbuf"], self.gbuf[:, :], sem="tap")
            self.finish(b)
            return

        b.w("g", "t", quad_t[(E - 1, 3, 15)])
        tst = {"k": 0, "tr": {}, "ev": []}
        for wnm, base in (("wq", 0), ("wo", 8192)):
            addI = wnm == "wq"
            for half in range(2):
                k = tst["k"]
                tst["k"] += 1
                buf = w_stage[k % 2]
                sem = "wn" if k % 2 == 0 else "wm"
                if k >= 2:
                    b.w("g", "t", tst["tr"][k - 2])
                gv = b.dma("g", buf.rearrange("p (o i) -> p o i", o=4),
                           io[wnm][half * 512:half * 512 + 512, :].rearrange(
                               "(o p) i -> p o i", p=128), sem=sem)
                for tile in range(k * 4, k * 4 + 4):
                    emit_b2mix(tile)
                b.w("t", sem, gv)
                for ib in range(IC):
                    bk = R0 + (ib % 2) * 512
                    if len(tst["ev"]) >= 2:
                        b.w("t", "v", tst["ev"][-2])
                    mms = []
                    for ob4 in range(4):
                        diag = addI and (ob4 == ib - half * 4)
                        mms.append((ob4, False, not diag))
                        if diag:
                            mms.append((ob4, True, True))
                    for mi, (ob4, ident, stop) in enumerate(mms):
                        lastmm = mi == len(mms) - 1
                        lhs = (self.cb["i128b"][:, :] if ident else
                               buf[:, ob4 * 1024 + ib * 128:
                                   ob4 * 1024 + ib * 128 + 128])
                        b.op("t", lambda t, lhs=lhs, ob4=ob4, ident=ident,
                             stop=stop, bk=bk: t.matmul(
                                 ps[:, bk + ob4 * 128:bk + ob4 * 128 + 128],
                                 lhs, self.cb["i128b"][:, :],
                                 start=not ident, stop=stop),
                             inc="t" if lastmm else None)
                    b.w("v", "t", b.c["t"])
                    b.op("v", lambda v, ib=ib, base=base, half=half, bk=bk:
                         v.tensor_copy(
                             self.big0[:, base + ib * 1024 + half * 512:
                                       base + ib * 1024 + half * 512 + 512],
                             ps[:, bk:bk + 512]), inc="v")
                    tst["ev"].append(b.c["v"])
                tst["tr"][k] = b.c["t"]
        wT_v = b.c["v"]
        silu_s = b.c["s"]
        if "gbuf" in TAPS:
            b.w("q", "s", silu_s)
            b.dma("q", self.tap_t["gbuf"], self.gbuf[:, :], sem="tap")

        qv = {}
        for cc in range(IC):
            pbase = (cc % 2) * 2048
            if cc >= 2:
                b.w("t", "s", qv[cc - 2])
            if cc == 0:
                b.w("t", "v", wT_v)
                b.w("t", "v", bm_v[NT - 1])
            for i in range(IC):
                for tg in range(4):
                    b.op("t", lambda t, cc=cc, i=i, tg=tg, pbase=pbase: t.matmul(
                        ps[:, pbase + tg * 512:pbase + tg * 512 + 512],
                        self.big0[:, i * 1024 + cc * 128:i * 1024 + cc * 128 + 128],
                        self.xnb[:, i * 2048 + tg * 512:i * 2048 + tg * 512 + 512],
                        start=(i == 0), stop=False))
            for tg in range(4):
                rr = "qa" if cc < 4 else "qb"
                c0 = (cc % 4) * 128
                rp = ROWSA[rr][0]
                b.op("t", lambda t, tg=tg, rr=rr, c0=c0, rp=rp, pbase=pbase: t.matmul(
                    ps[:, pbase + tg * 512:pbase + tg * 512 + 512],
                    self.rrow(rr, c0, 128), self.cb["ones_b"][rp:rp + 1, :],
                    start=False, stop=True), inc="t" if tg == 3 else None)
            b.w("s", "t", b.c["t"])
            b.op("s", lambda s, cc=cc, pbase=pbase: s.copy(
                self.big0[:, 16384 + cc * 2048:16384 + cc * 2048 + 2048],
                ps[:, pbase:pbase + 2048]), inc="s")
            qv[cc] = b.c["s"]
        q_done = b.c["s"]
        if "qb" in TAPS:
            b.w("q", "s", q_done)
            b.dma("q", self.tap_t["qb"], self.big0[:, 16384:32768], sem="tap")
        if self.stop_after == "Q":
            self.finish(b)
            return

        b.w("t", "s", q_done)
        b.w("t", "cr", rb_done[0])
        S1 = self.xa[:, 0:512]
        S2 = self.xa[:, 512:1024]

        def eslot(par, i):
            p = par * 32
            return [self.xf[p:p + 1, 0:512], self.xf[p:p + 1, 512:1024],
                    self.ob2[p:p + 1, 0:512], self.ob2[p:p + 1, 512:1024],
                    self.xa[p:p + 1, 1536:2048], self.mfin[p:p + 1, 0:512]][i]
        gr_t = {}
        gr_mm = {}
        for r8 in range(8):
            gi, par = r8 // 2, r8 % 2
            gcol = (18 + gi) if par == 0 else (22 + gi)
            pslot = R0 + (r8 % 2) * 512
            if r8 == 0:
                b.w("s", "v", bm_v[NT - 1])
            else:
                b.w("s", "t", gr_mm[r8 - 1])
            b.op("s", lambda s, gcol=gcol: s.copy(
                self.geb[0:8, 0:1], self.smal[0:8, gcol:gcol + 1]), inc="s")
            b.w("t", "s", b.c["s"])
            if r8 >= 2:
                b.w("t", "v", gr_t[r8 - 2])
            b.op("t", lambda t, pslot=pslot: t.matmul(
                ps[0:1, pslot:pslot + 512],
                self.geb[0:8, 0:1], self.cb["sel8"][:, :],
                start=True, stop=True), inc="t")
            gr_mm[r8] = b.c["t"]
            b.w("v", "t", b.c["t"])
            if r8 == 0:
                b.w("v", "cr", rb_done[0])
            b.op("v", lambda v, par=par, gi=gi, pslot=pslot: v.tensor_copy(
                eslot(par, gi), ps[0:1, pslot:pslot + 512]), inc="v")
            gr_t[r8] = b.c["v"]
        if self.stop_after == "M1":
            self.finish(b)
            return
        for vi in range(2):
            for par in range(2):
                dst = eslot(par, 4 + vi)
                for hlf in range(2):
                    srow = (vi * 2 + hlf) * 32
                    b.op("v", lambda v, dst=dst, srow=srow, par=par, hlf=hlf:
                         v.tensor_copy(
                             dst[:, hlf * 256:hlf * 256 + 256]
                             .rearrange("p (a b) -> p a b", b=64),
                             self.xa[srow:srow + 1, 1024:1536]
                             .rearrange("p (a t b) -> p a t b", t=2, b=64)
                             [:, :, par, :]),
                         inc="v" if (vi, par, hlf) == (1, 1, 1) else None)
        vpar_v = b.c["v"]
        if self.stop_after == "M2":
            self.finish(b)
            return
        b.op("v", lambda v: v.tensor_tensor(self.csx[:, :], S1, S2, ALU.mult),
             inc="v")
        s12_v = b.c["v"]
        b.w("t", "v", s12_v)
        for par in range(2):
            ev = self.cb["etop"] if par == 0 else self.cb["ebot"]
            b.op("t", lambda t, par=par, ev=ev: t.matmul(
                ps[0:1, R0 + par * 512:R0 + par * 512 + 512], ev[:, :],
                self.csx[:, :], start=True, stop=True),
                inc="t" if par == 1 else None)
        y2s_t = b.c["t"]
        b.w("v", "t", y2s_t)
        for par in range(2):
            b.op("v", lambda v, par=par: v.tensor_tensor(
                ps[0:1, R0 + par * 512:R0 + par * 512 + 512],
                ps[0:1, R0 + par * 512:R0 + par * 512 + 512],
                eslot(par, 1), ALU.mult))
        b.op("v", lambda v: v.tensor_scalar(ps[0:1, R0:R0 + 1024],
                                            ps[0:1, R0:R0 + 1024],
                                            INVN2, None, ALU.mult), inc="v")
        y2_v = b.c["v"]
        b.w("s", "v", y2_v)
        b.op("s", lambda s: s.copy(self.sgs[0:1, 0:512], ps[0:1, R0:R0 + 512]))
        b.op("s", lambda s: s.copy(self.sgs[64:65, 0:512], ps[0:1, R0 + 512:R0 + 1024]),
             inc="s")
        y2_s = b.c["s"]
        if self.stop_after == "M3":
            self.finish(b)
            return
        if "y2r" in TAPS:
            b.w("q", "v", y2_v)
            b.dma("q", self.tap_t["y2r"], ps[0:1, R0:R0 + 1024], sem="tap")
        for par in range(2):
            b.op("v", lambda v, par=par: v.tensor_tensor(
                eslot(par, 0), eslot(par, 0), eslot(par, 5), ALU.mult))
            b.op("v", lambda v, par=par: v.tensor_tensor(
                eslot(par, 2), eslot(par, 2), eslot(par, 4), ALU.mult))
            for idx in (0, 2, 3):
                b.op("v", lambda v, par=par, idx=idx: v.tensor_scalar(
                    eslot(par, idx), eslot(par, idx), INVN2, None, ALU.mult),
                    inc="v" if (par == 1 and idx == 3) else None)
        csr_v = b.c["v"]
        if self.stop_after == "M4":
            self.finish(b)
            return
        b.w("t", "v", csr_v)
        b.w("t", "s", y2_s)
        for term, sidx2 in ((0, 0), (1, 2), (2, 3)):
            for par in range(2):
                p0 = par * 32
                b.op("t", lambda t, term=term, sidx2=sidx2, par=par, p0=p0: t.matmul(
                    ps[par * 64:par * 64 + 64,
                       R1 + term * 512:R1 + term * 512 + 512],
                    self.cb["ones_f"][p0:p0 + 1, 0:64],
                    eslot(par, sidx2), start=True, stop=True),
                    inc="t" if par == 1 else None)
        cs_t = b.c["t"]
        if self.stop_after == "M4a":
            self.finish(b)
            return
        b.w("v", "t", cs_t)
        b.op("v", lambda v: v.tensor_tensor(self.mfin[:, :], S1,
                                            ps[:, R1:R1 + 512], ALU.mult))
        b.op("v", lambda v: v.tensor_tensor(self.csx[:, :], S2,
                                            ps[:, R1 + 512:R1 + 1024], ALU.mult))
        b.op("v", lambda v: v.tensor_tensor(self.mfin[:, :], self.mfin[:, :],
                                            self.csx[:, :], ALU.add))
        b.op("v", lambda v: v.tensor_tensor(self.csx[:, :], S1, S2, ALU.mult))
        b.op("v", lambda v: v.tensor_tensor(self.csx[:, :], self.csx[:, :],
                                            ps[:, R1 + 1024:R1 + 1536], ALU.mult))
        b.op("v", lambda v: v.tensor_tensor(self.mfin[:, :], self.mfin[:, :],
                                            self.csx[:, :], ALU.add), inc="v")
        mf_v = b.c["v"]
        if self.stop_after == "M4b":
            self.finish(b)
            return
        b.w("s", "v", mf_v)
        b.op("s", lambda s: s.copy(self.mbuf[:, :], self.mfin[:, :]), inc="s")
        m_s = b.c["s"]
        if "mm" in TAPS:
            b.w("q", "v", mf_v)
            b.dma("q", self.tap_t["mm"], self.mfin[:, :], sem="tap")
        if self.stop_after == "M":
            self.finish(b)
            return

        ynat_s = {}
        for j in range(IC):
            pbase = (j % 2) * 2048
            if j >= 2:
                b.w("t", "s", ynat_s[j - 2])
            if j == 0:
                b.w("t", "s", m_s)
                b.w("t", "s", y2_s)
            for hh in range(2):
                r0 = hh * 64
                for tg in range(4):
                    b.op("t", lambda t, j=j, r0=r0, tg=tg, pbase=pbase: t.matmul(
                        ps[r0:r0 + 64, pbase + tg * 512:pbase + tg * 512 + 512],
                        self.mbuf[r0:r0 + 64, j * 64:j * 64 + 64],
                        self.big0[r0:r0 + 64,
                                  16384 + j * 2048 + tg * 512:
                                  16384 + j * 2048 + tg * 512 + 512],
                        start=True, stop=False))
                for tg in range(4):
                    b.op("t", lambda t, j=j, r0=r0, tg=tg, pbase=pbase: t.matmul(
                        ps[r0:r0 + 64, pbase + tg * 512:pbase + tg * 512 + 512],
                        self.sgs[r0:r0 + 1, j * 64:j * 64 + 64],
                        self.cb["ones_b"][r0:r0 + 1, :], start=False, stop=True),
                        inc="t" if tg == 3 else None)
            b.w("s", "t", b.c["t"])
            b.op("s", lambda s, j=j, pbase=pbase: s.copy(
                self.big1[:, j * 2048:j * 2048 + 2048],
                ps[:, pbase:pbase + 2048]), inc="s")
            ynat_s[j] = b.c["s"]
        ynat_done = b.c["s"]
        if "ynat" in TAPS:
            b.w("q", "s", ynat_done)
            b.dma("q", self.tap_t["ynat"], self.big1[:, :], sem="tap")
        if self.stop_after == "Y":
            self.finish(b)
            return

        def sqsl(cc):
            return self.wsl[:, (cc % 3) * 2048:(cc % 3) * 2048 + 2048]

        def stat_sl(k):
            return ps[(k % 3) * 32:(k % 3) * 32 + 1,
                      R0 + (k // 3) * 512:R0 + (k // 3) * 512 + 512]
        sq_v = {}
        sq_eng = {}
        ln_cc_t = {}
        for cc in range(IC):
            eng = "v" if cc % 2 == 0 else "s"
            if eng == "v":
                b.w("v", "s", ynat_s[cc])
            if cc >= 3:
                b.w(eng, "t", ln_cc_t[cc - 3])
            if eng == "v":
                b.op("v", lambda v, cc=cc: v.tensor_tensor(
                    sqsl(cc), self.big1[:, cc * 2048:cc * 2048 + 2048],
                    self.big1[:, cc * 2048:cc * 2048 + 2048], ALU.mult),
                    inc="v")
                sq_v[cc] = b.c["v"]
            else:
                b.op("s", lambda s, cc=cc: s.activation(
                    sqsl(cc), self.big1[:, cc * 2048:cc * 2048 + 2048],
                    AF.Square), inc="s")
                sq_v[cc] = b.c["s"]
            sq_eng[cc] = eng
            if cc == 0:
                b.w("t", "s", ynat_done)
            b.w("t", eng, sq_v[cc])
            for gg in range(4):
                for which in range(2):
                    k = gg + 4 * which
                    src2 = (self.big1[:, cc * 2048 + gg * 512:cc * 2048 + gg * 512 + 512]
                            if which == 0 else
                            sqsl(cc)[:, gg * 512:gg * 512 + 512])
                    b.op("t", lambda t, k=k, src2=src2, cc=cc: t.matmul(
                        stat_sl(k), self.cb["onescol_b"][:, :], src2,
                        start=(cc == 0), stop=(cc == IC - 1)),
                        inc="t" if (gg == 3 and which == 1) else None)
            ln_cc_t[cc] = b.c["t"]
        lnsum_t = b.c["t"]
        b.w("v", "t", lnsum_t)

        def syr(k):
            return self.xa[0:1, k * 512:k * 512 + 512]

        def sy2r(k):
            t2 = [self.xf, self.xf, self.ob2, self.ob2][k]
            c = [0, 512, 0, 512][k]
            return t2[0:1, c:c + 512]
        musq = self.mfin[0:1, 0:512]
        for k in range(8):
            dst = syr(k) if k < 4 else sy2r(k - 4)
            b.op("v", lambda v, k=k, dst=dst: v.tensor_copy(dst, stat_sl(k)),
                 inc="v" if k == 7 else None)
        for k in range(4):
            b.op("v", lambda v, k=k: v.tensor_scalar(syr(k), syr(k), 1.0 / C,
                                                     None, ALU.mult))
            b.op("v", lambda v, k=k: v.tensor_scalar(sy2r(k), sy2r(k), 1.0 / C,
                                                     None, ALU.mult))
            b.op("v", lambda v, k=k: v.tensor_tensor(musq, syr(k), syr(k),
                                                     ALU.mult))
            b.op("v", lambda v, k=k: v.tensor_tensor(sy2r(k), sy2r(k), musq,
                                                     ALU.subtract))
            b.op("v", lambda v, k=k: v.tensor_scalar(sy2r(k), sy2r(k), 1.0, 1e-5,
                                                     ALU.mult, ALU.add),
                 inc="v" if k == 3 else None)
        b.w("s", "v", b.c["v"])
        for k in range(4):
            b.op("s", lambda s, k=k: s.sqrt(sy2r(k), sy2r(k)),
                 inc="s" if k == 3 else None)
        b.w("v", "s", b.c["s"])
        for k in range(4):
            b.op("v", lambda v, k=k: v.reciprocal(sy2r(k), sy2r(k)))
            b.op("v", lambda v, k=k: v.tensor_tensor(syr(k), syr(k), sy2r(k),
                                                     ALU.mult))
            b.op("v", lambda v, k=k: v.tensor_scalar(syr(k), syr(k), -1.0, None,
                                                     ALU.mult),
                 inc="v" if k == 3 else None)
        stat_v = b.c["v"]
        b.w("t", "v", stat_v)
        bc_slist = []
        for k in range(8):
            which, gg = k // 4, k % 4
            row = sy2r(gg) if which == 0 else syr(gg)
            slot = R1 + (k % 2) * 512
            if k >= 2:
                b.w("t", "s", bc_slist[k - 2])
            b.op("t", lambda t, row=row, slot=slot: t.matmul(
                ps[:, slot:slot + 512], self.cb["ones_f"][0:1, :],
                row, start=True, stop=True), inc="t")
            b.w("s", "t", b.c["t"])
            b.op("s", lambda s, which=which, gg=gg, slot=slot: s.copy(
                self.big0[:, 16384 + which * 2048 + gg * 512:
                          16384 + which * 2048 + gg * 512 + 512],
                ps[:, slot:slot + 512]), inc="s")
            bc_slist.append(b.c["s"])
        bc_s = b.c["s"]
        for cc in range(IC):
            if cc == 0:
                b.w("v", "s", bc_s)
            for gg in range(4):
                slc = slice(cc * 2048 + gg * 512, cc * 2048 + gg * 512 + 512)
                abc = self.big0[:, 16384 + gg * 512:16384 + gg * 512 + 512]
                bbc = self.big0[:, 18432 + gg * 512:18432 + gg * 512 + 512]
                b.op("v", lambda v, slc=slc, abc=abc: v.tensor_tensor(
                    self.big1[:, slc], self.big1[:, slc], abc, ALU.mult))
                b.op("v", lambda v, slc=slc, bbc=bbc: v.tensor_tensor(
                    self.big1[:, slc], self.big1[:, slc], bbc, ALU.add),
                    inc="v" if gg == 3 else None)
            b.w("s", "v", b.c["v"])
            b.op("s", lambda s, cc=cc: s.activation(
                self.big1[:, cc * 2048:cc * 2048 + 2048],
                self.big1[:, cc * 2048:cc * 2048 + 2048], AF.Identity,
                bias=self.smal[:, 8 + cc:9 + cc], scale=self.smal[:, cc:cc + 1]),
                inc="s")
        ln_done2 = b.c["s"]
        if "ln" in TAPS:
            b.w("q", "s", ln_done2)
            b.dma("q", self.tap_t["ln"], self.big1[:, :], sem="tap")
        if self.stop_after == "LN":
            self.finish(b)
            return

        sfm_s = {}
        for oc in range(IC):
            pbase = (oc % 2) * 2048
            if oc >= 2:
                b.w("t", "s", sfm_s[oc - 2])
            if oc == 0:
                b.w("t", "s", ln_done2)
            for i in range(IC):
                for tg in range(4):
                    b.op("t", lambda t, oc=oc, i=i, tg=tg, pbase=pbase: t.matmul(
                        ps[:, pbase + tg * 512:pbase + tg * 512 + 512],
                        self.big0[:, 8192 + i * 1024 + oc * 128:
                                  8192 + i * 1024 + oc * 128 + 128],
                        self.big1[:, i * 2048 + tg * 512:i * 2048 + tg * 512 + 512],
                        start=(i == 0), stop=False))
            for tg in range(4):
                rr = "oa" if oc < 4 else "ob"
                c0 = (oc % 4) * 128
                rp = ROWSA[rr][0]
                b.op("t", lambda t, tg=tg, rr=rr, c0=c0, rp=rp, pbase=pbase: t.matmul(
                    ps[:, pbase + tg * 512:pbase + tg * 512 + 512],
                    self.rrow(rr, c0, 128), self.cb["ones_b"][rp:rp + 1, :],
                    start=False, stop=True), inc="t" if tg == 3 else None)
            b.w("s", "t", b.c["t"])
            b.op("s", lambda s, oc=oc, pbase=pbase: s.copy(
                self.xnb[:, oc * 2048:oc * 2048 + 2048], ps[:, pbase:pbase + 2048]),
                inc="s")
            sfm_s[oc] = b.c["s"]
        sfm_done = b.c["s"]
        if "sfm" in TAPS:
            b.w("q", "s", sfm_done)
            b.dma("q", self.tap_t["sfm"], self.xnb[:, :], sem="tap")
        if self.stop_after == "SFM":
            self.finish(b)
            return

        stm_s = {}
        for T in range(NT):
            pb = R0 + (T % 2) * 1024
            if T >= 2:
                b.w("t", "s", stm_s[T - 2])
            if T == 0:
                b.w("t", "s", sfm_done)
            for j in range(IC):
                b.op("t", lambda t, T=T, j=j, pb=pb: t.matmul(
                    ps[:, pb + j * 128:pb + j * 128 + 128],
                    self.xnb[:, j * 2048 + T * 128:j * 2048 + T * 128 + 128],
                    self.cb["i128b"][:, :], start=True, stop=True),
                    inc="t" if j == IC - 1 else None)
            b.w("s", "t", b.c["t"])
            b.op("s", lambda s, T=T, pb=pb: s.copy(
                self.big1[:, T * 1024:T * 1024 + 1024], ps[:, pb:pb + 1024]),
                inc="s")
            stm_s[T] = b.c["s"]
        if "stm" in TAPS:
            b.w("q", "s", b.c["s"])
            b.dma("q", self.tap_t["stm"], self.big1[:, :], sem="tap")
        if self.stop_after == "STM":
            self.finish(b)
            return

        fin_v = {}
        od = {}
        for T in range(NTL):
            sl = (T % 2) * 1024
            if T >= 2:
                b.w("q", "v", fin_v[T - 2])
            elif T == 0:
                b.w("q", "s", bc_s)
            xd = b.dma("q", self.xa[:, sl:sl + 1024],
                       io["x"][T * 128:(T + 1) * 128, :],
                       sem="xa" if T % 2 == 0 else "xb")
            ob = self.xf[:, 0:1024] if T % 2 == 0 else self.ob2[:, 0:1024]
            b.w("v", "s", stm_s[T])
            if T == 0:
                b.w("v", "s", silu_s)
            b.w("v", "xa" if T % 2 == 0 else "xb", xd)
            if T >= 2:
                b.w("v", "oa" if T % 2 == 0 else "ob", od[T - 2])
            b.op("v", lambda v, T=T: v.tensor_tensor(
                self.sgs[:, :], self.big1[:, T * 1024:T * 1024 + 1024],
                self.gbuf[:, T * 1024:T * 1024 + 1024], ALU.mult))
            b.op("v", lambda v, T=T, sl=sl, ob=ob: v.tensor_tensor(
                ob, self.xa[:, sl:sl + 1024], self.sgs[:, :], ALU.add), inc="v")
            fin_v[T] = b.c["v"]
            b.w("q", "v", fin_v[T])
            od[T] = b.dma("q", io["out"][T * 128:(T + 1) * 128, :], ob,
                          sem="oa" if T % 2 == 0 else "ob")
        self.finish(b)

    def finish(self, b):
        if self.stop_after:
            b.w("q", "v", b.c["v"])
            b.w("q", "s", b.c["s"])
            b.w("q", "t", b.c["t"])
            b.dma("q", self.io["out"][0:128, :], self.xa[:, 0:1024], sem="oa")
        b.w("q", "oa", b.c["oa"])
        b.w("q", "ob", b.c["ob"])
        b.w("q", "tap", b.c["tap"])
        b.w("q", "pa", b.c["pa"])
        b.w("q", "pb", b.c["pb"])
        b.w("q", "pc", b.c["pc"])
        b.w("q", "wm", b.c["wm"])
        b.w("q", "w1", b.c["w1"])
        b.w("q", "pd", b.c["pd"])
        b.w("q", "pe", b.c["pe"])

    def build(self):
        nc = self.nc
        import contextlib
        names = ["t", "v", "s", "cc",
                 "qi", "gi", "wn", "xa", "xb", "w1", "sa", "sb",
                 "cw", "cr", "oa", "ob", "tap",
                 "pa", "pb", "pc", "wm", "pd", "pe"]
        with contextlib.ExitStack() as stk:
            sems = {n: stk.enter_context(nc.semaphore("sem_" + n)) for n in names}
            block = stk.enter_context(nc.Block())

            @block.sync
            def _(e):
                self.walk(Wk("q", e, sems))

            @block.gpsimd
            def _(e):
                self.walk(Wk("g", e, sems))

            @block.tensor
            def _(e):
                self.walk(Wk("t", e, sems))

            @block.vector
            def _(e):
                self.walk(Wk("v", e, sems))

            @block.scalar
            def _(e):
                self.walk(Wk("s", e, sems))
        return nc


def build_program(taps=(), stop_after=None, nt=NT, init_only=False, a_stage=4,
                  dbg=()):
    return MK(taps=taps, stop_after=stop_after, nt=nt, init_only=init_only,
              a_stage=a_stage, dbg=dbg).build()


def make_in_maps(inputs):
    x = inputs["L0_attn_x"]
    maps = []
    for c in range(8):
        bsel = c // 4
        t0 = (c % 4) * TC
        m = {
            "x": x[bsel, t0:t0 + TC, :],
            "wq": inputs["L0_attn_Wq"], "wk1": inputs["L0_attn_Wk1"],
            "wk2": inputs["L0_attn_Wk2"], "wv1": inputs["L0_attn_Wv1"],
            "wv2": inputs["L0_attn_Wv2"], "wo": inputs["L0_attn_Wo"],
            "biases": np.stack([
                inputs["L0_attn_bq"], inputs["L0_attn_bk1"],
                inputs["L0_attn_bk2"], inputs["L0_attn_bv1"],
                inputs["L0_attn_bv2"], inputs["L0_attn_bo"]]),
            "gvec": np.stack([
                inputs["L0_attn_g1"], inputs["L0_attn_g2"],
                inputs["L0_attn_g3"], inputs["L0_attn_g4"]]),
            "lng": inputs["L0_attn_ln_g"], "lnb": inputs["L0_attn_ln_b"],
            "gw": inputs["L0_ffn_gate_w"], "gb": inputs["L0_ffn_gate_b"],
            "eb": inputs["L0_ffn_expert_bias"],
            "w1": inputs["L0_ffn_w1"], "b1": inputs["L0_ffn_b1"],
            "w2": inputs["L0_ffn_w2"], "b2": inputs["L0_ffn_b2"],
        }
        maps.append({k: np.ascontiguousarray(v, dtype=np.float32)
                     for k, v in m.items()})
    return maps


def kernel(**inputs):
    nc = build_program()
    in_maps = make_in_maps(inputs)
    res = run_bass_kernel_spmd(nc, in_maps, core_ids=list(range(8)))
    out = np.empty((2, NFULL, C), dtype=np.float32)
    for c in range(8):
        out[c // 4, (c % 4) * TC:(c % 4 + 1) * TC, :] = res.results[c]["out"]
    return out


# version 09
# speedup vs baseline: 1.0357x; 1.0108x over previous
import numpy as np
import ml_dtypes
import concourse.bass as bass
import concourse.mybir as mybir
from concourse.bass_utils import run_bass_kernel_spmd

F32 = mybir.dt.float32
BF16 = mybir.dt.bfloat16
AF = mybir.ActivationFunctionType
ALU = mybir.AluOpType
AX = mybir.AxisListType

TC = 2048
NT = 16
C = 1024
IC = 8
H = 16
E = 8
FF = 4096
FC = 32
NFULL = 8192
RG = [[0, 1, 2, 3], [4, 5, 6, 7]]
BF = ml_dtypes.bfloat16
INVN2 = 1.0 / (float(NFULL) * float(NFULL))

PS_S1 = 0
PS_S2 = 512
PS_MA = 1024
PS_MB = 1536
R0, R1, R2, R3 = 2048, 2560, 3072, 3584

ROWSA = {
    "k1a": (0, 0), "k1b": (32, 0), "k2a": (64, 0),
    "k2b": (0, 1), "v1a": (32, 1), "v1b": (64, 1),
    "v2a": (0, 2), "v2b": (32, 2), "qa": (64, 2),
    "qb": (0, 3), "oa": (32, 3), "ob": (64, 3),
}


class Wk:

    def __init__(self, cur, eng, sems):
        self.cur = cur
        self.e = eng
        self.sems = sems
        self.c = {k: 0 for k in sems}

    def dma(self, qn, out, in_, sem=None, slow=False):
        sem = sem or qn
        self.c[sem] += 16
        if self.cur == qn:
            self.e.dma_start(out=out, in_=in_,
                             allow_slow_non_contiguous=slow).then_inc(
                self.sems[sem], 16)
        return self.c[sem]

    def w(self, qn, sem, val):
        if self.cur == qn and val > 0:
            self.e.wait_ge(self.sems[sem], val)

    def op(self, qn, fn, inc=None):
        if inc:
            self.c[inc] += 1
        if self.cur == qn:
            ins = fn(self.e)
            if inc:
                ins.then_inc(self.sems[inc], 1)
        return self.c[inc] if inc else None


class MK:
    def __init__(self, taps=(), stop_after=None, nt=NT, init_only=False,
                 a_stage=4, dbg=()):
        self.taps = set(taps)
        self.stop_after = stop_after
        self.nt = nt
        self.init_only = init_only
        self.a_stage = a_stage
        self.dbg = set(dbg)
        nc = bass.Bass("TRN2", target_bir_lowering=False,
                       detect_race_conditions=False)
        self.nc = nc

        self.io = {}
        for nm, sh in (
            ("x", [TC, C]), ("wq", [C, C]), ("wk1", [C, C]), ("wk2", [C, C]),
            ("wv1", [C, C]), ("wv2", [C, C]), ("wo", [C, C]),
            ("biases", [6, C]), ("gvec", [4, H]), ("lng", [C]), ("lnb", [C]),
            ("gw", [E, C]), ("gb", [E]), ("eb", [E]),
            ("w1", [E, C, FF]), ("b1", [E, FF]), ("w2", [E, FF, C]), ("b2", [E, C]),
        ):
            self.io[nm] = nc.dram_tensor(nm, sh, F32, kind="ExternalInput").ap()
        self.io["out"] = nc.dram_tensor("out", [TC, C], F32, kind="ExternalOutput").ap()
        self.io["cc_in"] = nc.dram_tensor("cc_in", [128, 1536], F32, kind="Internal").ap()
        self.io["cc_out"] = nc.dram_tensor("cc_out", [128, 1536], F32, kind="Internal").ap()
        self.w2b = nc.dram_tensor("w2b", [E, 16, 128, 2048], BF16, kind="Internal").ap()
        self.w1b = nc.dram_tensor("w1b", [E, 128, 32768], BF16, kind="Internal").ap()

        self.tap_t = {}
        for nm, sh, dt in (
            ("xnb", [128, IC * TC], BF16), ("kv0", [128, 4096], BF16),
            ("scc", [128, 1536], F32), ("rcc", [128, 1536], F32),
            ("onehot", [128, NT * 8], F32), ("gbuf", [128, NT * C], BF16),
            ("qb", [128, 16384], BF16), ("mm", [128, 512], F32),
            ("y2r", [2, 512], F32), ("ynat", [128, IC * TC], BF16),
            ("ln", [128, IC * TC], BF16), ("sfm", [128, IC * TC], BF16),
            ("stm", [128, NT * C], BF16),
        ):
            if nm in self.taps:
                self.tap_t[nm] = nc.dram_tensor("tap_" + nm, sh, dt,
                                                kind="ExternalOutput").ap()

        def inl(name, arr):
            return nc.inline_tensor(arr, name).ap()
        et = np.zeros((128, 1), np.float32); et[:64] = 1.0
        ebv = np.zeros((128, 1), np.float32); ebv[64:] = 1.0
        st = np.zeros((8, 512), BF)
        for qq in range(512):
            st[qq // 64, qq] = 1.0
        self.consts = {
            "i128b": np.eye(128, dtype=BF), "i128f": np.eye(128, dtype=np.float32),
            "i8f": np.eye(8, dtype=np.float32),
            "ones_b": np.ones((128, 512), dtype=BF),
            "ones_f": np.ones((128, 128), np.float32),
            "onescol_b": np.ones((128, 1), dtype=BF),
            "etop": et, "ebot": ebv,
            "sel8": st,
        }
        self.cin = {k: inl(k, v) for k, v in self.consts.items()}

        A = nc.alloc_sbuf_tensor
        self.xnb = A("xnb", [128, IC * TC], BF16)
        self.gbuf = A("gbuf", [128, NT * C], BF16)
        self.big0 = A("big0", [128, 32768], BF16)
        self.big1 = A("big1", [128, 16384], BF16)
        self.wsl = A("wsl", [128, 6144], BF16)
        self.xa = A("xa", [128, 2048], F32)
        self.xf = A("xf", [128, 1024], F32)
        self.ob2 = A("ob2", [128, 1024], F32)
        self.ohm = A("ohm", [128, NT * 8], F32)
        self.sgs = A("sgs", [128, 1024], BF16)
        self.mfin = A("mfin", [128, 512], F32)
        self.mbuf = A("mbuf", [128, 512], BF16)
        self.csx = A("csx", [128, 512], F32)
        self.smal = A("smal", [128, 40], F32)
        self.rowsA = A("rowsA", [128, 2048], BF16)
        self.b1c = A("b1c", [128, 256], F32)
        self.b2n = A("b2n", [8, 1024], BF16)
        self.ge = A("ge", [8, 128], F32)
        self.geb = A("geb", [8, 128], BF16)
        self.gwT = A("gwT", [128, 64], F32)
        self.cb = {}
        for k, v in self.consts.items():
            self.cb[k] = A("c_" + k, list(v.shape),
                           BF16 if v.dtype == BF else F32)
        self.ps = nc.alloc_psum_tensor("PALL", [128, 4096], F32)

    def rrow(self, name, c0=0, n=512):
        p, g = ROWSA[name]
        return self.rowsA[p:p + 1, g * 512 + c0:g * 512 + c0 + n]

    def walk(self, b):
        io, ps = self.io, self.ps
        TAPS = self.taps

        for k in self.consts:
            b.dma("q", self.cb[k][:, :], self.cin[k], sem="qi")
        for i, pfx in enumerate(("q", "k1", "k2", "v1", "v2", "o")):
            for half, suf in ((0, "a"), (1, "b")):
                p, g = ROWSA[pfx + suf]
                b.dma("g", self.rowsA[p:p + 1, g * 512:g * 512 + 512],
                      io["biases"][i:i + 1, half * 512:half * 512 + 512], sem="gi")
        b.dma("g", self.b2n[:, :], io["b2"], sem="gi")
        b.dma("q", self.smal[:, 0:8], io["lng"].rearrange("(c p) -> p c", p=128),
              sem="qi", slow=True)
        b.dma("q", self.smal[:, 8:16], io["lnb"].rearrange("(c p) -> p c", p=128),
              sem="qi", slow=True)
        b.dma("q", self.smal[0:8, 16:17], io["gb"].rearrange("(a e) -> e a", a=1),
              sem="qi", slow=True)
        b.dma("q", self.smal[0:8, 17:18], io["eb"].rearrange("(a e) -> e a", a=1),
              sem="qi", slow=True)
        b.dma("q", self.smal[0:8, 18:26].rearrange("p (t g) -> p t g", t=2),
              io["gvec"].rearrange("g (h t) -> h t g", t=2), sem="qi", slow=True)
        b.dma("q", self.b1c[:, :], io["b1"].rearrange("e (c p) -> p (e c)", p=128),
              sem="qi", slow=True)
        qinit = b.dma("q", self.gwT[:, :],
                      io["gw"].rearrange("e (c p) -> p (e c)", p=128), sem="qi", slow=True)

        qi_all = b.c["qi"]
        gi_all = b.c["gi"]
        b.op("v", lambda v: v.memset(self.smal[:, 38:39], 1e-5), inc="v")
        eps_v = b.c["v"]

        wst = {"k": 0, "tr": {}, "ev": []}
        w_stage = (self.wsl[:, 0:4096], self.big1[:, 10240:14336])

        def wT_build(wnm, dst, cstride, first, addI=False):
            for half in range(2):
                k = wst["k"]
                wst["k"] += 1
                buf = w_stage[k % 2]
                sem = "wn" if k % 2 == 0 else "wm"
                if k >= 2:
                    b.w("g", "t", wst["tr"][k - 2])
                gv = b.dma("g", buf.rearrange("p (o i) -> p o i", o=4),
                           io[wnm][half * 512:half * 512 + 512, :].rearrange(
                               "(o p) i -> p o i", p=128), sem=sem)
                b.w("t", sem, gv)
                if first and half == 0:
                    b.w("t", "qi", qi_all)
                for ib in range(IC):
                    bk = R0 + (ib % 2) * 512
                    if len(wst["ev"]) >= 2:
                        b.w("t", "v", wst["ev"][-2])
                    mms = []
                    for ob4 in range(4):
                        diag = addI and (ob4 == ib - half * 4)
                        mms.append((ob4, False, not diag))
                        if diag:
                            mms.append((ob4, True, True))
                    for mi, (ob4, ident, stop) in enumerate(mms):
                        lastmm = mi == len(mms) - 1
                        lhs = (self.cb["i128b"][:, :] if ident else
                               buf[:, ob4 * 1024 + ib * 128:
                                   ob4 * 1024 + ib * 128 + 128])
                        b.op("t", lambda t, ob4=ob4, ident=ident, stop=stop,
                             lhs=lhs, bk=bk: t.matmul(
                                 ps[:, bk + ob4 * 128:bk + ob4 * 128 + 128],
                                 lhs, self.cb["i128b"][:, :],
                                 start=not ident, stop=stop),
                             inc="t" if lastmm else None)
                    b.w("v", "t", b.c["t"])
                    b.op("v", lambda v, ib=ib, half=half, bk=bk: v.tensor_copy(
                        self.big0[:, dst + ib * cstride + half * 512:
                                  dst + ib * cstride + half * 512 + 512],
                        ps[:, bk:bk + 512]), inc="v")
                    wst["ev"].append(b.c["v"])
                wst["tr"][k] = b.c["t"]

        pc_units = ([("w1", e0, i, hf) for e0 in range(2)
                     for i in range(IC) for hf in range(2)] +
                    [("w2", e, qd) for e in range(2) for qd in range(16)])
        pcv = {}
        PC_SEMS = ("pa", "pb", "pd", "pe")
        pc_prev = {s: None for s in PC_SEMS}

        def emit_precast_unit(kind, u_parity):
            sem = PC_SEMS[u_parity % 4]
            if pc_prev[sem] is not None:
                b.w("g", sem, pc_prev[sem])
            if kind[0] == "w1":
                _, e0, i, hf = kind
                d = b.dma("g",
                          self.w1b[e0][:, i * 4096 + hf * 2048:
                                       i * 4096 + hf * 2048 + 2048],
                          io["w1"][e0][i * 128:(i + 1) * 128,
                                       hf * 2048:(hf + 1) * 2048], sem=sem)
            else:
                _, e, qd = kind
                d = b.dma("g", self.w2b[e, qd].rearrange("p (c o) -> p c o", c=2),
                          io["w2"][e, qd * 256:(qd + 1) * 256, :].rearrange(
                              "(c p) o -> p c o", p=128), sem=sem)
            pcv[kind] = (sem, d)
            pc_prev[sem] = d

        def emit_precast(u):
            emit_precast_unit(pc_units[u], u % 4)

        for u in range(4):
            emit_precast(u)

        wT_build("wk1", 0, 2048, True, addI=True)
        wT_build("wk2", 1024, 2048, False, addI=True)
        wT_build("wv1", 16384, 2048, False, addI=True)
        wT_build("wv2", 16384 + 1024, 2048, False, addI=True)
        wkv_v = b.c["v"]
        if self.init_only:
            if "kv0" in self.taps:
                b.w("q", "v", wkv_v)
                b.dma("q", self.tap_t["kv0"], self.big0[:, 0:4096], sem="tap")
            self.finish(b)
            return
        NTL = self.nt

        xdma = {}
        ln_done = {}
        xnT_s = {}
        trv = {}
        gate_t = {}
        ge_s = {}
        maskv = {}
        zev = {}
        smm_t = {}
        xdma[0] = b.dma("q", self.xa[:, 0:1024], io["x"][0:128, :], sem="xa")
        xdma[1] = b.dma("q", self.xa[:, 1024:2048], io["x"][128:256, :], sem="xb")
        for T in range(NTL):
            sl = (T % 2) * 1024
            xt = self.xa[:, sl:sl + 1024]
            b.w("v", "xa" if T % 2 == 0 else "xb", xdma[T])
            if T == 0:
                b.w("s", "qi", qi_all)
            b.op("v", lambda v, xt=xt: v.reduce_sum(self.smal[:, 32:33], xt, AX.X),
                 inc="v")
            b.w("s", "v", b.c["v"])
            b.op("s", lambda s: s.mul(self.smal[:, 33:34], self.smal[:, 32:33],
                                      -1.0 / C), inc="s")
            b.w("v", "s", b.c["s"])
            b.op("v", lambda v, xt=xt: v.tensor_scalar(xt, xt, self.smal[:, 33:34],
                                                       None, ALU.add), inc="v")
            b.w("s", "v", b.c["v"])
            b.op("s", lambda s, xt=xt, T=T: s.activation(
                self.big1[:, 8192 + (T % 2) * 1024:8192 + (T % 2) * 1024 + 1024],
                xt, AF.Square, accum_out=self.smal[:, 33:34]))
            b.op("s", lambda s: s.activation(self.smal[:, 34:35],
                                             self.smal[:, 33:34], AF.Sqrt,
                                             bias=self.smal[:, 38:39],
                                             scale=1.0 / C), inc="s")
            b.w("v", "s", b.c["s"])
            b.op("v", lambda v: v.reciprocal(self.smal[:, 35:36], self.smal[:, 34:35]),
                 inc="v")
            b.w("s", "v", b.c["v"])
            b.op("s", lambda s, xt=xt: s.mul(xt, xt, self.smal[:, 35:36]), inc="s")
            ln_done[T] = b.c["s"]

            if self.a_stage < 1:
                xnT_s[T] = b.c["s"]
                maskv[T] = b.c["v"]
                gate_t[T] = b.c["t"]
                ge_s[T] = b.c["s"]
                zev[(T, 0, 0)] = zev[(T, 0, 1)] = b.c["s"]
                zev[(T, 1, 0)] = zev[(T, 1, 1)] = b.c["s"]
                smm_t[T] = b.c["t"]
                continue
            b.w("t", "s", ln_done[T])
            if T > 0:
                b.w("t", "s", zev[(T - 1, 1, 0)])
            for j in range(IC):
                b.op("t", lambda t, j=j, xt=xt: t.matmul(
                    ps[:, R0 + j * 128:R0 + j * 128 + 128],
                    xt[:, j * 128:j * 128 + 128], self.cb["i128f"][:, :],
                    start=True, stop=True), inc="t" if j == IC - 1 else None)
            trT = b.c["t"]
            trv[T] = trT
            b.w("v", "t", trT)
            if "noxf" in self.dbg:
                b.op("v", lambda v: v.engine_nop(), inc="v")
            else:
                b.op("v", lambda v: v.tensor_copy(self.xf[:, :], ps[:, R0:R0 + 1024]),
                     inc="v")
            xf_v = b.c["v"]
            b.w("s", "t", trT)
            b.w("s", "v", xf_v)
            if "noxnb" in self.dbg:
                b.op("s", lambda s: s.copy(self.xnb[:, 0:1024],
                                           ps[:, R0:R0 + 1024]), inc="s")
            else:
                b.op("s", lambda s, T=T: s.copy(
                    self.xnb[:, :].rearrange("p (c f) -> p c f", c=IC)[
                        :, :, T * 128:T * 128 + 128],
                    ps[:, R0:R0 + 1024].rearrange("p (c f) -> p c f", f=128)),
                    inc="s")
            xnT_s[T] = b.c["s"]

            if self.a_stage < 2:
                maskv[T] = b.c["v"]
                gate_t[T] = b.c["t"]
                ge_s[T] = b.c["s"]
                zev[(T, 0, 0)] = zev[(T, 0, 1)] = b.c["s"]
                zev[(T, 1, 0)] = zev[(T, 1, 1)] = b.c["s"]
                smm_t[T] = b.c["t"]
                continue
            b.w("t", "v", xf_v)
            if T > 0:
                b.w("t", "s", ge_s[T - 1])
                b.w("t", "s", zev[(T - 1, 1, 1)])
            for i in range(IC):
                b.op("t", lambda t, i=i: t.matmul(
                    ps[0:8, R2:R2 + 128],
                    self.gwT[:, :].rearrange("p (e c) -> p c e", c=IC)[:, i, :],
                    self.xf[:, i * 128:i * 128 + 128],
                    start=(i == 0), stop=(i == IC - 1)),
                    inc="t" if i == IC - 1 else None)
            gmm = b.c["t"]
            b.w("s", "t", gmm)
            b.op("s", lambda s: s.activation(self.ge[:, :], ps[0:8, R2:R2 + 128],
                                             AF.Identity, bias=self.smal[0:8, 16:17]))
            b.op("s", lambda s: s.add(self.ge[:, :], self.ge[:, :],
                                      self.smal[0:8, 17:18]), inc="s")
            ge_s[T] = b.c["s"]
            b.w("t", "s", ge_s[T])
            b.op("t", lambda t: t.matmul(ps[:, R2 + 128:R2 + 136], self.ge[:, :],
                                         self.cb["i8f"][:, :], start=True, stop=True),
                 inc="t")
            gate_t[T] = b.c["t"]
            b.w("v", "t", gate_t[T])
            b.op("v", lambda v: v.reduce_max(self.smal[:, 36:37],
                                             ps[:, R2 + 128:R2 + 136], AX.X),
                 inc="v")
            b.w("s", "v", b.c["v"])
            b.op("s", lambda s: s.copy(self.smal[:, 37:38], self.smal[:, 36:37]),
                 inc="s")
            b.w("v", "s", b.c["s"])
            b.op("v", lambda v, T=T: v.tensor_scalar(
                self.ohm[:, T * 8:T * 8 + 8], ps[:, R2 + 128:R2 + 136],
                self.smal[:, 37:38], None, ALU.is_ge), inc="v")
            maskv[T] = b.c["v"]

            if self.a_stage < 3:
                zev[(T, 0, 0)] = zev[(T, 0, 1)] = b.c["s"]
                zev[(T, 1, 0)] = zev[(T, 1, 1)] = b.c["s"]
                smm_t[T] = b.c["t"]
                continue
            kvs = (T % 2) * 4096
            for rnd in range(2):
                wbase = rnd * 16384
                rks = (("k1a", "k1b", "k2a", "k2b") if rnd == 0
                       else ("v1a", "v1b", "v2a", "v2b"))
                b.w("t", "s", xnT_s[T])
                b.w("t", "v", maskv[T])
                if T == 0 and rnd == 0:
                    b.w("t", "v", wkv_v)
                    b.w("t", "gi", gi_all)
                if rnd == 0 and T > 0:
                    b.w("t", "s", zev[(T - 1, 1, 1)])
                zt_g = {}
                for g4 in range(4):
                    rot = R0 + g4 * 512
                    if rnd == 1 and g4 % 2 == 0:
                        b.w("t", "s", zev[(T, 0, g4 // 2)])
                    for i in range(IC):
                        b.op("t", lambda t, i=i, g4=g4, rot=rot, wbase=wbase, T=T:
                             t.matmul(
                                 ps[:, rot:rot + 512],
                                 self.xnb[:, i * 2048 + T * 128:
                                          i * 2048 + T * 128 + 128],
                                 self.big0[:, wbase + i * 2048 + g4 * 512:
                                           wbase + i * 2048 + g4 * 512 + 512],
                                 start=(i == 0), stop=False))
                    rp = ROWSA[rks[g4]][0]
                    b.op("t", lambda t, rot=rot, rr=rks[g4], rp=rp: t.matmul(
                        ps[:, rot:rot + 512], self.cb["ones_b"][rp:rp + 1, 0:128],
                        self.rrow(rr), start=False, stop=True), inc="t")
                    zt_g[g4] = b.c["t"]
                for g2 in range(2):
                    rot = R0 + g2 * 1024
                    b.w("s", "t", zt_g[2 * g2 + 1])
                    b.op("s", lambda s, g2=g2, rot=rot, rnd=rnd, kvs=kvs: s.copy(
                        self.big1[:, kvs + rnd * 2048 + g2 * 1024:
                                  kvs + rnd * 2048 + g2 * 1024 + 1024],
                        ps[:, rot:rot + 1024]), inc="s")
                    zev[(T, rnd, g2)] = b.c["s"]

            if self.a_stage < 4:
                smm_t[T] = b.c["t"]
                if T + 2 < NTL:
                    b.w("q", "s", zev[(T, 1, 1)])
                    xdma[T + 2] = b.dma("q", self.xa[:, sl:sl + 1024],
                                        io["x"][(T + 2) * 128:(T + 3) * 128, :],
                                        sem="xa" if T % 2 == 0 else "xb")
                continue
            b.w("t", "s", zev[(T, 1, 1)])
            for sidx in range(2):
                for h in range(H):
                    r0 = (h % 2) * 64
                    koff = kvs + sidx * 1024 + h * 64
                    voff = kvs + 2048 + sidx * 1024 + h * 64
                    cc0 = sidx * 512 + (h // 2) * 64
                    b.op("t", lambda t, koff=koff, voff=voff, r0=r0, cc0=cc0, T=T, h=h:
                         t.matmul(ps[r0:r0 + 64, cc0:cc0 + 64],
                                  self.big1[:, koff:koff + 64],
                                  self.big1[:, voff:voff + 64],
                                  start=(T == 0 and h < 2),
                                  stop=(T == NTL - 1 and h >= H - 2)))
            for mi in range(4):
                vb = kvs + 2048 + (mi // 2) * 1024 + (mi % 2) * 512
                mp = (mi % 3) * 32 if mi < 3 else 0
                mcol = PS_MA if mi < 3 else PS_MB
                b.op("t", lambda t, vb=vb, mp=mp, mcol=mcol, T=T, mi=mi: t.matmul(
                    ps[mp:mp + 1, mcol:mcol + 512], self.cb["onescol_b"][:, :],
                    self.big1[:, vb:vb + 512],
                    start=(T == 0), stop=(T == NTL - 1)),
                    inc="t" if mi == 3 else None)
            smm_t[T] = b.c["t"]
            if T + 2 < NTL:
                b.w("q", "t", trv[T])
                xdma[T + 2] = b.dma("q", self.xa[:, sl:sl + 1024],
                                    io["x"][(T + 2) * 128:(T + 3) * 128, :],
                                    sem="xa" if T % 2 == 0 else "xb")
            for u in range(max(4, T * 4), min((T + 1) * 4, len(pc_units))):
                emit_precast(u)
            if T == 10:
                for k in range(4):
                    for hf in range(2):
                        ps_, pv_ = pcv[("w1", 0, k, hf)]
                        b.w("g", ps_, pv_)
                for k in range(4):
                    gbuf_w1 = b.dma("g", self.gbuf[:, k * 4096:(k + 1) * 4096],
                                    self.w1b[0][:, k * 4096:(k + 1) * 4096],
                                    sem="w1")
        for u in range(max(4, NTL * 4), len(pc_units)):
            emit_precast(u)
        s_done_t = smm_t[NTL - 1]

        if self.a_stage < 4:
            if "xnb" in TAPS and self.a_stage >= 1:
                b.w("q", "s", xnT_s[NTL - 1])
                b.dma("q", self.tap_t["xnb"], self.xnb[:, :], sem="tap")
            if self.a_stage < 1:
                b.w("q", "s", ln_done[NTL - 1])
            if "onehot" in TAPS and self.a_stage >= 2:
                b.w("q", "v", maskv[NTL - 1])
                b.dma("q", self.tap_t["onehot"], self.ohm[:, :], sem="tap")
            if "kv0" in TAPS and self.a_stage >= 3:
                b.w("q", "s", zev[(max(0, NTL - 2), 1, 1)])
                b.dma("q", self.tap_t["kv0"], self.big1[:, 0:4096], sem="tap")
            self.finish(b)
            return
        b.w("v", "t", s_done_t)
        b.op("v", lambda v: v.tensor_copy(self.xa[:, 0:1024], ps[:, 0:1024]))
        b.op("v", lambda v: v.memset(self.xa[:, 1024:1536], 0.0))
        b.op("v", lambda v: v.tensor_copy(self.xa[0:1, 1024:1536], ps[0:1, PS_MA:PS_MA + 512]))
        b.op("v", lambda v: v.tensor_copy(self.xa[32:33, 1024:1536], ps[32:33, PS_MA:PS_MA + 512]))
        b.op("v", lambda v: v.tensor_copy(self.xa[64:65, 1024:1536], ps[64:65, PS_MA:PS_MA + 512]))
        b.op("v", lambda v: v.tensor_copy(self.xa[96:97, 1024:1536], ps[0:1, PS_MB:PS_MB + 512]),
             inc="v")
        ccv = b.c["v"]
        if "scc" in TAPS:
            b.w("q", "v", ccv)
            b.dma("q", self.tap_t["scc"], self.xa[:, 0:1536], sem="tap")
        b.w("g", "v", ccv)
        ccw = b.dma("g", io["cc_in"], self.xa[:, 0:1536], sem="cw")
        if self.stop_after != "A":
            b.w("g", "cw", ccw)
            b.op("g", lambda g: g.collective_compute(
                "AllReduce", ALU.add, replica_groups=RG,
                ins=[io["cc_in"]], outs=[io["cc_out"]]), inc="cc")

        if "xnb" in TAPS:
            b.w("q", "s", xnT_s[NTL - 1])
            b.dma("q", self.tap_t["xnb"], self.xnb[:, :], sem="tap")
        if "kv0" in TAPS:
            b.w("q", "s", zev[(max(0, NTL - 2), 1, 1)])
            b.dma("q", self.tap_t["kv0"], self.big1[:, 0:4096], sem="tap")
        if "onehot" in TAPS:
            b.w("q", "v", maskv[NTL - 1])
            b.dma("q", self.tap_t["onehot"], self.ohm[:, :], sem="tap")

        if self.stop_after == "A":
            self.finish(b)
            return

        b.w("g", "t", s_done_t)
        for i in range(4, IC):
            for hf in range(2):
                ps_, pv_ = pcv[("w1", 0, i, hf)]
                b.w("g", ps_, pv_)
        w1v = [None] * E
        for i in range(4, IC):
            w1hi = b.dma("g", self.big0[:, i * 4096:(i + 1) * 4096],
                         self.w1b[0][:, i * 4096:(i + 1) * 4096], sem="w1")
        b.w("g", "w1", w1hi)
        for i in range(4):
            w1v[0] = b.dma("g", self.big0[:, i * 4096:(i + 1) * 4096],
                           self.w1b[0][:, i * 4096:(i + 1) * 4096], sem="w1")
        slabv = {}
        quad_t = {}
        fq = {}
        slab_n = [0]
        rb_done = [0]
        SLAB_SEMS = ("sa", "sb", "pc", "wn", "wm", "cw")

        def emit_slab(e, ts, fcc, wait=None):
            gidx = slab_n[0]
            slab_n[0] += 1
            sidx = gidx % 6
            if wait is not None:
                b.w("q", "t", wait)
            psem, pval = pcv[("w2", e, fcc // 2)]
            b.w("q", psem, pval)
            slabv[(e, ts, fcc)] = (sidx, b.dma(
                "q", self.wsl[:, sidx * 1024:sidx * 1024 + 1024],
                self.w2b[e, fcc // 2][:, (fcc % 2) * 1024:
                                      (fcc % 2) * 1024 + 1024],
                sem=SLAB_SEMS[sidx]))

        h_ev = {}
        h_last_t = {}
        yev = {}
        yev_g0 = {}
        yev_h0 = {}

        for e in range(E):
            prev = (e - 1, 3) if e > 0 else None
            for ts in range(4):
                for f0 in range(6):
                    wait = None
                    if (e, ts) != (0, 0):
                        pe, pts = (e, ts - 1) if ts > 0 else (e - 1, 3)
                        wait = fq[(pe, pts, 26 + f0)]
                    emit_slab(e, ts, f0, wait)
                if e + 2 < E:
                    for uu in range(4):
                        idx = ts * 4 + uu
                        emit_precast_unit(("w1", e + 2, idx // 2, idx % 2), uu)
                        emit_precast_unit(("w2", e + 2, idx), uu + 1)
                if e == 0 and ts == 0:
                    b.w("g", "cc", 1)
                    rb_done[0] = b.dma("g", self.xa[:, 0:1536], io["cc_out"],
                                       sem="cr")
                    if "rcc" in TAPS:
                        b.w("q", "cr", rb_done[0])
                        b.dma("q", self.tap_t["rcc"], self.xa[:, 0:1536], sem="tap")
                t0 = ts * 512
                prev2 = (e, ts - 1) if ts > 0 else prev
                for fc in range(FC):
                    hb = (fc % 2) * 512
                    if fc == 0 and ts == 0:
                        b.w("t", "w1", w1hi if e == 0 else w1v[e])
                    elif e == 0 and ts == 1 and fc == 0:
                        b.w("t", "w1", w1v[0])
                    if fc >= 2:
                        b.w("t", "s", h_ev[(e, ts, fc - 2)])
                    elif prev2 is not None:
                        b.w("t", "v", (yev_h0 if fc == 0 else yev_g0)[prev2])
                    elif fc == 0:
                        b.w("t", "v", ccv)
                    w1lo = self.gbuf if (e == 0 and ts == 0) else self.big0
                    for i in range(IC):
                        src = w1lo if i < 4 else self.big0
                        b.op("t", lambda t, i=i, fc=fc, hb=hb, t0=t0, src=src:
                             t.matmul(
                                 ps[:, hb:hb + 512],
                                 src[:, i * 4096 + fc * 128:
                                     i * 4096 + fc * 128 + 128],
                                 self.xnb[:, i * 2048 + t0:i * 2048 + t0 + 512],
                                 start=(i == 0), stop=(i == IC - 1)),
                             inc="t" if i == IC - 1 else None)
                    b.w("s", "t", b.c["t"])
                    b.op("s", lambda s, fc=fc, hb=hb, e=e: s.activation(
                        self.big1[:, fc * 512:fc * 512 + 512], ps[:, hb:hb + 512],
                        AF.Relu, bias=self.b1c[:, e * 32 + fc:e * 32 + fc + 1]),
                        inc="s")
                    h_ev[(e, ts, fc)] = b.c["s"]
                h_last_t[(e, ts)] = b.c["t"]

                if prev2 is not None:
                    b.w("t", "v", yev[prev2])
                else:
                    b.w("t", "v", ccv)
                for fc in range(FC):
                    sidx, sv0 = slabv[(e, ts, fc)]
                    b.w("t", SLAB_SEMS[sidx], sv0)
                    gseq = (2, 3, 1, 0) if fc == 0 else (0, 1, 2, 3)
                    for g in gseq:
                        if fc == 0 and g == 0:
                            b.w("t", "s", h_ev[(e, ts, FC - 1)])
                        for og in range(2):
                            b.op("t", lambda t, fc=fc, g=g, og=og, sidx=sidx: t.matmul(
                                ps[:, g * 1024 + og * 512:g * 1024 + og * 512 + 512],
                                self.big1[:, fc * 512 + g * 128:fc * 512 + g * 128 + 128],
                                self.wsl[:, sidx * 1024 + og * 512:
                                         sidx * 1024 + og * 512 + 512],
                                start=(fc == 0), stop=(fc == FC - 1)),
                                inc="t" if (gseq[-1] == g and og == 1) else None)
                    fq[(e, ts, fc)] = b.c["t"]
                    if fc % 2 == 1:
                        quad_t[(e, ts, fc // 2)] = b.c["t"]
                    if fc + 6 < FC:
                        emit_slab(e, ts, fc + 6, fq[(e, ts, fc)])
                    if ts == 3 and e + 1 < E and fc % 2 == 1 and fc < 16:
                        qd = fc // 2
                        if qd == 0:
                            b.w("g", "t", h_last_t[(e, 3)])
                        for hf in range(2):
                            ps_, pv_ = pcv[("w1", e + 1, qd, hf)]
                            b.w("g", ps_, pv_)
                        w1v[e + 1] = b.dma(
                            "g",
                            self.big0[:, qd * 4096:qd * 4096 + 4096],
                            self.w1b[e + 1][:, qd * 4096:qd * 4096 + 4096],
                            sem="w1")
                ymm = b.c["t"]
                b.w("v", "t", ymm)
                for g in range(4):
                    tile = ts * 4 + g
                    for hh in range(2 if g == 0 else 1):
                        wid = 512 if g == 0 else 1024
                        c0 = tile * 1024 + hh * 512
                        p0 = g * 1024 + hh * 512
                        if e == 0:
                            b.op("v", lambda v, c0=c0, p0=p0, wid=wid, tile=tile,
                                 e=e: v.tensor_scalar(
                                     self.gbuf[:, c0:c0 + wid],
                                     ps[:, p0:p0 + wid],
                                     self.ohm[:, tile * 8 + e:tile * 8 + e + 1],
                                     None, ALU.mult), inc="v")
                        else:
                            b.op("v", lambda v, c0=c0, p0=p0, wid=wid, tile=tile,
                                 e=e: v.scalar_tensor_tensor(
                                     self.gbuf[:, c0:c0 + wid],
                                     ps[:, p0:p0 + wid],
                                     self.ohm[:, tile * 8 + e:tile * 8 + e + 1],
                                     self.gbuf[:, c0:c0 + wid],
                                     ALU.mult, ALU.add), inc="v")
                        if g == 0 and hh == 0:
                            yev_h0[(e, ts)] = b.c["v"]
                    if g == 0:
                        yev_g0[(e, ts)] = b.c["v"]
                yev[(e, ts)] = b.c["v"]

        bm_s = {}
        bm_v = {}

        def emit_b2mix(tile):
            pslot = (tile % 2) * 1024
            if tile == 0:
                b.w("t", "v", yev[(E - 1, 3)])
            if tile >= 2:
                b.w("t", "v", bm_v[tile - 2])
            b.op("t", lambda t, tile=tile, pslot=pslot: t.matmul(
                ps[0:8, pslot + 512:pslot + 640],
                self.ohm[:, tile * 8:tile * 8 + 8], self.cb["i128f"][:, :],
                start=True, stop=True), inc="t")
            b.w("s", "t", b.c["t"])
            if tile >= 2:
                b.w("s", "s", bm_s[tile - 2])
            b.op("s", lambda s, tile=tile, pslot=pslot: s.copy(
                self.geb[:, :], ps[0:8, pslot + 512:pslot + 640]), inc="s")
            bm_s[tile] = b.c["s"]
            b.w("t", "s", bm_s[tile])
            for og in range(2):
                b.op("t", lambda t, og=og, pslot=pslot: t.matmul(
                    ps[:, pslot + og * 512:pslot + og * 512 + 512],
                    self.geb[:, :], self.b2n[:, og * 512:og * 512 + 512],
                    start=True, stop=True), inc="t" if og == 1 else None)
            b.w("v", "t", b.c["t"])
            b.op("v", lambda v, tile=tile, pslot=pslot: v.tensor_tensor(
                self.gbuf[:, tile * 1024:tile * 1024 + 1024],
                self.gbuf[:, tile * 1024:tile * 1024 + 1024],
                ps[:, pslot:pslot + 1024], ALU.add), inc="v")
            bm_v[tile] = b.c["v"]
            b.w("s", "v", bm_v[tile])
            b.op("s", lambda s, tile=tile: s.activation(
                self.gbuf[:, tile * 1024:tile * 1024 + 1024],
                self.gbuf[:, tile * 1024:tile * 1024 + 1024], AF.Silu), inc="s")

        if self.stop_after == "FFN":
            for tile in range(NT):
                emit_b2mix(tile)
            silu_s = b.c["s"]
            if "gbuf" in TAPS:
                b.w("q", "s", silu_s)
                b.dma("q", self.tap_t["gbuf"], self.gbuf[:, :], sem="tap")
            self.finish(b)
            return

        b.w("g", "t", quad_t[(E - 1, 3, 15)])
        tst = {"k": 0, "tr": {}, "ev": []}
        for wnm, base in (("wq", 0), ("wo", 8192)):
            addI = wnm == "wq"
            for half in range(2):
                k = tst["k"]
                tst["k"] += 1
                buf = w_stage[k % 2]
                sem = "wn" if k % 2 == 0 else "wm"
                if k >= 2:
                    b.w("g", "t", tst["tr"][k - 2])
                gv = b.dma("g", buf.rearrange("p (o i) -> p o i", o=4),
                           io[wnm][half * 512:half * 512 + 512, :].rearrange(
                               "(o p) i -> p o i", p=128), sem=sem)
                for tile in range(k * 4, k * 4 + 4):
                    emit_b2mix(tile)
                b.w("t", sem, gv)
                for ib in range(IC):
                    bk = R0 + (ib % 2) * 512
                    if len(tst["ev"]) >= 2:
                        b.w("t", "v", tst["ev"][-2])
                    mms = []
                    for ob4 in range(4):
                        diag = addI and (ob4 == ib - half * 4)
                        mms.append((ob4, False, not diag))
                        if diag:
                            mms.append((ob4, True, True))
                    for mi, (ob4, ident, stop) in enumerate(mms):
                        lastmm = mi == len(mms) - 1
                        lhs = (self.cb["i128b"][:, :] if ident else
                               buf[:, ob4 * 1024 + ib * 128:
                                   ob4 * 1024 + ib * 128 + 128])
                        b.op("t", lambda t, lhs=lhs, ob4=ob4, ident=ident,
                             stop=stop, bk=bk: t.matmul(
                                 ps[:, bk + ob4 * 128:bk + ob4 * 128 + 128],
                                 lhs, self.cb["i128b"][:, :],
                                 start=not ident, stop=stop),
                             inc="t" if lastmm else None)
                    b.w("v", "t", b.c["t"])
                    b.op("v", lambda v, ib=ib, base=base, half=half, bk=bk:
                         v.tensor_copy(
                             self.big0[:, base + ib * 1024 + half * 512:
                                       base + ib * 1024 + half * 512 + 512],
                             ps[:, bk:bk + 512]), inc="v")
                    tst["ev"].append(b.c["v"])
                tst["tr"][k] = b.c["t"]
        wT_v = b.c["v"]
        silu_s = b.c["s"]
        if "gbuf" in TAPS:
            b.w("q", "s", silu_s)
            b.dma("q", self.tap_t["gbuf"], self.gbuf[:, :], sem="tap")

        qv = {}
        for cc in range(IC):
            pbase = (cc % 2) * 2048
            if cc >= 2:
                b.w("t", "s", qv[cc - 2])
            if cc == 0:
                b.w("t", "v", wT_v)
                b.w("t", "v", bm_v[NT - 1])
            for i in range(IC):
                for tg in range(4):
                    b.op("t", lambda t, cc=cc, i=i, tg=tg, pbase=pbase: t.matmul(
                        ps[:, pbase + tg * 512:pbase + tg * 512 + 512],
                        self.big0[:, i * 1024 + cc * 128:i * 1024 + cc * 128 + 128],
                        self.xnb[:, i * 2048 + tg * 512:i * 2048 + tg * 512 + 512],
                        start=(i == 0), stop=False))
            for tg in range(4):
                rr = "qa" if cc < 4 else "qb"
                c0 = (cc % 4) * 128
                rp = ROWSA[rr][0]
                b.op("t", lambda t, tg=tg, rr=rr, c0=c0, rp=rp, pbase=pbase: t.matmul(
                    ps[:, pbase + tg * 512:pbase + tg * 512 + 512],
                    self.rrow(rr, c0, 128), self.cb["ones_b"][rp:rp + 1, :],
                    start=False, stop=True), inc="t" if tg == 3 else None)
            b.w("s", "t", b.c["t"])
            b.op("s", lambda s, cc=cc, pbase=pbase: s.copy(
                self.big0[:, 16384 + cc * 2048:16384 + cc * 2048 + 2048],
                ps[:, pbase:pbase + 2048]), inc="s")
            qv[cc] = b.c["s"]
        q_done = b.c["s"]
        if "qb" in TAPS:
            b.w("q", "s", q_done)
            b.dma("q", self.tap_t["qb"], self.big0[:, 16384:32768], sem="tap")
        if self.stop_after == "Q":
            self.finish(b)
            return

        b.w("t", "s", q_done)
        b.w("t", "cr", rb_done[0])
        S1 = self.xa[:, 0:512]
        S2 = self.xa[:, 512:1024]

        def eslot(par, i):
            p = par * 32
            return [self.xf[p:p + 1, 0:512], self.xf[p:p + 1, 512:1024],
                    self.ob2[p:p + 1, 0:512], self.ob2[p:p + 1, 512:1024],
                    self.xa[p:p + 1, 1536:2048], self.mfin[p:p + 1, 0:512]][i]
        gr_t = {}
        gr_mm = {}
        for r8 in range(8):
            gi, par = r8 // 2, r8 % 2
            gcol = (18 + gi) if par == 0 else (22 + gi)
            pslot = R0 + (r8 % 2) * 512
            if r8 == 0:
                b.w("s", "v", bm_v[NT - 1])
            else:
                b.w("s", "t", gr_mm[r8 - 1])
            b.op("s", lambda s, gcol=gcol: s.copy(
                self.geb[0:8, 0:1], self.smal[0:8, gcol:gcol + 1]), inc="s")
            b.w("t", "s", b.c["s"])
            if r8 >= 2:
                b.w("t", "v", gr_t[r8 - 2])
            b.op("t", lambda t, pslot=pslot: t.matmul(
                ps[0:1, pslot:pslot + 512],
                self.geb[0:8, 0:1], self.cb["sel8"][:, :],
                start=True, stop=True), inc="t")
            gr_mm[r8] = b.c["t"]
            b.w("v", "t", b.c["t"])
            if r8 == 0:
                b.w("v", "cr", rb_done[0])
            b.op("v", lambda v, par=par, gi=gi, pslot=pslot: v.tensor_copy(
                eslot(par, gi), ps[0:1, pslot:pslot + 512]), inc="v")
            gr_t[r8] = b.c["v"]
        if self.stop_after == "M1":
            self.finish(b)
            return
        for vi in range(2):
            for par in range(2):
                dst = eslot(par, 4 + vi)
                for hlf in range(2):
                    srow = (vi * 2 + hlf) * 32
                    b.op("v", lambda v, dst=dst, srow=srow, par=par, hlf=hlf:
                         v.tensor_copy(
                             dst[:, hlf * 256:hlf * 256 + 256]
                             .rearrange("p (a b) -> p a b", b=64),
                             self.xa[srow:srow + 1, 1024:1536]
                             .rearrange("p (a t b) -> p a t b", t=2, b=64)
                             [:, :, par, :]),
                         inc="v" if (vi, par, hlf) == (1, 1, 1) else None)
        vpar_v = b.c["v"]
        if self.stop_after == "M2":
            self.finish(b)
            return
        b.op("v", lambda v: v.tensor_tensor(self.csx[:, :], S1, S2, ALU.mult),
             inc="v")
        s12_v = b.c["v"]
        b.w("t", "v", s12_v)
        for par in range(2):
            ev = self.cb["etop"] if par == 0 else self.cb["ebot"]
            b.op("t", lambda t, par=par, ev=ev: t.matmul(
                ps[0:1, R0 + par * 512:R0 + par * 512 + 512], ev[:, :],
                self.csx[:, :], start=True, stop=True),
                inc="t" if par == 1 else None)
        y2s_t = b.c["t"]
        b.w("v", "t", y2s_t)
        for par in range(2):
            b.op("v", lambda v, par=par: v.tensor_tensor(
                ps[0:1, R0 + par * 512:R0 + par * 512 + 512],
                ps[0:1, R0 + par * 512:R0 + par * 512 + 512],
                eslot(par, 1), ALU.mult))
        b.op("v", lambda v: v.tensor_scalar(ps[0:1, R0:R0 + 1024],
                                            ps[0:1, R0:R0 + 1024],
                                            INVN2, None, ALU.mult), inc="v")
        y2_v = b.c["v"]
        b.w("s", "v", y2_v)
        b.op("s", lambda s: s.copy(self.sgs[0:1, 0:512], ps[0:1, R0:R0 + 512]))
        b.op("s", lambda s: s.copy(self.sgs[64:65, 0:512], ps[0:1, R0 + 512:R0 + 1024]),
             inc="s")
        y2_s = b.c["s"]
        if self.stop_after == "M3":
            self.finish(b)
            return
        if "y2r" in TAPS:
            b.w("q", "v", y2_v)
            b.dma("q", self.tap_t["y2r"], ps[0:1, R0:R0 + 1024], sem="tap")
        for par in range(2):
            b.op("v", lambda v, par=par: v.tensor_tensor(
                eslot(par, 0), eslot(par, 0), eslot(par, 5), ALU.mult))
            b.op("v", lambda v, par=par: v.tensor_tensor(
                eslot(par, 2), eslot(par, 2), eslot(par, 4), ALU.mult))
            for idx in (0, 2, 3):
                b.op("v", lambda v, par=par, idx=idx: v.tensor_scalar(
                    eslot(par, idx), eslot(par, idx), INVN2, None, ALU.mult),
                    inc="v" if (par == 1 and idx == 3) else None)
        csr_v = b.c["v"]
        if self.stop_after == "M4":
            self.finish(b)
            return
        b.w("t", "v", csr_v)
        b.w("t", "s", y2_s)
        for term, sidx2 in ((0, 0), (1, 2), (2, 3)):
            for par in range(2):
                p0 = par * 32
                b.op("t", lambda t, term=term, sidx2=sidx2, par=par, p0=p0: t.matmul(
                    ps[par * 64:par * 64 + 64,
                       R1 + term * 512:R1 + term * 512 + 512],
                    self.cb["ones_f"][p0:p0 + 1, 0:64],
                    eslot(par, sidx2), start=True, stop=True),
                    inc="t" if par == 1 else None)
        cs_t = b.c["t"]
        if self.stop_after == "M4a":
            self.finish(b)
            return
        b.w("v", "t", cs_t)
        b.op("v", lambda v: v.tensor_tensor(self.mfin[:, :], S1,
                                            ps[:, R1:R1 + 512], ALU.mult))
        b.op("v", lambda v: v.tensor_tensor(self.csx[:, :], S2,
                                            ps[:, R1 + 512:R1 + 1024], ALU.mult))
        b.op("v", lambda v: v.tensor_tensor(self.mfin[:, :], self.mfin[:, :],
                                            self.csx[:, :], ALU.add))
        b.op("v", lambda v: v.tensor_tensor(self.csx[:, :], S1, S2, ALU.mult))
        b.op("v", lambda v: v.tensor_tensor(self.csx[:, :], self.csx[:, :],
                                            ps[:, R1 + 1024:R1 + 1536], ALU.mult))
        b.op("v", lambda v: v.tensor_tensor(self.mfin[:, :], self.mfin[:, :],
                                            self.csx[:, :], ALU.add), inc="v")
        mf_v = b.c["v"]
        if self.stop_after == "M4b":
            self.finish(b)
            return
        b.w("s", "v", mf_v)
        b.op("s", lambda s: s.copy(self.mbuf[:, :], self.mfin[:, :]), inc="s")
        m_s = b.c["s"]
        if "mm" in TAPS:
            b.w("q", "v", mf_v)
            b.dma("q", self.tap_t["mm"], self.mfin[:, :], sem="tap")
        if self.stop_after == "M":
            self.finish(b)
            return

        ynat_s = {}
        for j in range(IC):
            pbase = (j % 2) * 2048
            if j >= 2:
                b.w("t", "s", ynat_s[j - 2])
            if j == 0:
                b.w("t", "s", m_s)
                b.w("t", "s", y2_s)
            for hh in range(2):
                r0 = hh * 64
                for tg in range(4):
                    b.op("t", lambda t, j=j, r0=r0, tg=tg, pbase=pbase: t.matmul(
                        ps[r0:r0 + 64, pbase + tg * 512:pbase + tg * 512 + 512],
                        self.mbuf[r0:r0 + 64, j * 64:j * 64 + 64],
                        self.big0[r0:r0 + 64,
                                  16384 + j * 2048 + tg * 512:
                                  16384 + j * 2048 + tg * 512 + 512],
                        start=True, stop=False))
                for tg in range(4):
                    b.op("t", lambda t, j=j, r0=r0, tg=tg, pbase=pbase: t.matmul(
                        ps[r0:r0 + 64, pbase + tg * 512:pbase + tg * 512 + 512],
                        self.sgs[r0:r0 + 1, j * 64:j * 64 + 64],
                        self.cb["ones_b"][r0:r0 + 1, :], start=False, stop=True),
                        inc="t" if tg == 3 else None)
            b.w("s", "t", b.c["t"])
            b.op("s", lambda s, j=j, pbase=pbase: s.copy(
                self.big1[:, j * 2048:j * 2048 + 2048],
                ps[:, pbase:pbase + 2048]), inc="s")
            ynat_s[j] = b.c["s"]
        ynat_done = b.c["s"]
        if "ynat" in TAPS:
            b.w("q", "s", ynat_done)
            b.dma("q", self.tap_t["ynat"], self.big1[:, :], sem="tap")
        if self.stop_after == "Y":
            self.finish(b)
            return

        def sqsl(cc):
            return self.wsl[:, (cc % 3) * 2048:(cc % 3) * 2048 + 2048]

        def stat_sl(k):
            return ps[(k % 3) * 32:(k % 3) * 32 + 1,
                      R0 + (k // 3) * 512:R0 + (k // 3) * 512 + 512]
        sq_v = {}
        sq_eng = {}
        ln_cc_t = {}
        for cc in range(IC):
            eng = "v" if cc % 2 == 0 else "s"
            if eng == "v":
                b.w("v", "s", ynat_s[cc])
            if cc >= 3:
                b.w(eng, "t", ln_cc_t[cc - 3])
            if eng == "v":
                b.op("v", lambda v, cc=cc: v.tensor_tensor(
                    sqsl(cc), self.big1[:, cc * 2048:cc * 2048 + 2048],
                    self.big1[:, cc * 2048:cc * 2048 + 2048], ALU.mult),
                    inc="v")
                sq_v[cc] = b.c["v"]
            else:
                b.op("s", lambda s, cc=cc: s.activation(
                    sqsl(cc), self.big1[:, cc * 2048:cc * 2048 + 2048],
                    AF.Square), inc="s")
                sq_v[cc] = b.c["s"]
            sq_eng[cc] = eng
            if cc == 0:
                b.w("t", "s", ynat_done)
            b.w("t", eng, sq_v[cc])
            for gg in range(4):
                for which in range(2):
                    k = gg + 4 * which
                    src2 = (self.big1[:, cc * 2048 + gg * 512:cc * 2048 + gg * 512 + 512]
                            if which == 0 else
                            sqsl(cc)[:, gg * 512:gg * 512 + 512])
                    b.op("t", lambda t, k=k, src2=src2, cc=cc: t.matmul(
                        stat_sl(k), self.cb["onescol_b"][:, :], src2,
                        start=(cc == 0), stop=(cc == IC - 1)),
                        inc="t" if (gg == 3 and which == 1) else None)
            ln_cc_t[cc] = b.c["t"]
        lnsum_t = b.c["t"]
        b.w("v", "t", lnsum_t)

        def syr(k):
            return self.xa[0:1, k * 512:k * 512 + 512]

        def sy2r(k):
            t2 = [self.xf, self.xf, self.ob2, self.ob2][k]
            c = [0, 512, 0, 512][k]
            return t2[0:1, c:c + 512]
        musq = self.mfin[0:1, 0:512]
        for k in range(8):
            dst = syr(k) if k < 4 else sy2r(k - 4)
            b.op("v", lambda v, k=k, dst=dst: v.tensor_copy(dst, stat_sl(k)),
                 inc="v" if k == 7 else None)
        for k in range(4):
            b.op("v", lambda v, k=k: v.tensor_scalar(syr(k), syr(k), 1.0 / C,
                                                     None, ALU.mult))
            b.op("v", lambda v, k=k: v.tensor_scalar(sy2r(k), sy2r(k), 1.0 / C,
                                                     None, ALU.mult))
            b.op("v", lambda v, k=k: v.tensor_tensor(musq, syr(k), syr(k),
                                                     ALU.mult))
            b.op("v", lambda v, k=k: v.tensor_tensor(sy2r(k), sy2r(k), musq,
                                                     ALU.subtract))
            b.op("v", lambda v, k=k: v.tensor_scalar(sy2r(k), sy2r(k), 1.0, 1e-5,
                                                     ALU.mult, ALU.add),
                 inc="v" if k == 3 else None)
        b.w("s", "v", b.c["v"])
        for k in range(4):
            b.op("s", lambda s, k=k: s.sqrt(sy2r(k), sy2r(k)),
                 inc="s" if k == 3 else None)
        b.w("v", "s", b.c["s"])
        for k in range(4):
            b.op("v", lambda v, k=k: v.reciprocal(sy2r(k), sy2r(k)))
            b.op("v", lambda v, k=k: v.tensor_tensor(syr(k), syr(k), sy2r(k),
                                                     ALU.mult))
            b.op("v", lambda v, k=k: v.tensor_scalar(syr(k), syr(k), -1.0, None,
                                                     ALU.mult),
                 inc="v" if k == 3 else None)
        stat_v = b.c["v"]
        b.w("t", "v", stat_v)
        bc_slist = []
        for k in range(8):
            which, gg = k // 4, k % 4
            row = sy2r(gg) if which == 0 else syr(gg)
            slot = R1 + (k % 2) * 512
            if k >= 2:
                b.w("t", "s", bc_slist[k - 2])
            b.op("t", lambda t, row=row, slot=slot: t.matmul(
                ps[:, slot:slot + 512], self.cb["ones_f"][0:1, :],
                row, start=True, stop=True), inc="t")
            b.w("s", "t", b.c["t"])
            b.op("s", lambda s, which=which, gg=gg, slot=slot: s.copy(
                self.big0[:, 16384 + which * 2048 + gg * 512:
                          16384 + which * 2048 + gg * 512 + 512],
                ps[:, slot:slot + 512]), inc="s")
            bc_slist.append(b.c["s"])
        bc_s = b.c["s"]
        for cc in range(IC):
            if cc == 0:
                b.w("v", "s", bc_s)
            for gg in range(4):
                slc = slice(cc * 2048 + gg * 512, cc * 2048 + gg * 512 + 512)
                abc = self.big0[:, 16384 + gg * 512:16384 + gg * 512 + 512]
                bbc = self.big0[:, 18432 + gg * 512:18432 + gg * 512 + 512]
                b.op("v", lambda v, slc=slc, abc=abc: v.tensor_tensor(
                    self.big1[:, slc], self.big1[:, slc], abc, ALU.mult))
                b.op("v", lambda v, slc=slc, bbc=bbc: v.tensor_tensor(
                    self.big1[:, slc], self.big1[:, slc], bbc, ALU.add),
                    inc="v" if gg == 3 else None)
            b.w("s", "v", b.c["v"])
            b.op("s", lambda s, cc=cc: s.activation(
                self.big1[:, cc * 2048:cc * 2048 + 2048],
                self.big1[:, cc * 2048:cc * 2048 + 2048], AF.Identity,
                bias=self.smal[:, 8 + cc:9 + cc], scale=self.smal[:, cc:cc + 1]),
                inc="s")
        ln_done2 = b.c["s"]
        if "ln" in TAPS:
            b.w("q", "s", ln_done2)
            b.dma("q", self.tap_t["ln"], self.big1[:, :], sem="tap")
        if self.stop_after == "LN":
            self.finish(b)
            return

        sfm_s = {}
        for oc in range(IC):
            pbase = (oc % 2) * 2048
            if oc >= 2:
                b.w("t", "s", sfm_s[oc - 2])
            if oc == 0:
                b.w("t", "s", ln_done2)
            for i in range(IC):
                for tg in range(4):
                    b.op("t", lambda t, oc=oc, i=i, tg=tg, pbase=pbase: t.matmul(
                        ps[:, pbase + tg * 512:pbase + tg * 512 + 512],
                        self.big0[:, 8192 + i * 1024 + oc * 128:
                                  8192 + i * 1024 + oc * 128 + 128],
                        self.big1[:, i * 2048 + tg * 512:i * 2048 + tg * 512 + 512],
                        start=(i == 0), stop=False))
            for tg in range(4):
                rr = "oa" if oc < 4 else "ob"
                c0 = (oc % 4) * 128
                rp = ROWSA[rr][0]
                b.op("t", lambda t, tg=tg, rr=rr, c0=c0, rp=rp, pbase=pbase: t.matmul(
                    ps[:, pbase + tg * 512:pbase + tg * 512 + 512],
                    self.rrow(rr, c0, 128), self.cb["ones_b"][rp:rp + 1, :],
                    start=False, stop=True), inc="t" if tg == 3 else None)
            b.w("s", "t", b.c["t"])
            b.op("s", lambda s, oc=oc, pbase=pbase: s.copy(
                self.xnb[:, oc * 2048:oc * 2048 + 2048], ps[:, pbase:pbase + 2048]),
                inc="s")
            sfm_s[oc] = b.c["s"]
        sfm_done = b.c["s"]
        if "sfm" in TAPS:
            b.w("q", "s", sfm_done)
            b.dma("q", self.tap_t["sfm"], self.xnb[:, :], sem="tap")
        if self.stop_after == "SFM":
            self.finish(b)
            return

        stm_s = {}
        for T in range(NT):
            pb = R0 + (T % 2) * 1024
            if T >= 2:
                b.w("t", "s", stm_s[T - 2])
            if T == 0:
                b.w("t", "s", sfm_done)
            for j in range(IC):
                b.op("t", lambda t, T=T, j=j, pb=pb: t.matmul(
                    ps[:, pb + j * 128:pb + j * 128 + 128],
                    self.xnb[:, j * 2048 + T * 128:j * 2048 + T * 128 + 128],
                    self.cb["i128b"][:, :], start=True, stop=True),
                    inc="t" if j == IC - 1 else None)
            b.w("s", "t", b.c["t"])
            b.op("s", lambda s, T=T, pb=pb: s.copy(
                self.big1[:, T * 1024:T * 1024 + 1024], ps[:, pb:pb + 1024]),
                inc="s")
            stm_s[T] = b.c["s"]
        if "stm" in TAPS:
            b.w("q", "s", b.c["s"])
            b.dma("q", self.tap_t["stm"], self.big1[:, :], sem="tap")
        if self.stop_after == "STM":
            self.finish(b)
            return

        fin_v = {}
        od = {}
        for T in range(NTL):
            sl = (T % 2) * 1024
            if T >= 2:
                b.w("q", "v", fin_v[T - 2])
            elif T == 0:
                b.w("q", "s", bc_s)
            xd = b.dma("q", self.xa[:, sl:sl + 1024],
                       io["x"][T * 128:(T + 1) * 128, :],
                       sem="xa" if T % 2 == 0 else "xb")
            ob = self.xf[:, 0:1024] if T % 2 == 0 else self.ob2[:, 0:1024]
            b.w("v", "s", stm_s[T])
            if T == 0:
                b.w("v", "s", silu_s)
            b.w("v", "xa" if T % 2 == 0 else "xb", xd)
            if T >= 2:
                b.w("v", "oa" if T % 2 == 0 else "ob", od[T - 2])
            b.op("v", lambda v, T=T: v.tensor_tensor(
                self.sgs[:, :], self.big1[:, T * 1024:T * 1024 + 1024],
                self.gbuf[:, T * 1024:T * 1024 + 1024], ALU.mult))
            b.op("v", lambda v, T=T, sl=sl, ob=ob: v.tensor_tensor(
                ob, self.xa[:, sl:sl + 1024], self.sgs[:, :], ALU.add), inc="v")
            fin_v[T] = b.c["v"]
            b.w("q", "v", fin_v[T])
            od[T] = b.dma("q", io["out"][T * 128:(T + 1) * 128, :], ob,
                          sem="oa" if T % 2 == 0 else "ob")
        self.finish(b)

    def finish(self, b):
        if self.stop_after:
            b.w("q", "v", b.c["v"])
            b.w("q", "s", b.c["s"])
            b.w("q", "t", b.c["t"])
            b.dma("q", self.io["out"][0:128, :], self.xa[:, 0:1024], sem="oa")
        b.w("q", "oa", b.c["oa"])
        b.w("q", "ob", b.c["ob"])
        b.w("q", "tap", b.c["tap"])
        b.w("q", "pa", b.c["pa"])
        b.w("q", "pb", b.c["pb"])
        b.w("q", "pc", b.c["pc"])
        b.w("q", "wm", b.c["wm"])
        b.w("q", "w1", b.c["w1"])
        b.w("q", "pd", b.c["pd"])
        b.w("q", "pe", b.c["pe"])

    def build(self):
        nc = self.nc
        import contextlib
        names = ["t", "v", "s", "cc",
                 "qi", "gi", "wn", "xa", "xb", "w1", "sa", "sb",
                 "cw", "cr", "oa", "ob", "tap",
                 "pa", "pb", "pc", "wm", "pd", "pe"]
        with contextlib.ExitStack() as stk:
            sems = {n: stk.enter_context(nc.semaphore("sem_" + n)) for n in names}
            block = stk.enter_context(nc.Block())

            @block.sync
            def _(e):
                self.walk(Wk("q", e, sems))

            @block.gpsimd
            def _(e):
                self.walk(Wk("g", e, sems))

            @block.tensor
            def _(e):
                self.walk(Wk("t", e, sems))

            @block.vector
            def _(e):
                self.walk(Wk("v", e, sems))

            @block.scalar
            def _(e):
                self.walk(Wk("s", e, sems))
        return nc


def build_program(taps=(), stop_after=None, nt=NT, init_only=False, a_stage=4,
                  dbg=()):
    return MK(taps=taps, stop_after=stop_after, nt=nt, init_only=init_only,
              a_stage=a_stage, dbg=dbg).build()


def make_in_maps(inputs):
    x = inputs["L0_attn_x"]
    maps = []
    for c in range(8):
        bsel = c // 4
        t0 = (c % 4) * TC
        m = {
            "x": x[bsel, t0:t0 + TC, :],
            "wq": inputs["L0_attn_Wq"], "wk1": inputs["L0_attn_Wk1"],
            "wk2": inputs["L0_attn_Wk2"], "wv1": inputs["L0_attn_Wv1"],
            "wv2": inputs["L0_attn_Wv2"], "wo": inputs["L0_attn_Wo"],
            "biases": np.stack([
                inputs["L0_attn_bq"], inputs["L0_attn_bk1"],
                inputs["L0_attn_bk2"], inputs["L0_attn_bv1"],
                inputs["L0_attn_bv2"], inputs["L0_attn_bo"]]),
            "gvec": np.stack([
                inputs["L0_attn_g1"], inputs["L0_attn_g2"],
                inputs["L0_attn_g3"], inputs["L0_attn_g4"]]),
            "lng": inputs["L0_attn_ln_g"], "lnb": inputs["L0_attn_ln_b"],
            "gw": inputs["L0_ffn_gate_w"], "gb": inputs["L0_ffn_gate_b"],
            "eb": inputs["L0_ffn_expert_bias"],
            "w1": inputs["L0_ffn_w1"], "b1": inputs["L0_ffn_b1"],
            "w2": inputs["L0_ffn_w2"], "b2": inputs["L0_ffn_b2"],
        }
        maps.append({k: np.ascontiguousarray(v, dtype=np.float32)
                     for k, v in m.items()})
    return maps


def kernel(**inputs):
    nc = build_program()
    in_maps = make_in_maps(inputs)
    res = run_bass_kernel_spmd(nc, in_maps, core_ids=list(range(8)))
    out = np.empty((2, NFULL, C), dtype=np.float32)
    for c in range(8):
        out[c // 4, (c % 4) * TC:(c % 4 + 1) * TC, :] = res.results[c]["out"]
    return out


# version 10
# speedup vs baseline: 1.0539x; 1.0175x over previous
import numpy as np
import ml_dtypes
import concourse.bass as bass
import concourse.mybir as mybir
from concourse.bass_utils import run_bass_kernel_spmd

F32 = mybir.dt.float32
BF16 = mybir.dt.bfloat16
AF = mybir.ActivationFunctionType
ALU = mybir.AluOpType
AX = mybir.AxisListType

TC = 2048
NT = 16
C = 1024
IC = 8
H = 16
E = 8
FF = 4096
FC = 32
NFULL = 8192
RG = [[0, 1, 2, 3], [4, 5, 6, 7]]
BF = ml_dtypes.bfloat16
INVN2 = 1.0 / (float(NFULL) * float(NFULL))

PS_S1 = 0
PS_S2 = 512
PS_MA = 1024
PS_MB = 1536
R0, R1, R2, R3 = 2048, 2560, 3072, 3584

ROWSA = {
    "k1a": (0, 0), "k1b": (32, 0), "k2a": (64, 0),
    "k2b": (0, 1), "v1a": (32, 1), "v1b": (64, 1),
    "v2a": (0, 2), "v2b": (32, 2), "qa": (64, 2),
    "qb": (0, 3), "oa": (32, 3), "ob": (64, 3),
}


class Wk:

    def __init__(self, cur, eng, sems):
        self.cur = cur
        self.e = eng
        self.sems = sems
        self.c = {k: 0 for k in sems}

    def dma(self, qn, out, in_, sem=None, slow=False):
        sem = sem or qn
        self.c[sem] += 16
        if self.cur == qn:
            self.e.dma_start(out=out, in_=in_,
                             allow_slow_non_contiguous=slow).then_inc(
                self.sems[sem], 16)
        return self.c[sem]

    def w(self, qn, sem, val):
        if self.cur == qn and val > 0:
            self.e.wait_ge(self.sems[sem], val)

    def op(self, qn, fn, inc=None):
        if inc:
            self.c[inc] += 1
        if self.cur == qn:
            ins = fn(self.e)
            if inc:
                ins.then_inc(self.sems[inc], 1)
        return self.c[inc] if inc else None


class MK:
    def __init__(self, taps=(), stop_after=None, nt=NT, init_only=False,
                 a_stage=4, dbg=()):
        self.taps = set(taps)
        self.stop_after = stop_after
        self.nt = nt
        self.init_only = init_only
        self.a_stage = a_stage
        self.dbg = set(dbg)
        nc = bass.Bass("TRN2", target_bir_lowering=False,
                       detect_race_conditions=False)
        self.nc = nc

        self.io = {}
        for nm, sh in (
            ("x", [TC, C]), ("wq", [C, C]), ("wk1", [C, C]), ("wk2", [C, C]),
            ("wv1", [C, C]), ("wv2", [C, C]), ("wo", [C, C]),
            ("biases", [6, C]), ("gvec", [4, H]), ("lng", [C]), ("lnb", [C]),
            ("gw", [E, C]), ("gb", [E]), ("eb", [E]),
            ("w1", [E, C, FF]), ("b1", [E, FF]), ("w2", [E, FF, C]), ("b2", [E, C]),
        ):
            self.io[nm] = nc.dram_tensor(nm, sh, F32, kind="ExternalInput").ap()
        self.io["out"] = nc.dram_tensor("out", [TC, C], F32, kind="ExternalOutput").ap()
        self.io["cc_in"] = nc.dram_tensor("cc_in", [128, 1536], F32, kind="Internal").ap()
        self.io["cc_out"] = nc.dram_tensor("cc_out", [128, 1536], F32, kind="Internal").ap()
        self.w2b = nc.dram_tensor("w2b", [E, 16, 128, 2048], BF16, kind="Internal").ap()
        self.w1b = nc.dram_tensor("w1b", [E, 128, 32768], BF16, kind="Internal").ap()

        self.tap_t = {}
        for nm, sh, dt in (
            ("xnb", [128, IC * TC], BF16), ("kv0", [128, 4096], BF16),
            ("scc", [128, 1536], F32), ("rcc", [128, 1536], F32),
            ("onehot", [128, NT * 8], F32), ("gbuf", [128, NT * C], BF16),
            ("qb", [128, 16384], BF16), ("mm", [128, 512], F32),
            ("y2r", [2, 512], F32), ("ynat", [128, IC * TC], BF16),
            ("ln", [128, IC * TC], BF16), ("sfm", [128, IC * TC], BF16),
            ("stm", [128, NT * C], BF16),
        ):
            if nm in self.taps:
                self.tap_t[nm] = nc.dram_tensor("tap_" + nm, sh, dt,
                                                kind="ExternalOutput").ap()

        def inl(name, arr):
            return nc.inline_tensor(arr, name).ap()
        et = np.zeros((128, 1), np.float32); et[:64] = 1.0
        ebv = np.zeros((128, 1), np.float32); ebv[64:] = 1.0
        st = np.zeros((8, 512), BF)
        for qq in range(512):
            st[qq // 64, qq] = 1.0
        self.consts = {
            "i128b": np.eye(128, dtype=BF), "i128f": np.eye(128, dtype=np.float32),
            "i8f": np.eye(8, dtype=np.float32),
            "ones_b": np.ones((128, 512), dtype=BF),
            "ones_f": np.ones((128, 128), np.float32),
            "onescol_b": np.ones((128, 1), dtype=BF),
            "etop": et, "ebot": ebv,
            "sel8": st,
        }
        self.cin = {k: inl(k, v) for k, v in self.consts.items()}

        A = nc.alloc_sbuf_tensor
        self.xnb = A("xnb", [128, IC * TC], BF16)
        self.gbuf = A("gbuf", [128, NT * C], BF16)
        self.big0 = A("big0", [128, 32768], BF16)
        self.big1 = A("big1", [128, 16384], BF16)
        self.wsl = A("wsl", [128, 6144], BF16)
        self.xa = A("xa", [128, 2048], F32)
        self.xf = A("xf", [128, 1024], F32)
        self.ob2 = A("ob2", [128, 1024], F32)
        self.ohm = A("ohm", [128, NT * 8], F32)
        self.sgs = A("sgs", [128, 1024], BF16)
        self.mfin = A("mfin", [128, 512], F32)
        self.mbuf = A("mbuf", [128, 512], BF16)
        self.csx = A("csx", [128, 512], F32)
        self.smal = A("smal", [128, 40], F32)
        self.rowsA = A("rowsA", [128, 2048], BF16)
        self.b1c = A("b1c", [128, 256], F32)
        self.b2n = A("b2n", [8, 1024], BF16)
        self.ge = A("ge", [8, 128], F32)
        self.geb = A("geb", [8, 128], BF16)
        self.gwT = A("gwT", [128, 64], F32)
        self.cb = {}
        for k, v in self.consts.items():
            self.cb[k] = A("c_" + k, list(v.shape),
                           BF16 if v.dtype == BF else F32)
        self.ps = nc.alloc_psum_tensor("PALL", [128, 4096], F32)

    def rrow(self, name, c0=0, n=512):
        p, g = ROWSA[name]
        return self.rowsA[p:p + 1, g * 512 + c0:g * 512 + c0 + n]

    def walk(self, b):
        io, ps = self.io, self.ps
        TAPS = self.taps

        for k in self.consts:
            b.dma("q", self.cb[k][:, :], self.cin[k], sem="qi")
        for i, pfx in enumerate(("q", "k1", "k2", "v1", "v2", "o")):
            for half, suf in ((0, "a"), (1, "b")):
                p, g = ROWSA[pfx + suf]
                b.dma("g", self.rowsA[p:p + 1, g * 512:g * 512 + 512],
                      io["biases"][i:i + 1, half * 512:half * 512 + 512], sem="gi")
        b.dma("g", self.b2n[:, :], io["b2"], sem="gi")
        b.dma("q", self.smal[:, 0:8], io["lng"].rearrange("(c p) -> p c", p=128),
              sem="qi", slow=True)
        b.dma("q", self.smal[:, 8:16], io["lnb"].rearrange("(c p) -> p c", p=128),
              sem="qi", slow=True)
        b.dma("q", self.smal[0:8, 16:17], io["gb"].rearrange("(a e) -> e a", a=1),
              sem="qi", slow=True)
        b.dma("q", self.smal[0:8, 17:18], io["eb"].rearrange("(a e) -> e a", a=1),
              sem="qi", slow=True)
        b.dma("q", self.smal[0:8, 18:26].rearrange("p (t g) -> p t g", t=2),
              io["gvec"].rearrange("g (h t) -> h t g", t=2), sem="qi", slow=True)
        b.dma("q", self.b1c[:, :], io["b1"].rearrange("e (c p) -> p (e c)", p=128),
              sem="qi", slow=True)
        qinit = b.dma("q", self.gwT[:, :],
                      io["gw"].rearrange("e (c p) -> p (e c)", p=128), sem="qi", slow=True)

        qi_all = b.c["qi"]
        gi_all = b.c["gi"]
        b.op("v", lambda v: v.memset(self.smal[:, 38:39], 1e-5), inc="v")
        eps_v = b.c["v"]

        wst = {"k": 0, "tr": {}, "ev": []}
        w_stage = (self.wsl[:, 0:4096], self.big1[:, 10240:14336])

        def wT_build(wnm, dst, cstride, first, addI=False):
            for half in range(2):
                k = wst["k"]
                wst["k"] += 1
                buf = w_stage[k % 2]
                sem = "wn" if k % 2 == 0 else "wm"
                if k >= 2:
                    b.w("g", "t", wst["tr"][k - 2])
                gv = b.dma("g", buf.rearrange("p (o i) -> p o i", o=4),
                           io[wnm][half * 512:half * 512 + 512, :].rearrange(
                               "(o p) i -> p o i", p=128), sem=sem)
                b.w("t", sem, gv)
                if first and half == 0:
                    b.w("t", "qi", qi_all)
                for ib in range(IC):
                    bk = R0 + (ib % 2) * 512
                    if len(wst["ev"]) >= 2:
                        b.w("t", "v", wst["ev"][-2])
                    mms = []
                    for ob4 in range(4):
                        diag = addI and (ob4 == ib - half * 4)
                        mms.append((ob4, False, not diag))
                        if diag:
                            mms.append((ob4, True, True))
                    for mi, (ob4, ident, stop) in enumerate(mms):
                        lastmm = mi == len(mms) - 1
                        lhs = (self.cb["i128b"][:, :] if ident else
                               buf[:, ob4 * 1024 + ib * 128:
                                   ob4 * 1024 + ib * 128 + 128])
                        b.op("t", lambda t, ob4=ob4, ident=ident, stop=stop,
                             lhs=lhs, bk=bk: t.matmul(
                                 ps[:, bk + ob4 * 128:bk + ob4 * 128 + 128],
                                 lhs, self.cb["i128b"][:, :],
                                 start=not ident, stop=stop),
                             inc="t" if lastmm else None)
                    b.w("v", "t", b.c["t"])
                    b.op("v", lambda v, ib=ib, half=half, bk=bk: v.tensor_copy(
                        self.big0[:, dst + ib * cstride + half * 512:
                                  dst + ib * cstride + half * 512 + 512],
                        ps[:, bk:bk + 512]), inc="v")
                    wst["ev"].append(b.c["v"])
                wst["tr"][k] = b.c["t"]

        pc_units = ([("w1", e0, i, hf) for e0 in range(2)
                     for i in range(IC) for hf in range(2)] +
                    [("w2", e, qd) for e in range(2) for qd in range(16)])
        pcv = {}
        PC_SEMS = ("pa", "pb", "pd", "pe")
        pc_prev = {s: None for s in PC_SEMS}

        def emit_precast_unit(kind, u_parity):
            sem = PC_SEMS[u_parity % 4]
            if pc_prev[sem] is not None:
                b.w("g", sem, pc_prev[sem])
            if kind[0] == "w1":
                _, e0, i, hf = kind
                d = b.dma("g",
                          self.w1b[e0][:, i * 4096 + hf * 2048:
                                       i * 4096 + hf * 2048 + 2048],
                          io["w1"][e0][i * 128:(i + 1) * 128,
                                       hf * 2048:(hf + 1) * 2048], sem=sem)
            else:
                _, e, qd = kind
                d = b.dma("g", self.w2b[e, qd].rearrange("p (c o) -> p c o", c=2),
                          io["w2"][e, qd * 256:(qd + 1) * 256, :].rearrange(
                              "(c p) o -> p c o", p=128), sem=sem)
            pcv[kind] = (sem, d)
            pc_prev[sem] = d

        def emit_precast(u):
            emit_precast_unit(pc_units[u], u % 4)

        for u in range(4):
            emit_precast(u)

        wT_build("wk1", 0, 2048, True, addI=True)
        wT_build("wk2", 1024, 2048, False, addI=True)
        wT_build("wv1", 16384, 2048, False, addI=True)
        wT_build("wv2", 16384 + 1024, 2048, False, addI=True)
        wkv_v = b.c["v"]
        if self.init_only:
            if "kv0" in self.taps:
                b.w("q", "v", wkv_v)
                b.dma("q", self.tap_t["kv0"], self.big0[:, 0:4096], sem="tap")
            self.finish(b)
            return
        NTL = self.nt

        xdma = {}
        ln_done = {}
        xnT_s = {}
        trv = {}
        gate_t = {}
        ge_s = {}
        maskv = {}
        zev = {}
        smm_t = {}
        xdma[0] = b.dma("q", self.xa[:, 0:1024], io["x"][0:128, :], sem="xa")
        xdma[1] = b.dma("q", self.xa[:, 1024:2048], io["x"][128:256, :], sem="xb")
        for T in range(NTL):
            sl = (T % 2) * 1024
            xt = self.xa[:, sl:sl + 1024]
            b.w("v", "xa" if T % 2 == 0 else "xb", xdma[T])
            if T == 0:
                b.w("s", "qi", qi_all)
            b.op("v", lambda v, xt=xt: v.reduce_sum(self.smal[:, 32:33], xt, AX.X),
                 inc="v")
            b.w("s", "v", b.c["v"])
            b.op("s", lambda s: s.mul(self.smal[:, 33:34], self.smal[:, 32:33],
                                      -1.0 / C), inc="s")
            b.w("v", "s", b.c["s"])
            b.op("v", lambda v, xt=xt: v.tensor_scalar(xt, xt, self.smal[:, 33:34],
                                                       None, ALU.add), inc="v")
            b.w("s", "v", b.c["v"])
            b.op("s", lambda s, xt=xt, T=T: s.activation(
                self.big1[:, 8192 + (T % 2) * 1024:8192 + (T % 2) * 1024 + 1024],
                xt, AF.Square, accum_out=self.smal[:, 33:34]))
            b.op("s", lambda s: s.activation(self.smal[:, 34:35],
                                             self.smal[:, 33:34], AF.Sqrt,
                                             bias=self.smal[:, 38:39],
                                             scale=1.0 / C), inc="s")
            b.w("v", "s", b.c["s"])
            b.op("v", lambda v: v.reciprocal(self.smal[:, 35:36], self.smal[:, 34:35]),
                 inc="v")
            b.w("s", "v", b.c["v"])
            b.op("s", lambda s, xt=xt: s.mul(xt, xt, self.smal[:, 35:36]), inc="s")
            ln_done[T] = b.c["s"]

            if self.a_stage < 1:
                xnT_s[T] = b.c["s"]
                maskv[T] = b.c["v"]
                gate_t[T] = b.c["t"]
                ge_s[T] = b.c["s"]
                zev[(T, 0, 0)] = zev[(T, 0, 1)] = b.c["s"]
                zev[(T, 1, 0)] = zev[(T, 1, 1)] = b.c["s"]
                smm_t[T] = b.c["t"]
                continue
            b.w("t", "s", ln_done[T])
            if T > 0:
                b.w("t", "s", zev[(T - 1, 1, 0)])
            for j in range(IC):
                b.op("t", lambda t, j=j, xt=xt: t.matmul(
                    ps[:, R0 + j * 128:R0 + j * 128 + 128],
                    xt[:, j * 128:j * 128 + 128], self.cb["i128f"][:, :],
                    start=True, stop=True), inc="t" if j == IC - 1 else None)
            trT = b.c["t"]
            trv[T] = trT
            b.w("v", "t", trT)
            if "noxf" in self.dbg:
                b.op("v", lambda v: v.engine_nop(), inc="v")
            else:
                b.op("v", lambda v: v.tensor_copy(self.xf[:, :], ps[:, R0:R0 + 1024]),
                     inc="v")
            xf_v = b.c["v"]
            b.w("s", "t", trT)
            b.w("s", "v", xf_v)
            if "noxnb" in self.dbg:
                b.op("s", lambda s: s.copy(self.xnb[:, 0:1024],
                                           ps[:, R0:R0 + 1024]), inc="s")
            else:
                b.op("s", lambda s, T=T: s.copy(
                    self.xnb[:, :].rearrange("p (c f) -> p c f", c=IC)[
                        :, :, T * 128:T * 128 + 128],
                    ps[:, R0:R0 + 1024].rearrange("p (c f) -> p c f", f=128)),
                    inc="s")
            xnT_s[T] = b.c["s"]

            if self.a_stage < 2:
                maskv[T] = b.c["v"]
                gate_t[T] = b.c["t"]
                ge_s[T] = b.c["s"]
                zev[(T, 0, 0)] = zev[(T, 0, 1)] = b.c["s"]
                zev[(T, 1, 0)] = zev[(T, 1, 1)] = b.c["s"]
                smm_t[T] = b.c["t"]
                continue
            b.w("t", "v", xf_v)
            if T > 0:
                b.w("t", "s", ge_s[T - 1])
                b.w("t", "s", zev[(T - 1, 1, 1)])
            for i in range(IC):
                b.op("t", lambda t, i=i: t.matmul(
                    ps[0:8, R2:R2 + 128],
                    self.gwT[:, :].rearrange("p (e c) -> p c e", c=IC)[:, i, :],
                    self.xf[:, i * 128:i * 128 + 128],
                    start=(i == 0), stop=(i == IC - 1)),
                    inc="t" if i == IC - 1 else None)
            gmm = b.c["t"]
            b.w("s", "t", gmm)
            b.op("s", lambda s: s.activation(self.ge[:, :], ps[0:8, R2:R2 + 128],
                                             AF.Identity, bias=self.smal[0:8, 16:17]))
            b.op("s", lambda s: s.add(self.ge[:, :], self.ge[:, :],
                                      self.smal[0:8, 17:18]), inc="s")
            ge_s[T] = b.c["s"]
            b.w("t", "s", ge_s[T])
            b.op("t", lambda t: t.matmul(ps[:, R2 + 128:R2 + 136], self.ge[:, :],
                                         self.cb["i8f"][:, :], start=True, stop=True),
                 inc="t")
            gate_t[T] = b.c["t"]
            b.w("v", "t", gate_t[T])
            b.op("v", lambda v: v.reduce_max(self.smal[:, 36:37],
                                             ps[:, R2 + 128:R2 + 136], AX.X),
                 inc="v")
            b.w("s", "v", b.c["v"])
            b.op("s", lambda s: s.copy(self.smal[:, 37:38], self.smal[:, 36:37]),
                 inc="s")
            b.w("v", "s", b.c["s"])
            b.op("v", lambda v, T=T: v.tensor_scalar(
                self.ohm[:, T * 8:T * 8 + 8], ps[:, R2 + 128:R2 + 136],
                self.smal[:, 37:38], None, ALU.is_ge), inc="v")
            maskv[T] = b.c["v"]

            if self.a_stage < 3:
                zev[(T, 0, 0)] = zev[(T, 0, 1)] = b.c["s"]
                zev[(T, 1, 0)] = zev[(T, 1, 1)] = b.c["s"]
                smm_t[T] = b.c["t"]
                continue
            kvs = (T % 2) * 4096
            for rnd in range(2):
                wbase = rnd * 16384
                rks = (("k1a", "k1b", "k2a", "k2b") if rnd == 0
                       else ("v1a", "v1b", "v2a", "v2b"))
                b.w("t", "s", xnT_s[T])
                if T == 0 and rnd == 0:
                    b.w("t", "v", wkv_v)
                    b.w("t", "gi", gi_all)
                if rnd == 0 and T > 0:
                    b.w("t", "s", zev[(T - 1, 1, 1)])
                zt_g = {}
                for g4 in range(4):
                    rot = R0 + g4 * 512
                    if rnd == 0 and g4 == 2:
                        b.w("t", "v", maskv[T])
                    if rnd == 1 and g4 % 2 == 0:
                        b.w("t", "s", zev[(T, 0, g4 // 2)])
                    for i in range(IC):
                        b.op("t", lambda t, i=i, g4=g4, rot=rot, wbase=wbase, T=T:
                             t.matmul(
                                 ps[:, rot:rot + 512],
                                 self.xnb[:, i * 2048 + T * 128:
                                          i * 2048 + T * 128 + 128],
                                 self.big0[:, wbase + i * 2048 + g4 * 512:
                                           wbase + i * 2048 + g4 * 512 + 512],
                                 start=(i == 0), stop=False))
                    rp = ROWSA[rks[g4]][0]
                    b.op("t", lambda t, rot=rot, rr=rks[g4], rp=rp: t.matmul(
                        ps[:, rot:rot + 512], self.cb["ones_b"][rp:rp + 1, 0:128],
                        self.rrow(rr), start=False, stop=True), inc="t")
                    zt_g[g4] = b.c["t"]
                for g2 in range(2):
                    rot = R0 + g2 * 1024
                    b.w("s", "t", zt_g[2 * g2 + 1])
                    b.op("s", lambda s, g2=g2, rot=rot, rnd=rnd, kvs=kvs: s.copy(
                        self.big1[:, kvs + rnd * 2048 + g2 * 1024:
                                  kvs + rnd * 2048 + g2 * 1024 + 1024],
                        ps[:, rot:rot + 1024]), inc="s")
                    zev[(T, rnd, g2)] = b.c["s"]

            if self.a_stage < 4:
                smm_t[T] = b.c["t"]
                if T + 2 < NTL:
                    b.w("q", "s", zev[(T, 1, 1)])
                    xdma[T + 2] = b.dma("q", self.xa[:, sl:sl + 1024],
                                        io["x"][(T + 2) * 128:(T + 3) * 128, :],
                                        sem="xa" if T % 2 == 0 else "xb")
                continue
            for sidx in range(2):
                b.w("t", "s", zev[(T, 1, sidx)])
                for h in range(H):
                    r0 = (h % 2) * 64
                    koff = kvs + sidx * 1024 + h * 64
                    voff = kvs + 2048 + sidx * 1024 + h * 64
                    cc0 = sidx * 512 + (h // 2) * 64
                    b.op("t", lambda t, koff=koff, voff=voff, r0=r0, cc0=cc0, T=T, h=h:
                         t.matmul(ps[r0:r0 + 64, cc0:cc0 + 64],
                                  self.big1[:, koff:koff + 64],
                                  self.big1[:, voff:voff + 64],
                                  start=(T == 0 and h < 2),
                                  stop=(T == NTL - 1 and h >= H - 2)))
            for mi in range(4):
                vb = kvs + 2048 + (mi // 2) * 1024 + (mi % 2) * 512
                mp = (mi % 3) * 32 if mi < 3 else 0
                mcol = PS_MA if mi < 3 else PS_MB
                b.op("t", lambda t, vb=vb, mp=mp, mcol=mcol, T=T, mi=mi: t.matmul(
                    ps[mp:mp + 1, mcol:mcol + 512], self.cb["onescol_b"][:, :],
                    self.big1[:, vb:vb + 512],
                    start=(T == 0), stop=(T == NTL - 1)),
                    inc="t" if mi == 3 else None)
            smm_t[T] = b.c["t"]
            if T + 2 < NTL:
                b.w("q", "t", trv[T])
                xdma[T + 2] = b.dma("q", self.xa[:, sl:sl + 1024],
                                    io["x"][(T + 2) * 128:(T + 3) * 128, :],
                                    sem="xa" if T % 2 == 0 else "xb")
            for u in range(max(4, T * 4), min((T + 1) * 4, len(pc_units))):
                emit_precast(u)
            if T == 10:
                for k in range(4):
                    for hf in range(2):
                        ps_, pv_ = pcv[("w1", 0, k, hf)]
                        b.w("g", ps_, pv_)
                for k in range(4):
                    gbuf_w1 = b.dma("g", self.gbuf[:, k * 4096:(k + 1) * 4096],
                                    self.w1b[0][:, k * 4096:(k + 1) * 4096],
                                    sem="w1")
        for u in range(max(4, NTL * 4), len(pc_units)):
            emit_precast(u)
        s_done_t = smm_t[NTL - 1]

        if self.a_stage < 4:
            if "xnb" in TAPS and self.a_stage >= 1:
                b.w("q", "s", xnT_s[NTL - 1])
                b.dma("q", self.tap_t["xnb"], self.xnb[:, :], sem="tap")
            if self.a_stage < 1:
                b.w("q", "s", ln_done[NTL - 1])
            if "onehot" in TAPS and self.a_stage >= 2:
                b.w("q", "v", maskv[NTL - 1])
                b.dma("q", self.tap_t["onehot"], self.ohm[:, :], sem="tap")
            if "kv0" in TAPS and self.a_stage >= 3:
                b.w("q", "s", zev[(max(0, NTL - 2), 1, 1)])
                b.dma("q", self.tap_t["kv0"], self.big1[:, 0:4096], sem="tap")
            self.finish(b)
            return
        b.w("v", "t", s_done_t)
        b.op("v", lambda v: v.tensor_copy(self.xa[:, 0:1024], ps[:, 0:1024]))
        b.op("v", lambda v: v.memset(self.xa[:, 1024:1536], 0.0))
        b.op("v", lambda v: v.tensor_copy(self.xa[0:1, 1024:1536], ps[0:1, PS_MA:PS_MA + 512]))
        b.op("v", lambda v: v.tensor_copy(self.xa[32:33, 1024:1536], ps[32:33, PS_MA:PS_MA + 512]))
        b.op("v", lambda v: v.tensor_copy(self.xa[64:65, 1024:1536], ps[64:65, PS_MA:PS_MA + 512]))
        b.op("v", lambda v: v.tensor_copy(self.xa[96:97, 1024:1536], ps[0:1, PS_MB:PS_MB + 512]),
             inc="v")
        ccv = b.c["v"]
        if "scc" in TAPS:
            b.w("q", "v", ccv)
            b.dma("q", self.tap_t["scc"], self.xa[:, 0:1536], sem="tap")
        b.w("g", "v", ccv)
        ccw = b.dma("g", io["cc_in"], self.xa[:, 0:1536], sem="cw")
        if self.stop_after != "A":
            b.w("g", "cw", ccw)
            b.op("g", lambda g: g.collective_compute(
                "AllReduce", ALU.add, replica_groups=RG,
                ins=[io["cc_in"]], outs=[io["cc_out"]]), inc="cc")

        if "xnb" in TAPS:
            b.w("q", "s", xnT_s[NTL - 1])
            b.dma("q", self.tap_t["xnb"], self.xnb[:, :], sem="tap")
        if "kv0" in TAPS:
            b.w("q", "s", zev[(max(0, NTL - 2), 1, 1)])
            b.dma("q", self.tap_t["kv0"], self.big1[:, 0:4096], sem="tap")
        if "onehot" in TAPS:
            b.w("q", "v", maskv[NTL - 1])
            b.dma("q", self.tap_t["onehot"], self.ohm[:, :], sem="tap")

        if self.stop_after == "A":
            self.finish(b)
            return

        b.w("g", "t", s_done_t)
        for i in range(4, IC):
            for hf in range(2):
                ps_, pv_ = pcv[("w1", 0, i, hf)]
                b.w("g", ps_, pv_)
        w1v = [None] * E
        for i in range(4, IC):
            w1hi = b.dma("g", self.big0[:, i * 4096:(i + 1) * 4096],
                         self.w1b[0][:, i * 4096:(i + 1) * 4096], sem="w1")
        b.w("g", "w1", w1hi)
        for i in range(4):
            w1v[0] = b.dma("g", self.big0[:, i * 4096:(i + 1) * 4096],
                           self.w1b[0][:, i * 4096:(i + 1) * 4096], sem="w1")
        slabv = {}
        quad_t = {}
        fq = {}
        slab_n = [0]
        rb_done = [0]
        SLAB_SEMS = ("sa", "sb", "pc", "wn", "wm", "cw")

        def emit_slab(e, ts, fcc, wait=None):
            gidx = slab_n[0]
            slab_n[0] += 1
            sidx = gidx % 6
            if wait is not None:
                b.w("q", "t", wait)
            psem, pval = pcv[("w2", e, fcc // 2)]
            b.w("q", psem, pval)
            slabv[(e, ts, fcc)] = (sidx, b.dma(
                "q", self.wsl[:, sidx * 1024:sidx * 1024 + 1024],
                self.w2b[e, fcc // 2][:, (fcc % 2) * 1024:
                                      (fcc % 2) * 1024 + 1024],
                sem=SLAB_SEMS[sidx]))

        h_ev = {}
        h_last_t = {}
        yev = {}
        yev_g0 = {}
        yev_h0 = {}

        for e in range(E):
            prev = (e - 1, 3) if e > 0 else None
            for ts in range(4):
                for f0 in range(6):
                    wait = None
                    if (e, ts) != (0, 0):
                        pe, pts = (e, ts - 1) if ts > 0 else (e - 1, 3)
                        wait = fq[(pe, pts, 26 + f0)]
                    emit_slab(e, ts, f0, wait)
                if e + 2 < E:
                    for uu in range(4):
                        idx = ts * 4 + uu
                        emit_precast_unit(("w1", e + 2, idx // 2, idx % 2), uu)
                        emit_precast_unit(("w2", e + 2, idx), uu + 1)
                if e == 0 and ts == 0:
                    b.w("g", "cc", 1)
                    rb_done[0] = b.dma("g", self.xa[:, 0:1536], io["cc_out"],
                                       sem="cr")
                    if "rcc" in TAPS:
                        b.w("q", "cr", rb_done[0])
                        b.dma("q", self.tap_t["rcc"], self.xa[:, 0:1536], sem="tap")
                t0 = ts * 512
                prev2 = (e, ts - 1) if ts > 0 else prev
                for fc in range(FC):
                    hb = (fc % 2) * 512
                    if fc == 0 and ts == 0:
                        b.w("t", "w1", w1hi if e == 0 else w1v[e])
                    elif e == 0 and ts == 1 and fc == 0:
                        b.w("t", "w1", w1v[0])
                    if fc >= 2:
                        b.w("t", "s", h_ev[(e, ts, fc - 2)])
                    elif prev2 is not None:
                        b.w("t", "v", (yev_h0 if fc == 0 else yev_g0)[prev2])
                    elif fc == 0:
                        b.w("t", "v", ccv)
                    w1lo = self.gbuf if (e == 0 and ts == 0) else self.big0
                    for i in range(IC):
                        src = w1lo if i < 4 else self.big0
                        b.op("t", lambda t, i=i, fc=fc, hb=hb, t0=t0, src=src:
                             t.matmul(
                                 ps[:, hb:hb + 512],
                                 src[:, i * 4096 + fc * 128:
                                     i * 4096 + fc * 128 + 128],
                                 self.xnb[:, i * 2048 + t0:i * 2048 + t0 + 512],
                                 start=(i == 0), stop=(i == IC - 1)),
                             inc="t" if i == IC - 1 else None)
                    b.w("s", "t", b.c["t"])
                    b.op("s", lambda s, fc=fc, hb=hb, e=e: s.activation(
                        self.big1[:, fc * 512:fc * 512 + 512], ps[:, hb:hb + 512],
                        AF.Relu, bias=self.b1c[:, e * 32 + fc:e * 32 + fc + 1]),
                        inc="s")
                    h_ev[(e, ts, fc)] = b.c["s"]
                h_last_t[(e, ts)] = b.c["t"]

                if prev2 is not None:
                    b.w("t", "v", yev[prev2])
                else:
                    b.w("t", "v", ccv)
                for fc in range(FC):
                    sidx, sv0 = slabv[(e, ts, fc)]
                    b.w("t", SLAB_SEMS[sidx], sv0)
                    gseq = (2, 3, 1, 0) if fc == 0 else (0, 1, 2, 3)
                    for g in gseq:
                        if fc == 0 and g == 0:
                            b.w("t", "s", h_ev[(e, ts, FC - 1)])
                        for og in range(2):
                            b.op("t", lambda t, fc=fc, g=g, og=og, sidx=sidx: t.matmul(
                                ps[:, g * 1024 + og * 512:g * 1024 + og * 512 + 512],
                                self.big1[:, fc * 512 + g * 128:fc * 512 + g * 128 + 128],
                                self.wsl[:, sidx * 1024 + og * 512:
                                         sidx * 1024 + og * 512 + 512],
                                start=(fc == 0), stop=(fc == FC - 1)),
                                inc="t" if (gseq[-1] == g and og == 1) else None)
                    fq[(e, ts, fc)] = b.c["t"]
                    if fc % 2 == 1:
                        quad_t[(e, ts, fc // 2)] = b.c["t"]
                    if fc + 6 < FC:
                        emit_slab(e, ts, fc + 6, fq[(e, ts, fc)])
                    if ts == 3 and e + 1 < E and fc % 2 == 1 and fc < 16:
                        qd = fc // 2
                        if qd == 0:
                            b.w("g", "t", h_last_t[(e, 3)])
                        for hf in range(2):
                            ps_, pv_ = pcv[("w1", e + 1, qd, hf)]
                            b.w("g", ps_, pv_)
                        w1v[e + 1] = b.dma(
                            "g",
                            self.big0[:, qd * 4096:qd * 4096 + 4096],
                            self.w1b[e + 1][:, qd * 4096:qd * 4096 + 4096],
                            sem="w1")
                ymm = b.c["t"]
                b.w("v", "t", ymm)
                for g in range(4):
                    tile = ts * 4 + g
                    for hh in range(2 if g == 0 else 1):
                        wid = 512 if g == 0 else 1024
                        c0 = tile * 1024 + hh * 512
                        p0 = g * 1024 + hh * 512
                        if e == 0:
                            b.op("v", lambda v, c0=c0, p0=p0, wid=wid, tile=tile,
                                 e=e: v.tensor_scalar(
                                     self.gbuf[:, c0:c0 + wid],
                                     ps[:, p0:p0 + wid],
                                     self.ohm[:, tile * 8 + e:tile * 8 + e + 1],
                                     None, ALU.mult), inc="v")
                        else:
                            b.op("v", lambda v, c0=c0, p0=p0, wid=wid, tile=tile,
                                 e=e: v.scalar_tensor_tensor(
                                     self.gbuf[:, c0:c0 + wid],
                                     ps[:, p0:p0 + wid],
                                     self.ohm[:, tile * 8 + e:tile * 8 + e + 1],
                                     self.gbuf[:, c0:c0 + wid],
                                     ALU.mult, ALU.add), inc="v")
                        if g == 0 and hh == 0:
                            yev_h0[(e, ts)] = b.c["v"]
                    if g == 0:
                        yev_g0[(e, ts)] = b.c["v"]
                yev[(e, ts)] = b.c["v"]

        bm_s = {}
        bm_v = {}

        def emit_b2mix(tile):
            pslot = (tile % 2) * 1024
            if tile == 0:
                b.w("t", "v", yev[(E - 1, 3)])
            if tile >= 2:
                b.w("t", "v", bm_v[tile - 2])
            b.op("t", lambda t, tile=tile, pslot=pslot: t.matmul(
                ps[0:8, pslot + 512:pslot + 640],
                self.ohm[:, tile * 8:tile * 8 + 8], self.cb["i128f"][:, :],
                start=True, stop=True), inc="t")
            slot = self.geb[0:8, 0:128] if tile % 2 == 0 else self.mbuf[0:8, 0:128]
            b.w("s", "t", b.c["t"])
            if tile >= 2:
                b.w("s", "s", bm_s[tile - 2])
            b.op("s", lambda s, tile=tile, pslot=pslot, slot=slot: s.copy(
                slot, ps[0:8, pslot + 512:pslot + 640]), inc="s")
            bm_s[tile] = b.c["s"]
            b.w("t", "s", bm_s[tile])
            for og in range(2):
                b.op("t", lambda t, og=og, pslot=pslot, slot=slot: t.matmul(
                    ps[:, pslot + og * 512:pslot + og * 512 + 512],
                    slot, self.b2n[:, og * 512:og * 512 + 512],
                    start=True, stop=True), inc="t" if og == 1 else None)
            b.w("v", "t", b.c["t"])
            b.op("v", lambda v, tile=tile, pslot=pslot: v.tensor_tensor(
                self.gbuf[:, tile * 1024:tile * 1024 + 1024],
                self.gbuf[:, tile * 1024:tile * 1024 + 1024],
                ps[:, pslot:pslot + 1024], ALU.add), inc="v")
            bm_v[tile] = b.c["v"]
            b.w("s", "v", bm_v[tile])
            b.op("s", lambda s, tile=tile: s.activation(
                self.gbuf[:, tile * 1024:tile * 1024 + 1024],
                self.gbuf[:, tile * 1024:tile * 1024 + 1024], AF.Silu), inc="s")

        if self.stop_after == "FFN":
            for tile in range(NT):
                emit_b2mix(tile)
            silu_s = b.c["s"]
            if "gbuf" in TAPS:
                b.w("q", "s", silu_s)
                b.dma("q", self.tap_t["gbuf"], self.gbuf[:, :], sem="tap")
            self.finish(b)
            return

        b.w("g", "t", quad_t[(E - 1, 3, 15)])
        tst = {"k": 0, "tr": {}, "ev": []}
        for wnm, base in (("wq", 0), ("wo", 8192)):
            addI = wnm == "wq"
            for half in range(2):
                k = tst["k"]
                tst["k"] += 1
                buf = w_stage[k % 2]
                sem = "wn" if k % 2 == 0 else "wm"
                if k >= 2:
                    b.w("g", "t", tst["tr"][k - 2])
                gv = b.dma("g", buf.rearrange("p (o i) -> p o i", o=4),
                           io[wnm][half * 512:half * 512 + 512, :].rearrange(
                               "(o p) i -> p o i", p=128), sem=sem)
                for tile in range(k * 4, k * 4 + 4):
                    emit_b2mix(tile)
                b.w("t", sem, gv)
                for ib in range(IC):
                    bk = R0 + (ib % 2) * 512
                    if len(tst["ev"]) >= 2:
                        b.w("t", "v", tst["ev"][-2])
                    mms = []
                    for ob4 in range(4):
                        diag = addI and (ob4 == ib - half * 4)
                        mms.append((ob4, False, not diag))
                        if diag:
                            mms.append((ob4, True, True))
                    for mi, (ob4, ident, stop) in enumerate(mms):
                        lastmm = mi == len(mms) - 1
                        lhs = (self.cb["i128b"][:, :] if ident else
                               buf[:, ob4 * 1024 + ib * 128:
                                   ob4 * 1024 + ib * 128 + 128])
                        b.op("t", lambda t, lhs=lhs, ob4=ob4, ident=ident,
                             stop=stop, bk=bk: t.matmul(
                                 ps[:, bk + ob4 * 128:bk + ob4 * 128 + 128],
                                 lhs, self.cb["i128b"][:, :],
                                 start=not ident, stop=stop),
                             inc="t" if lastmm else None)
                    b.w("v", "t", b.c["t"])
                    b.op("v", lambda v, ib=ib, base=base, half=half, bk=bk:
                         v.tensor_copy(
                             self.big0[:, base + ib * 1024 + half * 512:
                                       base + ib * 1024 + half * 512 + 512],
                             ps[:, bk:bk + 512]), inc="v")
                    tst["ev"].append(b.c["v"])
                tst["tr"][k] = b.c["t"]
        wT_v = b.c["v"]
        silu_s = b.c["s"]
        if "gbuf" in TAPS:
            b.w("q", "s", silu_s)
            b.dma("q", self.tap_t["gbuf"], self.gbuf[:, :], sem="tap")

        qv = {}
        for cc in range(IC):
            pbase = (cc % 2) * 2048
            if cc >= 2:
                b.w("t", "s", qv[cc - 2])
            if cc == 0:
                b.w("t", "v", wT_v)
                b.w("t", "v", bm_v[NT - 1])
            for i in range(IC):
                for tg in range(4):
                    b.op("t", lambda t, cc=cc, i=i, tg=tg, pbase=pbase: t.matmul(
                        ps[:, pbase + tg * 512:pbase + tg * 512 + 512],
                        self.big0[:, i * 1024 + cc * 128:i * 1024 + cc * 128 + 128],
                        self.xnb[:, i * 2048 + tg * 512:i * 2048 + tg * 512 + 512],
                        start=(i == 0), stop=False))
            for tg in range(4):
                rr = "qa" if cc < 4 else "qb"
                c0 = (cc % 4) * 128
                rp = ROWSA[rr][0]
                b.op("t", lambda t, tg=tg, rr=rr, c0=c0, rp=rp, pbase=pbase: t.matmul(
                    ps[:, pbase + tg * 512:pbase + tg * 512 + 512],
                    self.rrow(rr, c0, 128), self.cb["ones_b"][rp:rp + 1, :],
                    start=False, stop=True), inc="t" if tg == 3 else None)
            b.w("s", "t", b.c["t"])
            b.op("s", lambda s, cc=cc, pbase=pbase: s.copy(
                self.big0[:, 16384 + cc * 2048:16384 + cc * 2048 + 2048],
                ps[:, pbase:pbase + 2048]), inc="s")
            qv[cc] = b.c["s"]
        q_done = b.c["s"]
        if "qb" in TAPS:
            b.w("q", "s", q_done)
            b.dma("q", self.tap_t["qb"], self.big0[:, 16384:32768], sem="tap")
        if self.stop_after == "Q":
            self.finish(b)
            return

        b.w("t", "s", q_done)
        b.w("t", "cr", rb_done[0])
        S1 = self.xa[:, 0:512]
        S2 = self.xa[:, 512:1024]

        def eslot(par, i):
            p = par * 32
            return [self.xf[p:p + 1, 0:512], self.xf[p:p + 1, 512:1024],
                    self.ob2[p:p + 1, 0:512], self.ob2[p:p + 1, 512:1024],
                    self.xa[p:p + 1, 1536:2048], self.mfin[p:p + 1, 0:512]][i]
        gr_t = {}
        gr_mm = {}
        for r8 in range(8):
            gi, par = r8 // 2, r8 % 2
            gcol = (18 + gi) if par == 0 else (22 + gi)
            pslot = R0 + (r8 % 2) * 512
            if r8 == 0:
                b.w("s", "v", bm_v[NT - 1])
            else:
                b.w("s", "t", gr_mm[r8 - 1])
            b.op("s", lambda s, gcol=gcol: s.copy(
                self.geb[0:8, 0:1], self.smal[0:8, gcol:gcol + 1]), inc="s")
            b.w("t", "s", b.c["s"])
            if r8 >= 2:
                b.w("t", "v", gr_t[r8 - 2])
            b.op("t", lambda t, pslot=pslot: t.matmul(
                ps[0:1, pslot:pslot + 512],
                self.geb[0:8, 0:1], self.cb["sel8"][:, :],
                start=True, stop=True), inc="t")
            gr_mm[r8] = b.c["t"]
            b.w("v", "t", b.c["t"])
            if r8 == 0:
                b.w("v", "cr", rb_done[0])
            b.op("v", lambda v, par=par, gi=gi, pslot=pslot: v.tensor_copy(
                eslot(par, gi), ps[0:1, pslot:pslot + 512]), inc="v")
            gr_t[r8] = b.c["v"]
        if self.stop_after == "M1":
            self.finish(b)
            return
        for vi in range(2):
            for par in range(2):
                dst = eslot(par, 4 + vi)
                for hlf in range(2):
                    srow = (vi * 2 + hlf) * 32
                    b.op("v", lambda v, dst=dst, srow=srow, par=par, hlf=hlf:
                         v.tensor_copy(
                             dst[:, hlf * 256:hlf * 256 + 256]
                             .rearrange("p (a b) -> p a b", b=64),
                             self.xa[srow:srow + 1, 1024:1536]
                             .rearrange("p (a t b) -> p a t b", t=2, b=64)
                             [:, :, par, :]),
                         inc="v" if (vi, par, hlf) == (1, 1, 1) else None)
        vpar_v = b.c["v"]
        if self.stop_after == "M2":
            self.finish(b)
            return
        b.op("v", lambda v: v.tensor_tensor(self.csx[:, :], S1, S2, ALU.mult),
             inc="v")
        s12_v = b.c["v"]
        b.w("t", "v", s12_v)
        for par in range(2):
            ev = self.cb["etop"] if par == 0 else self.cb["ebot"]
            b.op("t", lambda t, par=par, ev=ev: t.matmul(
                ps[0:1, R0 + par * 512:R0 + par * 512 + 512], ev[:, :],
                self.csx[:, :], start=True, stop=True),
                inc="t" if par == 1 else None)
        y2s_t = b.c["t"]
        b.w("v", "t", y2s_t)
        for par in range(2):
            b.op("v", lambda v, par=par: v.tensor_tensor(
                ps[0:1, R0 + par * 512:R0 + par * 512 + 512],
                ps[0:1, R0 + par * 512:R0 + par * 512 + 512],
                eslot(par, 1), ALU.mult))
        b.op("v", lambda v: v.tensor_scalar(ps[0:1, R0:R0 + 1024],
                                            ps[0:1, R0:R0 + 1024],
                                            INVN2, None, ALU.mult), inc="v")
        y2_v = b.c["v"]
        b.w("s", "v", y2_v)
        b.op("s", lambda s: s.copy(self.sgs[0:1, 0:512], ps[0:1, R0:R0 + 512]))
        b.op("s", lambda s: s.copy(self.sgs[64:65, 0:512], ps[0:1, R0 + 512:R0 + 1024]),
             inc="s")
        y2_s = b.c["s"]
        if self.stop_after == "M3":
            self.finish(b)
            return
        if "y2r" in TAPS:
            b.w("q", "v", y2_v)
            b.dma("q", self.tap_t["y2r"], ps[0:1, R0:R0 + 1024], sem="tap")
        for par in range(2):
            b.op("v", lambda v, par=par: v.tensor_tensor(
                eslot(par, 0), eslot(par, 0), eslot(par, 5), ALU.mult))
            b.op("v", lambda v, par=par: v.tensor_tensor(
                eslot(par, 2), eslot(par, 2), eslot(par, 4), ALU.mult))
            for idx in (0, 2, 3):
                b.op("v", lambda v, par=par, idx=idx: v.tensor_scalar(
                    eslot(par, idx), eslot(par, idx), INVN2, None, ALU.mult),
                    inc="v" if (par == 1 and idx == 3) else None)
        csr_v = b.c["v"]
        if self.stop_after == "M4":
            self.finish(b)
            return
        b.w("t", "v", csr_v)
        b.w("t", "s", y2_s)
        for term, sidx2 in ((0, 0), (1, 2), (2, 3)):
            for par in range(2):
                p0 = par * 32
                b.op("t", lambda t, term=term, sidx2=sidx2, par=par, p0=p0: t.matmul(
                    ps[par * 64:par * 64 + 64,
                       R1 + term * 512:R1 + term * 512 + 512],
                    self.cb["ones_f"][p0:p0 + 1, 0:64],
                    eslot(par, sidx2), start=True, stop=True),
                    inc="t" if par == 1 else None)
        cs_t = b.c["t"]
        if self.stop_after == "M4a":
            self.finish(b)
            return
        b.w("v", "t", cs_t)
        b.op("v", lambda v: v.tensor_tensor(self.mfin[:, :], S1,
                                            ps[:, R1:R1 + 512], ALU.mult))
        b.op("v", lambda v: v.tensor_tensor(self.csx[:, :], S2,
                                            ps[:, R1 + 512:R1 + 1024], ALU.mult))
        b.op("v", lambda v: v.tensor_tensor(self.mfin[:, :], self.mfin[:, :],
                                            self.csx[:, :], ALU.add))
        b.op("v", lambda v: v.tensor_tensor(self.csx[:, :], S1, S2, ALU.mult))
        b.op("v", lambda v: v.tensor_tensor(self.csx[:, :], self.csx[:, :],
                                            ps[:, R1 + 1024:R1 + 1536], ALU.mult))
        b.op("v", lambda v: v.tensor_tensor(self.mfin[:, :], self.mfin[:, :],
                                            self.csx[:, :], ALU.add), inc="v")
        mf_v = b.c["v"]
        if self.stop_after == "M4b":
            self.finish(b)
            return
        b.w("s", "v", mf_v)
        b.op("s", lambda s: s.copy(self.mbuf[:, :], self.mfin[:, :]), inc="s")
        m_s = b.c["s"]
        if "mm" in TAPS:
            b.w("q", "v", mf_v)
            b.dma("q", self.tap_t["mm"], self.mfin[:, :], sem="tap")
        if self.stop_after == "M":
            self.finish(b)
            return

        ynat_s = {}
        for j in range(IC):
            pbase = (j % 2) * 2048
            if j >= 2:
                b.w("t", "s", ynat_s[j - 2])
            if j == 0:
                b.w("t", "s", m_s)
                b.w("t", "s", y2_s)
            for hh in range(2):
                r0 = hh * 64
                for tg in range(4):
                    b.op("t", lambda t, j=j, r0=r0, tg=tg, pbase=pbase: t.matmul(
                        ps[r0:r0 + 64, pbase + tg * 512:pbase + tg * 512 + 512],
                        self.mbuf[r0:r0 + 64, j * 64:j * 64 + 64],
                        self.big0[r0:r0 + 64,
                                  16384 + j * 2048 + tg * 512:
                                  16384 + j * 2048 + tg * 512 + 512],
                        start=True, stop=False))
                for tg in range(4):
                    b.op("t", lambda t, j=j, r0=r0, tg=tg, pbase=pbase: t.matmul(
                        ps[r0:r0 + 64, pbase + tg * 512:pbase + tg * 512 + 512],
                        self.sgs[r0:r0 + 1, j * 64:j * 64 + 64],
                        self.cb["ones_b"][r0:r0 + 1, :], start=False, stop=True),
                        inc="t" if tg == 3 else None)
            b.w("s", "t", b.c["t"])
            b.op("s", lambda s, j=j, pbase=pbase: s.copy(
                self.big1[:, j * 2048:j * 2048 + 2048],
                ps[:, pbase:pbase + 2048]), inc="s")
            ynat_s[j] = b.c["s"]
        ynat_done = b.c["s"]
        if "ynat" in TAPS:
            b.w("q", "s", ynat_done)
            b.dma("q", self.tap_t["ynat"], self.big1[:, :], sem="tap")
        if self.stop_after == "Y":
            self.finish(b)
            return

        def sqsl(cc):
            return self.wsl[:, (cc % 3) * 2048:(cc % 3) * 2048 + 2048]

        def stat_sl(k):
            return ps[(k % 3) * 32:(k % 3) * 32 + 1,
                      R0 + (k // 3) * 512:R0 + (k // 3) * 512 + 512]
        sq_v = {}
        sq_eng = {}
        ln_cc_t = {}
        for cc in range(IC):
            eng = "v" if cc % 2 == 0 else "s"
            if eng == "v":
                b.w("v", "s", ynat_s[cc])
            if cc >= 3:
                b.w(eng, "t", ln_cc_t[cc - 3])
            if eng == "v":
                b.op("v", lambda v, cc=cc: v.tensor_tensor(
                    sqsl(cc), self.big1[:, cc * 2048:cc * 2048 + 2048],
                    self.big1[:, cc * 2048:cc * 2048 + 2048], ALU.mult),
                    inc="v")
                sq_v[cc] = b.c["v"]
            else:
                b.op("s", lambda s, cc=cc: s.activation(
                    sqsl(cc), self.big1[:, cc * 2048:cc * 2048 + 2048],
                    AF.Square), inc="s")
                sq_v[cc] = b.c["s"]
            sq_eng[cc] = eng
            if cc == 0:
                b.w("t", "s", ynat_done)
            b.w("t", eng, sq_v[cc])
            for gg in range(4):
                for which in range(2):
                    k = gg + 4 * which
                    src2 = (self.big1[:, cc * 2048 + gg * 512:cc * 2048 + gg * 512 + 512]
                            if which == 0 else
                            sqsl(cc)[:, gg * 512:gg * 512 + 512])
                    b.op("t", lambda t, k=k, src2=src2, cc=cc: t.matmul(
                        stat_sl(k), self.cb["onescol_b"][:, :], src2,
                        start=(cc == 0), stop=(cc == IC - 1)),
                        inc="t" if (gg == 3 and which == 1) else None)
            ln_cc_t[cc] = b.c["t"]
        lnsum_t = b.c["t"]
        b.w("v", "t", lnsum_t)

        def syr(k):
            return self.xa[0:1, k * 512:k * 512 + 512]

        def sy2r(k):
            t2 = [self.xf, self.xf, self.ob2, self.ob2][k]
            c = [0, 512, 0, 512][k]
            return t2[0:1, c:c + 512]
        musq = self.mfin[0:1, 0:512]
        for k in range(8):
            dst = syr(k) if k < 4 else sy2r(k - 4)
            b.op("v", lambda v, k=k, dst=dst: v.tensor_copy(dst, stat_sl(k)),
                 inc="v" if k == 7 else None)
        for k in range(4):
            b.op("v", lambda v, k=k: v.tensor_scalar(syr(k), syr(k), 1.0 / C,
                                                     None, ALU.mult))
            b.op("v", lambda v, k=k: v.tensor_scalar(sy2r(k), sy2r(k), 1.0 / C,
                                                     None, ALU.mult))
            b.op("v", lambda v, k=k: v.tensor_tensor(musq, syr(k), syr(k),
                                                     ALU.mult))
            b.op("v", lambda v, k=k: v.tensor_tensor(sy2r(k), sy2r(k), musq,
                                                     ALU.subtract))
            b.op("v", lambda v, k=k: v.tensor_scalar(sy2r(k), sy2r(k), 1.0, 1e-5,
                                                     ALU.mult, ALU.add),
                 inc="v" if k == 3 else None)
        b.w("s", "v", b.c["v"])
        for k in range(4):
            b.op("s", lambda s, k=k: s.sqrt(sy2r(k), sy2r(k)),
                 inc="s" if k == 3 else None)
        b.w("v", "s", b.c["s"])
        for k in range(4):
            b.op("v", lambda v, k=k: v.reciprocal(sy2r(k), sy2r(k)))
            b.op("v", lambda v, k=k: v.tensor_tensor(syr(k), syr(k), sy2r(k),
                                                     ALU.mult))
            b.op("v", lambda v, k=k: v.tensor_scalar(syr(k), syr(k), -1.0, None,
                                                     ALU.mult),
                 inc="v" if k == 3 else None)
        stat_v = b.c["v"]
        b.w("t", "v", stat_v)
        bc_slist = []
        for k in range(8):
            which, gg = k // 4, k % 4
            row = sy2r(gg) if which == 0 else syr(gg)
            slot = R1 + (k % 2) * 512
            if k >= 2:
                b.w("t", "s", bc_slist[k - 2])
            b.op("t", lambda t, row=row, slot=slot: t.matmul(
                ps[:, slot:slot + 512], self.cb["ones_f"][0:1, :],
                row, start=True, stop=True), inc="t")
            b.w("s", "t", b.c["t"])
            b.op("s", lambda s, which=which, gg=gg, slot=slot: s.copy(
                self.big0[:, 16384 + which * 2048 + gg * 512:
                          16384 + which * 2048 + gg * 512 + 512],
                ps[:, slot:slot + 512]), inc="s")
            bc_slist.append(b.c["s"])
        bc_s = b.c["s"]
        for cc in range(IC):
            if cc == 0:
                b.w("v", "s", bc_s)
            for gg in range(4):
                slc = slice(cc * 2048 + gg * 512, cc * 2048 + gg * 512 + 512)
                abc = self.big0[:, 16384 + gg * 512:16384 + gg * 512 + 512]
                bbc = self.big0[:, 18432 + gg * 512:18432 + gg * 512 + 512]
                b.op("v", lambda v, slc=slc, abc=abc: v.tensor_tensor(
                    self.big1[:, slc], self.big1[:, slc], abc, ALU.mult))
                b.op("v", lambda v, slc=slc, bbc=bbc: v.tensor_tensor(
                    self.big1[:, slc], self.big1[:, slc], bbc, ALU.add),
                    inc="v" if gg == 3 else None)
            b.w("s", "v", b.c["v"])
            b.op("s", lambda s, cc=cc: s.activation(
                self.big1[:, cc * 2048:cc * 2048 + 2048],
                self.big1[:, cc * 2048:cc * 2048 + 2048], AF.Identity,
                bias=self.smal[:, 8 + cc:9 + cc], scale=self.smal[:, cc:cc + 1]),
                inc="s")
        ln_done2 = b.c["s"]
        if "ln" in TAPS:
            b.w("q", "s", ln_done2)
            b.dma("q", self.tap_t["ln"], self.big1[:, :], sem="tap")
        if self.stop_after == "LN":
            self.finish(b)
            return

        sfm_s = {}
        for oc in range(IC):
            pbase = (oc % 2) * 2048
            if oc >= 2:
                b.w("t", "s", sfm_s[oc - 2])
            if oc == 0:
                b.w("t", "s", ln_done2)
            for i in range(IC):
                for tg in range(4):
                    b.op("t", lambda t, oc=oc, i=i, tg=tg, pbase=pbase: t.matmul(
                        ps[:, pbase + tg * 512:pbase + tg * 512 + 512],
                        self.big0[:, 8192 + i * 1024 + oc * 128:
                                  8192 + i * 1024 + oc * 128 + 128],
                        self.big1[:, i * 2048 + tg * 512:i * 2048 + tg * 512 + 512],
                        start=(i == 0), stop=False))
            for tg in range(4):
                rr = "oa" if oc < 4 else "ob"
                c0 = (oc % 4) * 128
                rp = ROWSA[rr][0]
                b.op("t", lambda t, tg=tg, rr=rr, c0=c0, rp=rp, pbase=pbase: t.matmul(
                    ps[:, pbase + tg * 512:pbase + tg * 512 + 512],
                    self.rrow(rr, c0, 128), self.cb["ones_b"][rp:rp + 1, :],
                    start=False, stop=True), inc="t" if tg == 3 else None)
            b.w("s", "t", b.c["t"])
            b.op("s", lambda s, oc=oc, pbase=pbase: s.copy(
                self.xnb[:, oc * 2048:oc * 2048 + 2048], ps[:, pbase:pbase + 2048]),
                inc="s")
            sfm_s[oc] = b.c["s"]
        sfm_done = b.c["s"]
        if "sfm" in TAPS:
            b.w("q", "s", sfm_done)
            b.dma("q", self.tap_t["sfm"], self.xnb[:, :], sem="tap")
        if self.stop_after == "SFM":
            self.finish(b)
            return

        stm_s = {}
        for T in range(NT):
            pb = R0 + (T % 2) * 1024
            if T >= 2:
                b.w("t", "s", stm_s[T - 2])
            if T == 0:
                b.w("t", "s", sfm_done)
            for j in range(IC):
                b.op("t", lambda t, T=T, j=j, pb=pb: t.matmul(
                    ps[:, pb + j * 128:pb + j * 128 + 128],
                    self.xnb[:, j * 2048 + T * 128:j * 2048 + T * 128 + 128],
                    self.cb["i128b"][:, :], start=True, stop=True),
                    inc="t" if j == IC - 1 else None)
            b.w("s", "t", b.c["t"])
            b.op("s", lambda s, T=T, pb=pb: s.copy(
                self.big1[:, T * 1024:T * 1024 + 1024], ps[:, pb:pb + 1024]),
                inc="s")
            stm_s[T] = b.c["s"]
        if "stm" in TAPS:
            b.w("q", "s", b.c["s"])
            b.dma("q", self.tap_t["stm"], self.big1[:, :], sem="tap")
        if self.stop_after == "STM":
            self.finish(b)
            return

        fin_v = {}
        od = {}
        for T in range(NTL):
            sl = (T % 2) * 1024
            if T >= 2:
                b.w("q", "v", fin_v[T - 2])
            elif T == 0:
                b.w("q", "s", bc_s)
            xd = b.dma("q", self.xa[:, sl:sl + 1024],
                       io["x"][T * 128:(T + 1) * 128, :],
                       sem="xa" if T % 2 == 0 else "xb")
            ob = self.xf[:, 0:1024] if T % 2 == 0 else self.ob2[:, 0:1024]
            b.w("v", "s", stm_s[T])
            if T == 0:
                b.w("v", "s", silu_s)
            b.w("v", "xa" if T % 2 == 0 else "xb", xd)
            if T >= 2:
                b.w("v", "oa" if T % 2 == 0 else "ob", od[T - 2])
            b.op("v", lambda v, T=T: v.tensor_tensor(
                self.sgs[:, :], self.big1[:, T * 1024:T * 1024 + 1024],
                self.gbuf[:, T * 1024:T * 1024 + 1024], ALU.mult))
            b.op("v", lambda v, T=T, sl=sl, ob=ob: v.tensor_tensor(
                ob, self.xa[:, sl:sl + 1024], self.sgs[:, :], ALU.add), inc="v")
            fin_v[T] = b.c["v"]
            b.w("q", "v", fin_v[T])
            od[T] = b.dma("q", io["out"][T * 128:(T + 1) * 128, :], ob,
                          sem="oa" if T % 2 == 0 else "ob")
        self.finish(b)

    def finish(self, b):
        if self.stop_after:
            b.w("q", "v", b.c["v"])
            b.w("q", "s", b.c["s"])
            b.w("q", "t", b.c["t"])
            b.dma("q", self.io["out"][0:128, :], self.xa[:, 0:1024], sem="oa")
        b.w("q", "oa", b.c["oa"])
        b.w("q", "ob", b.c["ob"])
        b.w("q", "tap", b.c["tap"])
        b.w("q", "pa", b.c["pa"])
        b.w("q", "pb", b.c["pb"])
        b.w("q", "pc", b.c["pc"])
        b.w("q", "wm", b.c["wm"])
        b.w("q", "w1", b.c["w1"])
        b.w("q", "pd", b.c["pd"])
        b.w("q", "pe", b.c["pe"])

    def build(self):
        nc = self.nc
        import contextlib
        names = ["t", "v", "s", "cc",
                 "qi", "gi", "wn", "xa", "xb", "w1", "sa", "sb",
                 "cw", "cr", "oa", "ob", "tap",
                 "pa", "pb", "pc", "wm", "pd", "pe"]
        with contextlib.ExitStack() as stk:
            sems = {n: stk.enter_context(nc.semaphore("sem_" + n)) for n in names}
            block = stk.enter_context(nc.Block())

            @block.sync
            def _(e):
                self.walk(Wk("q", e, sems))

            @block.gpsimd
            def _(e):
                self.walk(Wk("g", e, sems))

            @block.tensor
            def _(e):
                self.walk(Wk("t", e, sems))

            @block.vector
            def _(e):
                self.walk(Wk("v", e, sems))

            @block.scalar
            def _(e):
                self.walk(Wk("s", e, sems))
        return nc


def build_program(taps=(), stop_after=None, nt=NT, init_only=False, a_stage=4,
                  dbg=()):
    return MK(taps=taps, stop_after=stop_after, nt=nt, init_only=init_only,
              a_stage=a_stage, dbg=dbg).build()


def make_in_maps(inputs):
    x = inputs["L0_attn_x"]
    maps = []
    for c in range(8):
        bsel = c // 4
        t0 = (c % 4) * TC
        m = {
            "x": x[bsel, t0:t0 + TC, :],
            "wq": inputs["L0_attn_Wq"], "wk1": inputs["L0_attn_Wk1"],
            "wk2": inputs["L0_attn_Wk2"], "wv1": inputs["L0_attn_Wv1"],
            "wv2": inputs["L0_attn_Wv2"], "wo": inputs["L0_attn_Wo"],
            "biases": np.stack([
                inputs["L0_attn_bq"], inputs["L0_attn_bk1"],
                inputs["L0_attn_bk2"], inputs["L0_attn_bv1"],
                inputs["L0_attn_bv2"], inputs["L0_attn_bo"]]),
            "gvec": np.stack([
                inputs["L0_attn_g1"], inputs["L0_attn_g2"],
                inputs["L0_attn_g3"], inputs["L0_attn_g4"]]),
            "lng": inputs["L0_attn_ln_g"], "lnb": inputs["L0_attn_ln_b"],
            "gw": inputs["L0_ffn_gate_w"], "gb": inputs["L0_ffn_gate_b"],
            "eb": inputs["L0_ffn_expert_bias"],
            "w1": inputs["L0_ffn_w1"], "b1": inputs["L0_ffn_b1"],
            "w2": inputs["L0_ffn_w2"], "b2": inputs["L0_ffn_b2"],
        }
        maps.append({k: np.ascontiguousarray(v, dtype=np.float32)
                     for k, v in m.items()})
    return maps


def kernel(**inputs):
    nc = build_program()
    in_maps = make_in_maps(inputs)
    res = run_bass_kernel_spmd(nc, in_maps, core_ids=list(range(8)))
    out = np.empty((2, NFULL, C), dtype=np.float32)
    for c in range(8):
        out[c // 4, (c % 4) * TC:(c % 4 + 1) * TC, :] = res.results[c]["out"]
    return out


# version 11
# speedup vs baseline: 1.0568x; 1.0027x over previous
import numpy as np
import ml_dtypes
import concourse.bass as bass
import concourse.mybir as mybir
from concourse.bass_utils import run_bass_kernel_spmd

F32 = mybir.dt.float32
BF16 = mybir.dt.bfloat16
AF = mybir.ActivationFunctionType
ALU = mybir.AluOpType
AX = mybir.AxisListType

TC = 2048
NT = 16
C = 1024
IC = 8
H = 16
E = 8
FF = 4096
FC = 32
NFULL = 8192
RG = [[0, 1, 2, 3], [4, 5, 6, 7]]
BF = ml_dtypes.bfloat16
INVN2 = 1.0 / (float(NFULL) * float(NFULL))

PS_S1 = 0
PS_S2 = 512
PS_MA = 1024
PS_MB = 1536
R0, R1, R2, R3 = 2048, 2560, 3072, 3584

ROWSA = {
    "k1a": (0, 0), "k1b": (32, 0), "k2a": (64, 0),
    "k2b": (0, 1), "v1a": (32, 1), "v1b": (64, 1),
    "v2a": (0, 2), "v2b": (32, 2), "qa": (64, 2),
    "qb": (0, 3), "oa": (32, 3), "ob": (64, 3),
}


class Wk:

    def __init__(self, cur, eng, sems):
        self.cur = cur
        self.e = eng
        self.sems = sems
        self.c = {k: 0 for k in sems}

    def dma(self, qn, out, in_, sem=None, slow=False):
        sem = sem or qn
        self.c[sem] += 16
        if self.cur == qn:
            self.e.dma_start(out=out, in_=in_,
                             allow_slow_non_contiguous=slow).then_inc(
                self.sems[sem], 16)
        return self.c[sem]

    def w(self, qn, sem, val):
        if self.cur == qn and val > 0:
            self.e.wait_ge(self.sems[sem], val)

    def op(self, qn, fn, inc=None):
        if inc:
            self.c[inc] += 1
        if self.cur == qn:
            ins = fn(self.e)
            if inc:
                ins.then_inc(self.sems[inc], 1)
        return self.c[inc] if inc else None


class MK:
    def __init__(self, taps=(), stop_after=None, nt=NT, init_only=False,
                 a_stage=4, dbg=()):
        self.taps = set(taps)
        self.stop_after = stop_after
        self.nt = nt
        self.init_only = init_only
        self.a_stage = a_stage
        self.dbg = set(dbg)
        nc = bass.Bass("TRN2", target_bir_lowering=False,
                       detect_race_conditions=False)
        self.nc = nc

        self.io = {}
        for nm, sh in (
            ("x", [TC, C]), ("wq", [C, C]), ("wk1", [C, C]), ("wk2", [C, C]),
            ("wv1", [C, C]), ("wv2", [C, C]), ("wo", [C, C]),
            ("biases", [6, C]), ("gvec", [4, H]), ("lng", [C]), ("lnb", [C]),
            ("gw", [E, C]), ("gb", [E]), ("eb", [E]),
            ("w1", [E, C, FF]), ("b1", [E, FF]), ("w2", [E, FF, C]), ("b2", [E, C]),
        ):
            self.io[nm] = nc.dram_tensor(nm, sh, F32, kind="ExternalInput").ap()
        self.io["out"] = nc.dram_tensor("out", [TC, C], F32, kind="ExternalOutput").ap()
        self.io["cc_in"] = nc.dram_tensor("cc_in", [128, 1536], F32, kind="Internal").ap()
        self.io["cc_out"] = nc.dram_tensor("cc_out", [128, 1536], F32, kind="Internal").ap()
        self.w2b = nc.dram_tensor("w2b", [E, 16, 128, 2048], BF16, kind="Internal").ap()
        self.w1b = nc.dram_tensor("w1b", [E, 128, 32768], BF16, kind="Internal").ap()

        self.tap_t = {}
        for nm, sh, dt in (
            ("xnb", [128, IC * TC], BF16), ("kv0", [128, 4096], BF16),
            ("scc", [128, 1536], F32), ("rcc", [128, 1536], F32),
            ("onehot", [128, NT * 8], F32), ("gbuf", [128, NT * C], BF16),
            ("qb", [128, 16384], BF16), ("mm", [128, 512], F32),
            ("y2r", [2, 512], F32), ("ynat", [128, IC * TC], BF16),
            ("ln", [128, IC * TC], BF16), ("sfm", [128, IC * TC], BF16),
            ("stm", [128, NT * C], BF16),
        ):
            if nm in self.taps:
                self.tap_t[nm] = nc.dram_tensor("tap_" + nm, sh, dt,
                                                kind="ExternalOutput").ap()

        def inl(name, arr):
            return nc.inline_tensor(arr, name).ap()
        et = np.zeros((128, 1), np.float32); et[:64] = 1.0
        ebv = np.zeros((128, 1), np.float32); ebv[64:] = 1.0
        st = np.zeros((8, 512), BF)
        for qq in range(512):
            st[qq // 64, qq] = 1.0
        self.consts = {
            "i128b": np.eye(128, dtype=BF), "i128f": np.eye(128, dtype=np.float32),
            "i8f": np.eye(8, dtype=np.float32),
            "ones_b": np.ones((128, 512), dtype=BF),
            "ones_f": np.ones((128, 128), np.float32),
            "onescol_b": np.ones((128, 1), dtype=BF),
            "etop": et, "ebot": ebv,
            "sel8": st,
        }
        self.cin = {k: inl(k, v) for k, v in self.consts.items()}

        A = nc.alloc_sbuf_tensor
        self.xnb = A("xnb", [128, IC * TC], BF16)
        self.gbuf = A("gbuf", [128, NT * C], BF16)
        self.big0 = A("big0", [128, 32768], BF16)
        self.big1 = A("big1", [128, 16384], BF16)
        self.wsl = A("wsl", [128, 6144], BF16)
        self.xa = A("xa", [128, 2048], F32)
        self.xf = A("xf", [128, 1024], F32)
        self.ob2 = A("ob2", [128, 1024], F32)
        self.ohm = A("ohm", [128, NT * 8], F32)
        self.sgs = A("sgs", [128, 1024], BF16)
        self.mfin = A("mfin", [128, 512], F32)
        self.mbuf = A("mbuf", [128, 512], BF16)
        self.csx = A("csx", [128, 512], F32)
        self.smal = A("smal", [128, 40], F32)
        self.rowsA = A("rowsA", [128, 2048], BF16)
        self.b1c = A("b1c", [128, 256], F32)
        self.b2n = A("b2n", [8, 1024], BF16)
        self.ge = A("ge", [8, 128], F32)
        self.geb = A("geb", [8, 128], BF16)
        self.gwT = A("gwT", [128, 64], F32)
        self.cb = {}
        for k, v in self.consts.items():
            self.cb[k] = A("c_" + k, list(v.shape),
                           BF16 if v.dtype == BF else F32)
        self.ps = nc.alloc_psum_tensor("PALL", [128, 4096], F32)

    def rrow(self, name, c0=0, n=512):
        p, g = ROWSA[name]
        return self.rowsA[p:p + 1, g * 512 + c0:g * 512 + c0 + n]

    def walk(self, b):
        io, ps = self.io, self.ps
        TAPS = self.taps

        for k in self.consts:
            b.dma("q", self.cb[k][:, :], self.cin[k], sem="qi")
        for i, pfx in enumerate(("q", "k1", "k2", "v1", "v2", "o")):
            for half, suf in ((0, "a"), (1, "b")):
                p, g = ROWSA[pfx + suf]
                b.dma("g", self.rowsA[p:p + 1, g * 512:g * 512 + 512],
                      io["biases"][i:i + 1, half * 512:half * 512 + 512], sem="gi")
        b.dma("g", self.b2n[:, :], io["b2"], sem="gi")
        b.dma("q", self.smal[:, 0:8], io["lng"].rearrange("(c p) -> p c", p=128),
              sem="qi", slow=True)
        b.dma("q", self.smal[:, 8:16], io["lnb"].rearrange("(c p) -> p c", p=128),
              sem="qi", slow=True)
        b.dma("q", self.smal[0:8, 16:17], io["gb"].rearrange("(a e) -> e a", a=1),
              sem="qi", slow=True)
        b.dma("q", self.smal[0:8, 17:18], io["eb"].rearrange("(a e) -> e a", a=1),
              sem="qi", slow=True)
        b.dma("q", self.smal[0:8, 18:26].rearrange("p (t g) -> p t g", t=2),
              io["gvec"].rearrange("g (h t) -> h t g", t=2), sem="qi", slow=True)
        b.dma("q", self.b1c[:, :], io["b1"].rearrange("e (c p) -> p (e c)", p=128),
              sem="qi", slow=True)
        qinit = b.dma("q", self.gwT[:, :],
                      io["gw"].rearrange("e (c p) -> p (e c)", p=128), sem="qi", slow=True)

        qi_all = b.c["qi"]
        gi_all = b.c["gi"]
        b.op("v", lambda v: v.memset(self.smal[:, 38:39], 1e-5), inc="v")
        eps_v = b.c["v"]

        wst = {"k": 0, "tr": {}, "ev": []}
        w_stage = (self.wsl[:, 0:4096], self.big1[:, 10240:14336])

        def wT_build(wnm, dst, cstride, first, addI=False):
            for half in range(2):
                k = wst["k"]
                wst["k"] += 1
                buf = w_stage[k % 2]
                sem = "wn" if k % 2 == 0 else "wm"
                if k >= 2:
                    b.w("g", "t", wst["tr"][k - 2])
                gv = b.dma("g", buf.rearrange("p (o i) -> p o i", o=4),
                           io[wnm][half * 512:half * 512 + 512, :].rearrange(
                               "(o p) i -> p o i", p=128), sem=sem)
                b.w("t", sem, gv)
                if first and half == 0:
                    b.w("t", "qi", qi_all)
                for ib in range(IC):
                    bk = R0 + (ib % 2) * 512
                    if len(wst["ev"]) >= 2:
                        b.w("t", "v", wst["ev"][-2])
                    mms = []
                    for ob4 in range(4):
                        diag = addI and (ob4 == ib - half * 4)
                        mms.append((ob4, False, not diag))
                        if diag:
                            mms.append((ob4, True, True))
                    for mi, (ob4, ident, stop) in enumerate(mms):
                        lastmm = mi == len(mms) - 1
                        lhs = (self.cb["i128b"][:, :] if ident else
                               buf[:, ob4 * 1024 + ib * 128:
                                   ob4 * 1024 + ib * 128 + 128])
                        b.op("t", lambda t, ob4=ob4, ident=ident, stop=stop,
                             lhs=lhs, bk=bk: t.matmul(
                                 ps[:, bk + ob4 * 128:bk + ob4 * 128 + 128],
                                 lhs, self.cb["i128b"][:, :],
                                 start=not ident, stop=stop),
                             inc="t" if lastmm else None)
                    b.w("v", "t", b.c["t"])
                    b.op("v", lambda v, ib=ib, half=half, bk=bk: v.tensor_copy(
                        self.big0[:, dst + ib * cstride + half * 512:
                                  dst + ib * cstride + half * 512 + 512],
                        ps[:, bk:bk + 512]), inc="v")
                    wst["ev"].append(b.c["v"])
                wst["tr"][k] = b.c["t"]

        pc_units = ([("w1", e0, i, hf) for e0 in range(2)
                     for i in range(IC) for hf in range(2)] +
                    [("w2", e, qd) for e in range(2) for qd in range(16)])
        pcv = {}
        PC_SEMS = ("pa", "pb", "pd", "pe")
        pc_prev = {s: None for s in PC_SEMS}

        def emit_precast_unit(kind, u_parity):
            sem = PC_SEMS[u_parity % 4]
            if pc_prev[sem] is not None:
                b.w("g", sem, pc_prev[sem])
            if kind[0] == "w1":
                _, e0, i, hf = kind
                d = b.dma("g",
                          self.w1b[e0][:, i * 4096 + hf * 2048:
                                       i * 4096 + hf * 2048 + 2048],
                          io["w1"][e0][i * 128:(i + 1) * 128,
                                       hf * 2048:(hf + 1) * 2048], sem=sem)
            else:
                _, e, qd = kind
                d = b.dma("g", self.w2b[e, qd].rearrange("p (c o) -> p c o", c=2),
                          io["w2"][e, qd * 256:(qd + 1) * 256, :].rearrange(
                              "(c p) o -> p c o", p=128), sem=sem)
            pcv[kind] = (sem, d)
            pc_prev[sem] = d

        def emit_precast(u):
            emit_precast_unit(pc_units[u], u % 4)

        for u in range(4):
            emit_precast(u)

        wT_build("wk1", 0, 2048, True, addI=True)
        wT_build("wk2", 1024, 2048, False, addI=True)
        wT_build("wv1", 16384, 2048, False, addI=True)
        wT_build("wv2", 16384 + 1024, 2048, False, addI=True)
        wkv_v = b.c["v"]
        if self.init_only:
            if "kv0" in self.taps:
                b.w("q", "v", wkv_v)
                b.dma("q", self.tap_t["kv0"], self.big0[:, 0:4096], sem="tap")
            self.finish(b)
            return
        NTL = self.nt

        xdma = {}
        ln_done = {}
        xnT_s = {}
        trv = {}
        gate_t = {}
        ge_s = {}
        maskv = {}
        zev = {}
        smm_t = {}
        xdma[0] = b.dma("q", self.xa[:, 0:1024], io["x"][0:128, :], sem="xa")
        xdma[1] = b.dma("q", self.xa[:, 1024:2048], io["x"][128:256, :], sem="xb")
        for T in range(NTL):
            sl = (T % 2) * 1024
            xt = self.xa[:, sl:sl + 1024]
            b.w("v", "xa" if T % 2 == 0 else "xb", xdma[T])
            if T == 0:
                b.w("s", "qi", qi_all)
            b.op("v", lambda v, xt=xt: v.reduce_sum(self.smal[:, 32:33], xt, AX.X),
                 inc="v")
            b.w("s", "v", b.c["v"])
            b.op("s", lambda s: s.mul(self.smal[:, 33:34], self.smal[:, 32:33],
                                      -1.0 / C), inc="s")
            b.w("v", "s", b.c["s"])
            b.op("v", lambda v, xt=xt: v.tensor_scalar(xt, xt, self.smal[:, 33:34],
                                                       None, ALU.add), inc="v")
            b.w("s", "v", b.c["v"])
            b.op("s", lambda s, xt=xt, T=T: s.activation(
                self.big1[:, 8192 + (T % 2) * 1024:8192 + (T % 2) * 1024 + 1024],
                xt, AF.Square, accum_out=self.smal[:, 33:34]))
            b.op("s", lambda s: s.activation(self.smal[:, 34:35],
                                             self.smal[:, 33:34], AF.Sqrt,
                                             bias=self.smal[:, 38:39],
                                             scale=1.0 / C), inc="s")
            b.w("v", "s", b.c["s"])
            b.op("v", lambda v: v.reciprocal(self.smal[:, 35:36], self.smal[:, 34:35]),
                 inc="v")
            b.w("s", "v", b.c["v"])
            b.op("s", lambda s, xt=xt: s.mul(xt, xt, self.smal[:, 35:36]), inc="s")
            ln_done[T] = b.c["s"]

            if self.a_stage < 1:
                xnT_s[T] = b.c["s"]
                maskv[T] = b.c["v"]
                gate_t[T] = b.c["t"]
                ge_s[T] = b.c["s"]
                zev[(T, 0, 0)] = zev[(T, 0, 1)] = b.c["s"]
                zev[(T, 1, 0)] = zev[(T, 1, 1)] = b.c["s"]
                smm_t[T] = b.c["t"]
                continue
            b.w("t", "s", ln_done[T])
            if T > 0:
                b.w("t", "s", zev[(T - 1, 1, 0)])
            for j in range(IC):
                b.op("t", lambda t, j=j, xt=xt: t.matmul(
                    ps[:, R0 + j * 128:R0 + j * 128 + 128],
                    xt[:, j * 128:j * 128 + 128], self.cb["i128f"][:, :],
                    start=True, stop=True), inc="t" if j == IC - 1 else None)
            trT = b.c["t"]
            trv[T] = trT
            b.w("v", "t", trT)
            if "noxf" in self.dbg:
                b.op("v", lambda v: v.engine_nop(), inc="v")
            else:
                b.op("v", lambda v: v.tensor_copy(self.xf[:, :], ps[:, R0:R0 + 1024]),
                     inc="v")
            xf_v = b.c["v"]
            b.w("s", "t", trT)
            b.w("s", "v", xf_v)
            if "noxnb" in self.dbg:
                b.op("s", lambda s: s.copy(self.xnb[:, 0:1024],
                                           ps[:, R0:R0 + 1024]), inc="s")
            else:
                b.op("s", lambda s, T=T: s.copy(
                    self.xnb[:, :].rearrange("p (c f) -> p c f", c=IC)[
                        :, :, T * 128:T * 128 + 128],
                    ps[:, R0:R0 + 1024].rearrange("p (c f) -> p c f", f=128)),
                    inc="s")
            xnT_s[T] = b.c["s"]

            if self.a_stage < 2:
                maskv[T] = b.c["v"]
                gate_t[T] = b.c["t"]
                ge_s[T] = b.c["s"]
                zev[(T, 0, 0)] = zev[(T, 0, 1)] = b.c["s"]
                zev[(T, 1, 0)] = zev[(T, 1, 1)] = b.c["s"]
                smm_t[T] = b.c["t"]
                continue
            b.w("t", "v", xf_v)
            if T > 0:
                b.w("t", "s", ge_s[T - 1])
                b.w("t", "s", zev[(T - 1, 1, 1)])
            for i in range(IC):
                b.op("t", lambda t, i=i: t.matmul(
                    ps[0:8, R2:R2 + 128],
                    self.gwT[:, :].rearrange("p (e c) -> p c e", c=IC)[:, i, :],
                    self.xf[:, i * 128:i * 128 + 128],
                    start=(i == 0), stop=(i == IC - 1)),
                    inc="t" if i == IC - 1 else None)
            gmm = b.c["t"]
            b.w("s", "t", gmm)
            b.op("s", lambda s: s.activation(self.ge[:, :], ps[0:8, R2:R2 + 128],
                                             AF.Identity, bias=self.smal[0:8, 16:17]))
            b.op("s", lambda s: s.add(self.ge[:, :], self.ge[:, :],
                                      self.smal[0:8, 17:18]), inc="s")
            ge_s[T] = b.c["s"]
            b.w("t", "s", ge_s[T])
            b.op("t", lambda t: t.matmul(ps[:, R2 + 128:R2 + 136], self.ge[:, :],
                                         self.cb["i8f"][:, :], start=True, stop=True),
                 inc="t")
            gate_t[T] = b.c["t"]
            b.w("v", "t", gate_t[T])
            b.op("v", lambda v: v.reduce_max(self.smal[:, 36:37],
                                             ps[:, R2 + 128:R2 + 136], AX.X),
                 inc="v")
            b.w("s", "v", b.c["v"])
            b.op("s", lambda s: s.copy(self.smal[:, 37:38], self.smal[:, 36:37]),
                 inc="s")
            b.w("v", "s", b.c["s"])
            b.op("v", lambda v, T=T: v.tensor_scalar(
                self.ohm[:, T * 8:T * 8 + 8], ps[:, R2 + 128:R2 + 136],
                self.smal[:, 37:38], None, ALU.is_ge), inc="v")
            maskv[T] = b.c["v"]

            if self.a_stage < 3:
                zev[(T, 0, 0)] = zev[(T, 0, 1)] = b.c["s"]
                zev[(T, 1, 0)] = zev[(T, 1, 1)] = b.c["s"]
                smm_t[T] = b.c["t"]
                continue
            kvs = (T % 2) * 4096
            for rnd in range(2):
                wbase = rnd * 16384
                rks = (("k1a", "k1b", "k2a", "k2b") if rnd == 0
                       else ("v1a", "v1b", "v2a", "v2b"))
                b.w("t", "s", xnT_s[T])
                if T == 0 and rnd == 0:
                    b.w("t", "v", wkv_v)
                    b.w("t", "gi", gi_all)
                if rnd == 0 and T > 0:
                    b.w("t", "s", zev[(T - 1, 1, 1)])
                zt_g = {}
                for g4 in range(4):
                    rot = R0 + g4 * 512
                    if rnd == 0 and g4 == 2:
                        b.w("t", "v", maskv[T])
                    if rnd == 1 and g4 % 2 == 0:
                        b.w("t", "s", zev[(T, 0, g4 // 2)])
                    for i in range(IC):
                        b.op("t", lambda t, i=i, g4=g4, rot=rot, wbase=wbase, T=T:
                             t.matmul(
                                 ps[:, rot:rot + 512],
                                 self.xnb[:, i * 2048 + T * 128:
                                          i * 2048 + T * 128 + 128],
                                 self.big0[:, wbase + i * 2048 + g4 * 512:
                                           wbase + i * 2048 + g4 * 512 + 512],
                                 start=(i == 0), stop=False))
                    rp = ROWSA[rks[g4]][0]
                    b.op("t", lambda t, rot=rot, rr=rks[g4], rp=rp: t.matmul(
                        ps[:, rot:rot + 512], self.cb["ones_b"][rp:rp + 1, 0:128],
                        self.rrow(rr), start=False, stop=True), inc="t")
                    zt_g[g4] = b.c["t"]
                for g2 in range(2):
                    rot = R0 + g2 * 1024
                    b.w("s", "t", zt_g[2 * g2 + 1])
                    b.op("s", lambda s, g2=g2, rot=rot, rnd=rnd, kvs=kvs: s.copy(
                        self.big1[:, kvs + rnd * 2048 + g2 * 1024:
                                  kvs + rnd * 2048 + g2 * 1024 + 1024],
                        ps[:, rot:rot + 1024]), inc="s")
                    zev[(T, rnd, g2)] = b.c["s"]

            if self.a_stage < 4:
                smm_t[T] = b.c["t"]
                if T + 2 < NTL:
                    b.w("q", "s", zev[(T, 1, 1)])
                    xdma[T + 2] = b.dma("q", self.xa[:, sl:sl + 1024],
                                        io["x"][(T + 2) * 128:(T + 3) * 128, :],
                                        sem="xa" if T % 2 == 0 else "xb")
                continue
            for sidx in range(2):
                b.w("t", "s", zev[(T, 1, sidx)])
                for h in range(H):
                    r0 = (h % 2) * 64
                    koff = kvs + sidx * 1024 + h * 64
                    voff = kvs + 2048 + sidx * 1024 + h * 64
                    cc0 = sidx * 512 + (h // 2) * 64
                    b.op("t", lambda t, koff=koff, voff=voff, r0=r0, cc0=cc0, T=T, h=h:
                         t.matmul(ps[r0:r0 + 64, cc0:cc0 + 64],
                                  self.big1[:, koff:koff + 64],
                                  self.big1[:, voff:voff + 64],
                                  start=(T == 0 and h < 2),
                                  stop=(T == NTL - 1 and h >= H - 2)))
            for mi in range(4):
                vb = kvs + 2048 + (mi // 2) * 1024 + (mi % 2) * 512
                mp = (mi % 3) * 32 if mi < 3 else 0
                mcol = PS_MA if mi < 3 else PS_MB
                b.op("t", lambda t, vb=vb, mp=mp, mcol=mcol, T=T, mi=mi: t.matmul(
                    ps[mp:mp + 1, mcol:mcol + 512], self.cb["onescol_b"][:, :],
                    self.big1[:, vb:vb + 512],
                    start=(T == 0), stop=(T == NTL - 1)),
                    inc="t" if mi == 3 else None)
            smm_t[T] = b.c["t"]
            if T + 2 < NTL:
                b.w("q", "t", trv[T])
                xdma[T + 2] = b.dma("q", self.xa[:, sl:sl + 1024],
                                    io["x"][(T + 2) * 128:(T + 3) * 128, :],
                                    sem="xa" if T % 2 == 0 else "xb")
            for u in range(max(4, T * 4), min((T + 1) * 4, len(pc_units))):
                emit_precast(u)
            if T == 10:
                for k in range(4):
                    for hf in range(2):
                        ps_, pv_ = pcv[("w1", 0, k, hf)]
                        b.w("g", ps_, pv_)
                for k in range(4):
                    gbuf_w1 = b.dma("g", self.gbuf[:, k * 4096:(k + 1) * 4096],
                                    self.w1b[0][:, k * 4096:(k + 1) * 4096],
                                    sem="w1")
        for u in range(max(4, NTL * 4), len(pc_units)):
            emit_precast(u)
        s_done_t = smm_t[NTL - 1]

        if self.a_stage < 4:
            if "xnb" in TAPS and self.a_stage >= 1:
                b.w("q", "s", xnT_s[NTL - 1])
                b.dma("q", self.tap_t["xnb"], self.xnb[:, :], sem="tap")
            if self.a_stage < 1:
                b.w("q", "s", ln_done[NTL - 1])
            if "onehot" in TAPS and self.a_stage >= 2:
                b.w("q", "v", maskv[NTL - 1])
                b.dma("q", self.tap_t["onehot"], self.ohm[:, :], sem="tap")
            if "kv0" in TAPS and self.a_stage >= 3:
                b.w("q", "s", zev[(max(0, NTL - 2), 1, 1)])
                b.dma("q", self.tap_t["kv0"], self.big1[:, 0:4096], sem="tap")
            self.finish(b)
            return
        b.w("v", "t", s_done_t)
        b.op("v", lambda v: v.tensor_copy(self.xa[:, 0:1024], ps[:, 0:1024]))
        b.op("v", lambda v: v.memset(self.xa[:, 1024:1536], 0.0))
        b.op("v", lambda v: v.tensor_copy(self.xa[0:1, 1024:1536], ps[0:1, PS_MA:PS_MA + 512]))
        b.op("v", lambda v: v.tensor_copy(self.xa[32:33, 1024:1536], ps[32:33, PS_MA:PS_MA + 512]))
        b.op("v", lambda v: v.tensor_copy(self.xa[64:65, 1024:1536], ps[64:65, PS_MA:PS_MA + 512]))
        b.op("v", lambda v: v.tensor_copy(self.xa[96:97, 1024:1536], ps[0:1, PS_MB:PS_MB + 512]),
             inc="v")
        ccv = b.c["v"]
        if "scc" in TAPS:
            b.w("q", "v", ccv)
            b.dma("q", self.tap_t["scc"], self.xa[:, 0:1536], sem="tap")
        b.w("g", "v", ccv)
        ccw = b.dma("g", io["cc_in"], self.xa[:, 0:1536], sem="cw")
        if self.stop_after != "A":
            b.w("g", "cw", ccw)
            b.op("g", lambda g: g.collective_compute(
                "AllReduce", ALU.add, replica_groups=RG,
                ins=[io["cc_in"]], outs=[io["cc_out"]]), inc="cc")

        if "xnb" in TAPS:
            b.w("q", "s", xnT_s[NTL - 1])
            b.dma("q", self.tap_t["xnb"], self.xnb[:, :], sem="tap")
        if "kv0" in TAPS:
            b.w("q", "s", zev[(max(0, NTL - 2), 1, 1)])
            b.dma("q", self.tap_t["kv0"], self.big1[:, 0:4096], sem="tap")
        if "onehot" in TAPS:
            b.w("q", "v", maskv[NTL - 1])
            b.dma("q", self.tap_t["onehot"], self.ohm[:, :], sem="tap")

        if self.stop_after == "A":
            self.finish(b)
            return

        b.w("g", "t", s_done_t)
        for i in range(4, IC):
            for hf in range(2):
                ps_, pv_ = pcv[("w1", 0, i, hf)]
                b.w("g", ps_, pv_)
        w1v = [None] * E
        for i in range(4, IC):
            w1hi = b.dma("g", self.big0[:, i * 4096:(i + 1) * 4096],
                         self.w1b[0][:, i * 4096:(i + 1) * 4096], sem="w1")
        b.w("g", "w1", w1hi)
        for i in range(4):
            w1v[0] = b.dma("g", self.big0[:, i * 4096:(i + 1) * 4096],
                           self.w1b[0][:, i * 4096:(i + 1) * 4096], sem="w1")
        slabv = {}
        quad_t = {}
        fq = {}
        slab_n = [0]
        rb_done = [0]
        SLAB_SEMS = ("sa", "sb", "pc", "wn", "wm", "cw")

        def emit_slab(e, ts, fcc, wait=None):
            gidx = slab_n[0]
            slab_n[0] += 1
            sidx = gidx % 6
            if wait is not None:
                b.w("q", "t", wait)
            psem, pval = pcv[("w2", e, fcc // 2)]
            b.w("q", psem, pval)
            slabv[(e, ts, fcc)] = (sidx, b.dma(
                "q", self.wsl[:, sidx * 1024:sidx * 1024 + 1024],
                self.w2b[e, fcc // 2][:, (fcc % 2) * 1024:
                                      (fcc % 2) * 1024 + 1024],
                sem=SLAB_SEMS[sidx]))

        h_ev = {}
        h_last_t = {}
        yev = {}
        yev_g0 = {}
        yev_h0 = {}

        for e in range(E):
            prev = (e - 1, 3) if e > 0 else None
            for ts in range(4):
                for f0 in range(6):
                    wait = None
                    if (e, ts) != (0, 0):
                        pe, pts = (e, ts - 1) if ts > 0 else (e - 1, 3)
                        wait = fq[(pe, pts, 26 + f0)]
                    emit_slab(e, ts, f0, wait)
                if e + 2 < E:
                    for uu in range(4):
                        idx = ts * 4 + uu
                        emit_precast_unit(("w1", e + 2, idx // 2, idx % 2), uu)
                        emit_precast_unit(("w2", e + 2, idx), uu + 1)
                if e == 0 and ts == 0:
                    b.w("g", "cc", 1)
                    rb_done[0] = b.dma("g", self.xa[:, 0:1536], io["cc_out"],
                                       sem="cr")
                    if "rcc" in TAPS:
                        b.w("q", "cr", rb_done[0])
                        b.dma("q", self.tap_t["rcc"], self.xa[:, 0:1536], sem="tap")
                t0 = ts * 512
                prev2 = (e, ts - 1) if ts > 0 else prev
                for fc in range(FC):
                    hb = (fc % 2) * 512
                    if fc == 0 and ts == 0:
                        b.w("t", "w1", w1hi if e == 0 else w1v[e])
                    elif e == 0 and ts == 1 and fc == 0:
                        b.w("t", "w1", w1v[0])
                    if fc >= 2:
                        b.w("t", "s", h_ev[(e, ts, fc - 2)])
                    elif prev2 is not None:
                        b.w("t", "v", (yev_h0 if fc == 0 else yev_g0)[prev2])
                    elif fc == 0:
                        b.w("t", "v", ccv)
                    w1lo = self.gbuf if (e == 0 and ts == 0) else self.big0
                    for i in range(IC):
                        src = w1lo if i < 4 else self.big0
                        b.op("t", lambda t, i=i, fc=fc, hb=hb, t0=t0, src=src:
                             t.matmul(
                                 ps[:, hb:hb + 512],
                                 src[:, i * 4096 + fc * 128:
                                     i * 4096 + fc * 128 + 128],
                                 self.xnb[:, i * 2048 + t0:i * 2048 + t0 + 512],
                                 start=(i == 0), stop=(i == IC - 1)),
                             inc="t" if i == IC - 1 else None)
                    b.w("s", "t", b.c["t"])
                    b.op("s", lambda s, fc=fc, hb=hb, e=e: s.activation(
                        self.big1[:, fc * 512:fc * 512 + 512], ps[:, hb:hb + 512],
                        AF.Relu, bias=self.b1c[:, e * 32 + fc:e * 32 + fc + 1]),
                        inc="s")
                    h_ev[(e, ts, fc)] = b.c["s"]
                h_last_t[(e, ts)] = b.c["t"]

                if prev2 is not None:
                    b.w("t", "v", yev[prev2])
                else:
                    b.w("t", "v", ccv)
                y_gt = {}
                for fc in range(FC):
                    sidx, sv0 = slabv[(e, ts, fc)]
                    b.w("t", SLAB_SEMS[sidx], sv0)
                    gseq = (2, 3, 1, 0) if fc == 0 else (0, 1, 2, 3)
                    for g in gseq:
                        if fc == 0 and g == 0:
                            b.w("t", "s", h_ev[(e, ts, FC - 1)])
                        last_fc = fc == FC - 1
                        for og in range(2):
                            b.op("t", lambda t, fc=fc, g=g, og=og, sidx=sidx: t.matmul(
                                ps[:, g * 1024 + og * 512:g * 1024 + og * 512 + 512],
                                self.big1[:, fc * 512 + g * 128:fc * 512 + g * 128 + 128],
                                self.wsl[:, sidx * 1024 + og * 512:
                                         sidx * 1024 + og * 512 + 512],
                                start=(fc == 0), stop=last_fc),
                                inc="t" if (og == 1 and
                                            (last_fc or g == gseq[-1])) else None)
                        if last_fc:
                            y_gt[g] = b.c["t"]
                    fq[(e, ts, fc)] = b.c["t"]
                    if fc % 2 == 1:
                        quad_t[(e, ts, fc // 2)] = b.c["t"]
                    if fc + 6 < FC:
                        emit_slab(e, ts, fc + 6, fq[(e, ts, fc)])
                    if ts == 3 and e + 1 < E and fc % 2 == 1 and fc < 16:
                        qd = fc // 2
                        if qd == 0:
                            b.w("g", "t", h_last_t[(e, 3)])
                        for hf in range(2):
                            ps_, pv_ = pcv[("w1", e + 1, qd, hf)]
                            b.w("g", ps_, pv_)
                        w1v[e + 1] = b.dma(
                            "g",
                            self.big0[:, qd * 4096:qd * 4096 + 4096],
                            self.w1b[e + 1][:, qd * 4096:qd * 4096 + 4096],
                            sem="w1")
                for g in range(4):
                    b.w("v", "t", y_gt[g])
                    tile = ts * 4 + g
                    for hh in range(2 if g == 0 else 1):
                        wid = 512 if g == 0 else 1024
                        c0 = tile * 1024 + hh * 512
                        p0 = g * 1024 + hh * 512
                        if e == 0:
                            b.op("v", lambda v, c0=c0, p0=p0, wid=wid, tile=tile,
                                 e=e: v.tensor_scalar(
                                     self.gbuf[:, c0:c0 + wid],
                                     ps[:, p0:p0 + wid],
                                     self.ohm[:, tile * 8 + e:tile * 8 + e + 1],
                                     None, ALU.mult), inc="v")
                        else:
                            b.op("v", lambda v, c0=c0, p0=p0, wid=wid, tile=tile,
                                 e=e: v.scalar_tensor_tensor(
                                     self.gbuf[:, c0:c0 + wid],
                                     ps[:, p0:p0 + wid],
                                     self.ohm[:, tile * 8 + e:tile * 8 + e + 1],
                                     self.gbuf[:, c0:c0 + wid],
                                     ALU.mult, ALU.add), inc="v")
                        if g == 0 and hh == 0:
                            yev_h0[(e, ts)] = b.c["v"]
                    if g == 0:
                        yev_g0[(e, ts)] = b.c["v"]
                yev[(e, ts)] = b.c["v"]

        bm_s = {}
        bm_v = {}

        def emit_b2mix(tile):
            pslot = (tile % 2) * 1024
            if tile == 0:
                b.w("t", "v", yev[(E - 1, 3)])
            if tile >= 2:
                b.w("t", "v", bm_v[tile - 2])
            b.op("t", lambda t, tile=tile, pslot=pslot: t.matmul(
                ps[0:8, pslot + 512:pslot + 640],
                self.ohm[:, tile * 8:tile * 8 + 8], self.cb["i128f"][:, :],
                start=True, stop=True), inc="t")
            slot = self.geb[0:8, 0:128] if tile % 2 == 0 else self.mbuf[0:8, 0:128]
            b.w("s", "t", b.c["t"])
            if tile >= 2:
                b.w("s", "s", bm_s[tile - 2])
            b.op("s", lambda s, tile=tile, pslot=pslot, slot=slot: s.copy(
                slot, ps[0:8, pslot + 512:pslot + 640]), inc="s")
            bm_s[tile] = b.c["s"]
            b.w("t", "s", bm_s[tile])
            for og in range(2):
                b.op("t", lambda t, og=og, pslot=pslot, slot=slot: t.matmul(
                    ps[:, pslot + og * 512:pslot + og * 512 + 512],
                    slot, self.b2n[:, og * 512:og * 512 + 512],
                    start=True, stop=True), inc="t" if og == 1 else None)
            b.w("v", "t", b.c["t"])
            b.op("v", lambda v, tile=tile, pslot=pslot: v.tensor_tensor(
                self.gbuf[:, tile * 1024:tile * 1024 + 1024],
                self.gbuf[:, tile * 1024:tile * 1024 + 1024],
                ps[:, pslot:pslot + 1024], ALU.add), inc="v")
            bm_v[tile] = b.c["v"]
            b.w("s", "v", bm_v[tile])
            b.op("s", lambda s, tile=tile: s.activation(
                self.gbuf[:, tile * 1024:tile * 1024 + 1024],
                self.gbuf[:, tile * 1024:tile * 1024 + 1024], AF.Silu), inc="s")

        if self.stop_after == "FFN":
            for tile in range(NT):
                emit_b2mix(tile)
            silu_s = b.c["s"]
            if "gbuf" in TAPS:
                b.w("q", "s", silu_s)
                b.dma("q", self.tap_t["gbuf"], self.gbuf[:, :], sem="tap")
            self.finish(b)
            return

        b.w("g", "t", quad_t[(E - 1, 3, 15)])
        tst = {"k": 0, "tr": {}, "ev": []}
        for wnm, base in (("wq", 0), ("wo", 8192)):
            addI = wnm == "wq"
            for half in range(2):
                k = tst["k"]
                tst["k"] += 1
                buf = w_stage[k % 2]
                sem = "wn" if k % 2 == 0 else "wm"
                if k >= 2:
                    b.w("g", "t", tst["tr"][k - 2])
                gv = b.dma("g", buf.rearrange("p (o i) -> p o i", o=4),
                           io[wnm][half * 512:half * 512 + 512, :].rearrange(
                               "(o p) i -> p o i", p=128), sem=sem)
                for tile in range(k * 4, k * 4 + 4):
                    emit_b2mix(tile)
                b.w("t", sem, gv)
                for ib in range(IC):
                    bk = R0 + (ib % 2) * 512
                    if len(tst["ev"]) >= 2:
                        b.w("t", "v", tst["ev"][-2])
                    mms = []
                    for ob4 in range(4):
                        diag = addI and (ob4 == ib - half * 4)
                        mms.append((ob4, False, not diag))
                        if diag:
                            mms.append((ob4, True, True))
                    for mi, (ob4, ident, stop) in enumerate(mms):
                        lastmm = mi == len(mms) - 1
                        lhs = (self.cb["i128b"][:, :] if ident else
                               buf[:, ob4 * 1024 + ib * 128:
                                   ob4 * 1024 + ib * 128 + 128])
                        b.op("t", lambda t, lhs=lhs, ob4=ob4, ident=ident,
                             stop=stop, bk=bk: t.matmul(
                                 ps[:, bk + ob4 * 128:bk + ob4 * 128 + 128],
                                 lhs, self.cb["i128b"][:, :],
                                 start=not ident, stop=stop),
                             inc="t" if lastmm else None)
                    b.w("v", "t", b.c["t"])
                    b.op("v", lambda v, ib=ib, base=base, half=half, bk=bk:
                         v.tensor_copy(
                             self.big0[:, base + ib * 1024 + half * 512:
                                       base + ib * 1024 + half * 512 + 512],
                             ps[:, bk:bk + 512]), inc="v")
                    tst["ev"].append(b.c["v"])
                tst["tr"][k] = b.c["t"]
        wT_v = b.c["v"]
        silu_s = b.c["s"]
        if "gbuf" in TAPS:
            b.w("q", "s", silu_s)
            b.dma("q", self.tap_t["gbuf"], self.gbuf[:, :], sem="tap")

        qv = {}
        for cc in range(IC):
            pbase = (cc % 2) * 2048
            if cc >= 2:
                b.w("t", "s", qv[cc - 2])
            if cc == 0:
                b.w("t", "v", wT_v)
                b.w("t", "v", bm_v[NT - 1])
            for i in range(IC):
                for tg in range(4):
                    b.op("t", lambda t, cc=cc, i=i, tg=tg, pbase=pbase: t.matmul(
                        ps[:, pbase + tg * 512:pbase + tg * 512 + 512],
                        self.big0[:, i * 1024 + cc * 128:i * 1024 + cc * 128 + 128],
                        self.xnb[:, i * 2048 + tg * 512:i * 2048 + tg * 512 + 512],
                        start=(i == 0), stop=False))
            for tg in range(4):
                rr = "qa" if cc < 4 else "qb"
                c0 = (cc % 4) * 128
                rp = ROWSA[rr][0]
                b.op("t", lambda t, tg=tg, rr=rr, c0=c0, rp=rp, pbase=pbase: t.matmul(
                    ps[:, pbase + tg * 512:pbase + tg * 512 + 512],
                    self.rrow(rr, c0, 128), self.cb["ones_b"][rp:rp + 1, :],
                    start=False, stop=True), inc="t" if tg == 3 else None)
            b.w("s", "t", b.c["t"])
            b.op("s", lambda s, cc=cc, pbase=pbase: s.copy(
                self.big0[:, 16384 + cc * 2048:16384 + cc * 2048 + 2048],
                ps[:, pbase:pbase + 2048]), inc="s")
            qv[cc] = b.c["s"]
        q_done = b.c["s"]
        if "qb" in TAPS:
            b.w("q", "s", q_done)
            b.dma("q", self.tap_t["qb"], self.big0[:, 16384:32768], sem="tap")
        if self.stop_after == "Q":
            self.finish(b)
            return

        b.w("t", "s", q_done)
        b.w("t", "cr", rb_done[0])
        S1 = self.xa[:, 0:512]
        S2 = self.xa[:, 512:1024]

        def eslot(par, i):
            p = par * 32
            return [self.xf[p:p + 1, 0:512], self.xf[p:p + 1, 512:1024],
                    self.ob2[p:p + 1, 0:512], self.ob2[p:p + 1, 512:1024],
                    self.xa[p:p + 1, 1536:2048], self.mfin[p:p + 1, 0:512]][i]
        gr_t = {}
        gr_mm = {}
        for r8 in range(8):
            gi, par = r8 // 2, r8 % 2
            gcol = (18 + gi) if par == 0 else (22 + gi)
            pslot = R0 + (r8 % 2) * 512
            if r8 == 0:
                b.w("s", "v", bm_v[NT - 1])
            else:
                b.w("s", "t", gr_mm[r8 - 1])
            b.op("s", lambda s, gcol=gcol: s.copy(
                self.geb[0:8, 0:1], self.smal[0:8, gcol:gcol + 1]), inc="s")
            b.w("t", "s", b.c["s"])
            if r8 >= 2:
                b.w("t", "v", gr_t[r8 - 2])
            b.op("t", lambda t, pslot=pslot: t.matmul(
                ps[0:1, pslot:pslot + 512],
                self.geb[0:8, 0:1], self.cb["sel8"][:, :],
                start=True, stop=True), inc="t")
            gr_mm[r8] = b.c["t"]
            b.w("v", "t", b.c["t"])
            if r8 == 0:
                b.w("v", "cr", rb_done[0])
            b.op("v", lambda v, par=par, gi=gi, pslot=pslot: v.tensor_copy(
                eslot(par, gi), ps[0:1, pslot:pslot + 512]), inc="v")
            gr_t[r8] = b.c["v"]
        if self.stop_after == "M1":
            self.finish(b)
            return
        for vi in range(2):
            for par in range(2):
                dst = eslot(par, 4 + vi)
                for hlf in range(2):
                    srow = (vi * 2 + hlf) * 32
                    b.op("v", lambda v, dst=dst, srow=srow, par=par, hlf=hlf:
                         v.tensor_copy(
                             dst[:, hlf * 256:hlf * 256 + 256]
                             .rearrange("p (a b) -> p a b", b=64),
                             self.xa[srow:srow + 1, 1024:1536]
                             .rearrange("p (a t b) -> p a t b", t=2, b=64)
                             [:, :, par, :]),
                         inc="v" if (vi, par, hlf) == (1, 1, 1) else None)
        vpar_v = b.c["v"]
        if self.stop_after == "M2":
            self.finish(b)
            return
        b.op("v", lambda v: v.tensor_tensor(self.csx[:, :], S1, S2, ALU.mult),
             inc="v")
        s12_v = b.c["v"]
        b.w("t", "v", s12_v)
        for par in range(2):
            ev = self.cb["etop"] if par == 0 else self.cb["ebot"]
            b.op("t", lambda t, par=par, ev=ev: t.matmul(
                ps[0:1, R0 + par * 512:R0 + par * 512 + 512], ev[:, :],
                self.csx[:, :], start=True, stop=True),
                inc="t" if par == 1 else None)
        y2s_t = b.c["t"]
        b.w("v", "t", y2s_t)
        for par in range(2):
            b.op("v", lambda v, par=par: v.tensor_tensor(
                ps[0:1, R0 + par * 512:R0 + par * 512 + 512],
                ps[0:1, R0 + par * 512:R0 + par * 512 + 512],
                eslot(par, 1), ALU.mult))
        b.op("v", lambda v: v.tensor_scalar(ps[0:1, R0:R0 + 1024],
                                            ps[0:1, R0:R0 + 1024],
                                            INVN2, None, ALU.mult), inc="v")
        y2_v = b.c["v"]
        b.w("s", "v", y2_v)
        b.op("s", lambda s: s.copy(self.sgs[0:1, 0:512], ps[0:1, R0:R0 + 512]))
        b.op("s", lambda s: s.copy(self.sgs[64:65, 0:512], ps[0:1, R0 + 512:R0 + 1024]),
             inc="s")
        y2_s = b.c["s"]
        if self.stop_after == "M3":
            self.finish(b)
            return
        if "y2r" in TAPS:
            b.w("q", "v", y2_v)
            b.dma("q", self.tap_t["y2r"], ps[0:1, R0:R0 + 1024], sem="tap")
        for par in range(2):
            b.op("v", lambda v, par=par: v.tensor_tensor(
                eslot(par, 0), eslot(par, 0), eslot(par, 5), ALU.mult))
            b.op("v", lambda v, par=par: v.tensor_tensor(
                eslot(par, 2), eslot(par, 2), eslot(par, 4), ALU.mult))
            for idx in (0, 2, 3):
                b.op("v", lambda v, par=par, idx=idx: v.tensor_scalar(
                    eslot(par, idx), eslot(par, idx), INVN2, None, ALU.mult),
                    inc="v" if (par == 1 and idx == 3) else None)
        csr_v = b.c["v"]
        if self.stop_after == "M4":
            self.finish(b)
            return
        b.w("t", "v", csr_v)
        b.w("t", "s", y2_s)
        for term, sidx2 in ((0, 0), (1, 2), (2, 3)):
            for par in range(2):
                p0 = par * 32
                b.op("t", lambda t, term=term, sidx2=sidx2, par=par, p0=p0: t.matmul(
                    ps[par * 64:par * 64 + 64,
                       R1 + term * 512:R1 + term * 512 + 512],
                    self.cb["ones_f"][p0:p0 + 1, 0:64],
                    eslot(par, sidx2), start=True, stop=True),
                    inc="t" if par == 1 else None)
        cs_t = b.c["t"]
        if self.stop_after == "M4a":
            self.finish(b)
            return
        b.w("v", "t", cs_t)
        b.op("v", lambda v: v.tensor_tensor(self.mfin[:, :], S1,
                                            ps[:, R1:R1 + 512], ALU.mult))
        b.op("v", lambda v: v.tensor_tensor(self.csx[:, :], S2,
                                            ps[:, R1 + 512:R1 + 1024], ALU.mult))
        b.op("v", lambda v: v.tensor_tensor(self.mfin[:, :], self.mfin[:, :],
                                            self.csx[:, :], ALU.add))
        b.op("v", lambda v: v.tensor_tensor(self.csx[:, :], S1, S2, ALU.mult))
        b.op("v", lambda v: v.tensor_tensor(self.csx[:, :], self.csx[:, :],
                                            ps[:, R1 + 1024:R1 + 1536], ALU.mult))
        b.op("v", lambda v: v.tensor_tensor(self.mfin[:, :], self.mfin[:, :],
                                            self.csx[:, :], ALU.add), inc="v")
        mf_v = b.c["v"]
        if self.stop_after == "M4b":
            self.finish(b)
            return
        b.w("s", "v", mf_v)
        b.op("s", lambda s: s.copy(self.mbuf[:, :], self.mfin[:, :]), inc="s")
        m_s = b.c["s"]
        if "mm" in TAPS:
            b.w("q", "v", mf_v)
            b.dma("q", self.tap_t["mm"], self.mfin[:, :], sem="tap")
        if self.stop_after == "M":
            self.finish(b)
            return

        ynat_s = {}
        for j in range(IC):
            pbase = (j % 2) * 2048
            if j >= 2:
                b.w("t", "s", ynat_s[j - 2])
            if j == 0:
                b.w("t", "s", m_s)
                b.w("t", "s", y2_s)
            for hh in range(2):
                r0 = hh * 64
                for tg in range(4):
                    b.op("t", lambda t, j=j, r0=r0, tg=tg, pbase=pbase: t.matmul(
                        ps[r0:r0 + 64, pbase + tg * 512:pbase + tg * 512 + 512],
                        self.mbuf[r0:r0 + 64, j * 64:j * 64 + 64],
                        self.big0[r0:r0 + 64,
                                  16384 + j * 2048 + tg * 512:
                                  16384 + j * 2048 + tg * 512 + 512],
                        start=True, stop=False))
                for tg in range(4):
                    b.op("t", lambda t, j=j, r0=r0, tg=tg, pbase=pbase: t.matmul(
                        ps[r0:r0 + 64, pbase + tg * 512:pbase + tg * 512 + 512],
                        self.sgs[r0:r0 + 1, j * 64:j * 64 + 64],
                        self.cb["ones_b"][r0:r0 + 1, :], start=False, stop=True),
                        inc="t" if tg == 3 else None)
            b.w("s", "t", b.c["t"])
            b.op("s", lambda s, j=j, pbase=pbase: s.copy(
                self.big1[:, j * 2048:j * 2048 + 2048],
                ps[:, pbase:pbase + 2048]), inc="s")
            ynat_s[j] = b.c["s"]
        ynat_done = b.c["s"]
        if "ynat" in TAPS:
            b.w("q", "s", ynat_done)
            b.dma("q", self.tap_t["ynat"], self.big1[:, :], sem="tap")
        if self.stop_after == "Y":
            self.finish(b)
            return

        def sqsl(cc):
            return self.wsl[:, (cc % 3) * 2048:(cc % 3) * 2048 + 2048]

        def stat_sl(k):
            return ps[(k % 3) * 32:(k % 3) * 32 + 1,
                      R0 + (k // 3) * 512:R0 + (k // 3) * 512 + 512]
        sq_v = {}
        sq_eng = {}
        ln_cc_t = {}
        for cc in range(IC):
            eng = "v" if cc % 2 == 0 else "s"
            if eng == "v":
                b.w("v", "s", ynat_s[cc])
            if cc >= 3:
                b.w(eng, "t", ln_cc_t[cc - 3])
            if eng == "v":
                b.op("v", lambda v, cc=cc: v.tensor_tensor(
                    sqsl(cc), self.big1[:, cc * 2048:cc * 2048 + 2048],
                    self.big1[:, cc * 2048:cc * 2048 + 2048], ALU.mult),
                    inc="v")
                sq_v[cc] = b.c["v"]
            else:
                b.op("s", lambda s, cc=cc: s.activation(
                    sqsl(cc), self.big1[:, cc * 2048:cc * 2048 + 2048],
                    AF.Square), inc="s")
                sq_v[cc] = b.c["s"]
            sq_eng[cc] = eng
            if cc == 0:
                b.w("t", "s", ynat_done)
            b.w("t", eng, sq_v[cc])
            for gg in range(4):
                for which in range(2):
                    k = gg + 4 * which
                    src2 = (self.big1[:, cc * 2048 + gg * 512:cc * 2048 + gg * 512 + 512]
                            if which == 0 else
                            sqsl(cc)[:, gg * 512:gg * 512 + 512])
                    b.op("t", lambda t, k=k, src2=src2, cc=cc: t.matmul(
                        stat_sl(k), self.cb["onescol_b"][:, :], src2,
                        start=(cc == 0), stop=(cc == IC - 1)),
                        inc="t" if (gg == 3 and which == 1) else None)
            ln_cc_t[cc] = b.c["t"]
        lnsum_t = b.c["t"]
        b.w("v", "t", lnsum_t)

        def syr(k):
            return self.xa[0:1, k * 512:k * 512 + 512]

        def sy2r(k):
            t2 = [self.xf, self.xf, self.ob2, self.ob2][k]
            c = [0, 512, 0, 512][k]
            return t2[0:1, c:c + 512]
        musq = self.mfin[0:1, 0:512]
        for k in range(8):
            dst = syr(k) if k < 4 else sy2r(k - 4)
            b.op("v", lambda v, k=k, dst=dst: v.tensor_copy(dst, stat_sl(k)),
                 inc="v" if k == 7 else None)
        for k in range(4):
            b.op("v", lambda v, k=k: v.tensor_scalar(syr(k), syr(k), 1.0 / C,
                                                     None, ALU.mult))
            b.op("v", lambda v, k=k: v.tensor_scalar(sy2r(k), sy2r(k), 1.0 / C,
                                                     None, ALU.mult))
            b.op("v", lambda v, k=k: v.tensor_tensor(musq, syr(k), syr(k),
                                                     ALU.mult))
            b.op("v", lambda v, k=k: v.tensor_tensor(sy2r(k), sy2r(k), musq,
                                                     ALU.subtract))
            b.op("v", lambda v, k=k: v.tensor_scalar(sy2r(k), sy2r(k), 1.0, 1e-5,
                                                     ALU.mult, ALU.add),
                 inc="v" if k == 3 else None)
        b.w("s", "v", b.c["v"])
        for k in range(4):
            b.op("s", lambda s, k=k: s.sqrt(sy2r(k), sy2r(k)),
                 inc="s" if k == 3 else None)
        b.w("v", "s", b.c["s"])
        for k in range(4):
            b.op("v", lambda v, k=k: v.reciprocal(sy2r(k), sy2r(k)))
            b.op("v", lambda v, k=k: v.tensor_tensor(syr(k), syr(k), sy2r(k),
                                                     ALU.mult))
            b.op("v", lambda v, k=k: v.tensor_scalar(syr(k), syr(k), -1.0, None,
                                                     ALU.mult),
                 inc="v" if k == 3 else None)
        stat_v = b.c["v"]
        b.w("t", "v", stat_v)
        bc_slist = []
        for k in range(8):
            which, gg = k // 4, k % 4
            row = sy2r(gg) if which == 0 else syr(gg)
            slot = R1 + (k % 2) * 512
            if k >= 2:
                b.w("t", "s", bc_slist[k - 2])
            b.op("t", lambda t, row=row, slot=slot: t.matmul(
                ps[:, slot:slot + 512], self.cb["ones_f"][0:1, :],
                row, start=True, stop=True), inc="t")
            b.w("s", "t", b.c["t"])
            b.op("s", lambda s, which=which, gg=gg, slot=slot: s.copy(
                self.big0[:, 16384 + which * 2048 + gg * 512:
                          16384 + which * 2048 + gg * 512 + 512],
                ps[:, slot:slot + 512]), inc="s")
            bc_slist.append(b.c["s"])
        bc_s = b.c["s"]
        for cc in range(IC):
            if cc == 0:
                b.w("v", "s", bc_s)
            for gg in range(4):
                slc = slice(cc * 2048 + gg * 512, cc * 2048 + gg * 512 + 512)
                abc = self.big0[:, 16384 + gg * 512:16384 + gg * 512 + 512]
                bbc = self.big0[:, 18432 + gg * 512:18432 + gg * 512 + 512]
                b.op("v", lambda v, slc=slc, abc=abc: v.tensor_tensor(
                    self.big1[:, slc], self.big1[:, slc], abc, ALU.mult))
                b.op("v", lambda v, slc=slc, bbc=bbc: v.tensor_tensor(
                    self.big1[:, slc], self.big1[:, slc], bbc, ALU.add),
                    inc="v" if gg == 3 else None)
            b.w("s", "v", b.c["v"])
            b.op("s", lambda s, cc=cc: s.activation(
                self.big1[:, cc * 2048:cc * 2048 + 2048],
                self.big1[:, cc * 2048:cc * 2048 + 2048], AF.Identity,
                bias=self.smal[:, 8 + cc:9 + cc], scale=self.smal[:, cc:cc + 1]),
                inc="s")
        ln_done2 = b.c["s"]
        if "ln" in TAPS:
            b.w("q", "s", ln_done2)
            b.dma("q", self.tap_t["ln"], self.big1[:, :], sem="tap")
        if self.stop_after == "LN":
            self.finish(b)
            return

        sfm_s = {}
        for oc in range(IC):
            pbase = (oc % 2) * 2048
            if oc >= 2:
                b.w("t", "s", sfm_s[oc - 2])
            if oc == 0:
                b.w("t", "s", ln_done2)
            for i in range(IC):
                for tg in range(4):
                    b.op("t", lambda t, oc=oc, i=i, tg=tg, pbase=pbase: t.matmul(
                        ps[:, pbase + tg * 512:pbase + tg * 512 + 512],
                        self.big0[:, 8192 + i * 1024 + oc * 128:
                                  8192 + i * 1024 + oc * 128 + 128],
                        self.big1[:, i * 2048 + tg * 512:i * 2048 + tg * 512 + 512],
                        start=(i == 0), stop=False))
            for tg in range(4):
                rr = "oa" if oc < 4 else "ob"
                c0 = (oc % 4) * 128
                rp = ROWSA[rr][0]
                b.op("t", lambda t, tg=tg, rr=rr, c0=c0, rp=rp, pbase=pbase: t.matmul(
                    ps[:, pbase + tg * 512:pbase + tg * 512 + 512],
                    self.rrow(rr, c0, 128), self.cb["ones_b"][rp:rp + 1, :],
                    start=False, stop=True), inc="t" if tg == 3 else None)
            b.w("s", "t", b.c["t"])
            b.op("s", lambda s, oc=oc, pbase=pbase: s.copy(
                self.xnb[:, oc * 2048:oc * 2048 + 2048], ps[:, pbase:pbase + 2048]),
                inc="s")
            sfm_s[oc] = b.c["s"]
        sfm_done = b.c["s"]
        if "sfm" in TAPS:
            b.w("q", "s", sfm_done)
            b.dma("q", self.tap_t["sfm"], self.xnb[:, :], sem="tap")
        if self.stop_after == "SFM":
            self.finish(b)
            return

        stm_s = {}
        for T in range(NT):
            pb = R0 + (T % 2) * 1024
            if T >= 2:
                b.w("t", "s", stm_s[T - 2])
            if T == 0:
                b.w("t", "s", sfm_done)
            for j in range(IC):
                b.op("t", lambda t, T=T, j=j, pb=pb: t.matmul(
                    ps[:, pb + j * 128:pb + j * 128 + 128],
                    self.xnb[:, j * 2048 + T * 128:j * 2048 + T * 128 + 128],
                    self.cb["i128b"][:, :], start=True, stop=True),
                    inc="t" if j == IC - 1 else None)
            b.w("s", "t", b.c["t"])
            b.op("s", lambda s, T=T, pb=pb: s.copy(
                self.big1[:, T * 1024:T * 1024 + 1024], ps[:, pb:pb + 1024]),
                inc="s")
            stm_s[T] = b.c["s"]
        if "stm" in TAPS:
            b.w("q", "s", b.c["s"])
            b.dma("q", self.tap_t["stm"], self.big1[:, :], sem="tap")
        if self.stop_after == "STM":
            self.finish(b)
            return

        fin_v = {}
        od = {}
        for T in range(NTL):
            sl = (T % 2) * 1024
            if T >= 2:
                b.w("q", "v", fin_v[T - 2])
            elif T == 0:
                b.w("q", "s", bc_s)
            xd = b.dma("q", self.xa[:, sl:sl + 1024],
                       io["x"][T * 128:(T + 1) * 128, :],
                       sem="xa" if T % 2 == 0 else "xb")
            ob = self.xf[:, 0:1024] if T % 2 == 0 else self.ob2[:, 0:1024]
            b.w("v", "s", stm_s[T])
            if T == 0:
                b.w("v", "s", silu_s)
            b.w("v", "xa" if T % 2 == 0 else "xb", xd)
            if T >= 2:
                b.w("v", "oa" if T % 2 == 0 else "ob", od[T - 2])
            b.op("v", lambda v, T=T: v.tensor_tensor(
                self.sgs[:, :], self.big1[:, T * 1024:T * 1024 + 1024],
                self.gbuf[:, T * 1024:T * 1024 + 1024], ALU.mult))
            b.op("v", lambda v, T=T, sl=sl, ob=ob: v.tensor_tensor(
                ob, self.xa[:, sl:sl + 1024], self.sgs[:, :], ALU.add), inc="v")
            fin_v[T] = b.c["v"]
            b.w("q", "v", fin_v[T])
            od[T] = b.dma("q", io["out"][T * 128:(T + 1) * 128, :], ob,
                          sem="oa" if T % 2 == 0 else "ob")
        self.finish(b)

    def finish(self, b):
        if self.stop_after:
            b.w("q", "v", b.c["v"])
            b.w("q", "s", b.c["s"])
            b.w("q", "t", b.c["t"])
            b.dma("q", self.io["out"][0:128, :], self.xa[:, 0:1024], sem="oa")
        b.w("q", "oa", b.c["oa"])
        b.w("q", "ob", b.c["ob"])
        b.w("q", "tap", b.c["tap"])
        b.w("q", "pa", b.c["pa"])
        b.w("q", "pb", b.c["pb"])
        b.w("q", "pc", b.c["pc"])
        b.w("q", "wm", b.c["wm"])
        b.w("q", "w1", b.c["w1"])
        b.w("q", "pd", b.c["pd"])
        b.w("q", "pe", b.c["pe"])

    def build(self):
        nc = self.nc
        import contextlib
        names = ["t", "v", "s", "cc",
                 "qi", "gi", "wn", "xa", "xb", "w1", "sa", "sb",
                 "cw", "cr", "oa", "ob", "tap",
                 "pa", "pb", "pc", "wm", "pd", "pe"]
        with contextlib.ExitStack() as stk:
            sems = {n: stk.enter_context(nc.semaphore("sem_" + n)) for n in names}
            block = stk.enter_context(nc.Block())

            @block.sync
            def _(e):
                self.walk(Wk("q", e, sems))

            @block.gpsimd
            def _(e):
                self.walk(Wk("g", e, sems))

            @block.tensor
            def _(e):
                self.walk(Wk("t", e, sems))

            @block.vector
            def _(e):
                self.walk(Wk("v", e, sems))

            @block.scalar
            def _(e):
                self.walk(Wk("s", e, sems))
        return nc


def build_program(taps=(), stop_after=None, nt=NT, init_only=False, a_stage=4,
                  dbg=()):
    return MK(taps=taps, stop_after=stop_after, nt=nt, init_only=init_only,
              a_stage=a_stage, dbg=dbg).build()


def make_in_maps(inputs):
    x = inputs["L0_attn_x"]
    maps = []
    for c in range(8):
        bsel = c // 4
        t0 = (c % 4) * TC
        m = {
            "x": x[bsel, t0:t0 + TC, :],
            "wq": inputs["L0_attn_Wq"], "wk1": inputs["L0_attn_Wk1"],
            "wk2": inputs["L0_attn_Wk2"], "wv1": inputs["L0_attn_Wv1"],
            "wv2": inputs["L0_attn_Wv2"], "wo": inputs["L0_attn_Wo"],
            "biases": np.stack([
                inputs["L0_attn_bq"], inputs["L0_attn_bk1"],
                inputs["L0_attn_bk2"], inputs["L0_attn_bv1"],
                inputs["L0_attn_bv2"], inputs["L0_attn_bo"]]),
            "gvec": np.stack([
                inputs["L0_attn_g1"], inputs["L0_attn_g2"],
                inputs["L0_attn_g3"], inputs["L0_attn_g4"]]),
            "lng": inputs["L0_attn_ln_g"], "lnb": inputs["L0_attn_ln_b"],
            "gw": inputs["L0_ffn_gate_w"], "gb": inputs["L0_ffn_gate_b"],
            "eb": inputs["L0_ffn_expert_bias"],
            "w1": inputs["L0_ffn_w1"], "b1": inputs["L0_ffn_b1"],
            "w2": inputs["L0_ffn_w2"], "b2": inputs["L0_ffn_b2"],
        }
        maps.append({k: np.ascontiguousarray(v, dtype=np.float32)
                     for k, v in m.items()})
    return maps


def kernel(**inputs):
    nc = build_program()
    in_maps = make_in_maps(inputs)
    res = run_bass_kernel_spmd(nc, in_maps, core_ids=list(range(8)))
    out = np.empty((2, NFULL, C), dtype=np.float32)
    for c in range(8):
        out[c // 4, (c % 4) * TC:(c % 4 + 1) * TC, :] = res.results[c]["out"]
    return out
